# version 14
# baseline (speedup 1.0000x reference)
import os
import numpy as np
import ml_dtypes

import concourse.bass as bass
import concourse.mybir as mybir
from concourse.bass_utils import run_bass_kernel_spmd

F32 = mybir.dt.float32
BF16 = mybir.dt.bfloat16
AF = mybir.ActivationFunctionType
ALU = mybir.AluOpType
AX = mybir.AxisListType

NCORES = 8
T = 2048
D = 1024
H = 16
DH = 64
NP = 8
C = 16
WIN = 128
NWIN = T // WIN
NCH = WIN // C
SBT = 512
NSB = T // SBT
WPS = SBT // WIN
EPS_LN = 1e-5
EPS_RMS = 1e-6
FP6_MAX = 7.5

STAGE = os.environ.get("BASS_STAGE", "full")
NWINLIM = int(os.environ.get("BASS_NWINLIM", "999"))
NCHLIM = int(os.environ.get("BASS_NCHLIM", "999"))

ENGINES = ("sync", "tensor", "vector", "scalar", "gpsimd")


DMA_SLOTS = {"sync": 10, "gpsimd": 6}


class Plan:

    def __init__(self):
        self.steps = []
        self.counts = {e: 0 for e in ENGINES}
        self.dq_counts = {}
        self.dma_idx = {q: 0 for q in DMA_SLOTS}
        self.slot_prev = {}
        self.last_write = {}
        self.readers = {}

    def add(self, engine, fn, reads=(), writes=(), dma=False, tag=""):
        deps = {}

        def upd(ev):
            e, v = ev
            if v > deps.get(e, -1):
                deps[e] = v

        for r in reads:
            lw = self.last_write.get(r)
            if lw:
                upd(lw)
        for r in writes:
            lw = self.last_write.get(r)
            if lw:
                upd(lw)
            for rd in self.readers.get(r, ()):
                upd(rd)
        if dma:
            q = engine
            slot = self.dma_idx[q] % DMA_SLOTS[q]
            self.dma_idx[q] += 1
            semk = f"dq_{q}_{slot}"
            prevres = self.slot_prev.get((q, slot))
            if prevres is not None:
                upd((semk, self.dq_counts.get(semk, 0)))
                for r in prevres:
                    for rd in self.readers.get(r, ()):
                        upd(rd)
            tgt = self.dq_counts.get(semk, 0) + 16
            self.dq_counts[semk] = tgt
            self.steps.append(dict(engine=engine, fn=fn, waits=dict(deps),
                                   sem=semk, inc=16, tag=tag))
            sig = (semk, tgt)
            self.slot_prev[(q, slot)] = list(writes) + list(reads)
        else:
            after = self.counts[engine] + 1
            self.counts[engine] = after
            self.steps.append(dict(engine=engine, fn=fn, waits=dict(deps),
                                   sem=engine, inc=1, tag=tag))
            sig = (engine, after)
        for r in writes:
            self.last_write[r] = sig
            self.readers[r] = []
        for r in reads:
            self.readers.setdefault(r, []).append(sig)


def build_consts():
    c = {}
    c["ident_f32"] = np.eye(128, dtype=np.float32)
    c["ident_bf16"] = np.eye(128, dtype=ml_dtypes.bfloat16)
    j = np.arange(C)
    negtril = -(j[:, None] > j[None, :]).astype(np.float32)
    c["negtril"] = np.ascontiguousarray(
        np.broadcast_to(negtril[None, :, None, :], (NCH, C, H, C)).reshape(128, H * C))
    triincl = (j[:, None] <= j[None, :]).astype(np.float32)
    c["triincl"] = np.ascontiguousarray(
        np.broadcast_to(triincl[None, :, None, :], (NCH, C, H, C)).reshape(128, H * C))
    e = np.zeros((128, 2), dtype=np.float32)
    e[0:64, 0] = 1.0
    e[64:128, 1] = 1.0
    c["esel2"] = e
    e2t = np.zeros((2, 128), dtype=np.float32)
    e2t[0, 0:64] = 1.0
    e2t[1, 64:128] = 1.0
    c["esel2t"] = e2t
    c["mska"] = np.full((128, 1), ~((1 << 19) - 1), np.int32)
    c["mskb"] = np.full((128, 1), ~((1 << 20) - 1), np.int32)
    return c


def build_program():
    nc = bass.Bass("TRN2", target_bir_lowering=False)
    P = Plan()

    def inp(name, shape, dtype=F32):
        return nc.dram_tensor(name, list(shape), dtype, kind="ExternalInput").ap()

    d_x = inp("x", (T, D))
    d_w = {m: inp("W" + m, (D, D)) for m in "qkvgb"}
    d_wa = inp("Wa", (H, D))
    d_dt = inp("dt_bias", (H, 1))
    d_alog = inp("A_log", (H, 1))
    d_nw = inp("norm_w", (DH,))
    d_wo = inp("Wo", (D, D))
    d_bo = inp("bo", (D,))
    d_wf = inp("Wf", (D, D))
    d_bf = inp("bfv", (D,))
    d_idf = inp("ident_f32", (128, 128))
    d_idb = inp("ident_bf16", (128, 128), BF16)
    d_ntril = inp("negtril", (128, H * C))
    d_ttri = inp("triincl", (128, H * C))
    d_esel = inp("esel2", (128, 2))
    d_e2t = inp("esel2t", (2, 128))
    d_mska = inp("mska", (128, 1), mybir.dt.int32)
    d_mskb = inp("mskb", (128, 1), mybir.dt.int32)

    d_out = nc.dram_tensor("out", [T, D], F32, kind="ExternalOutput").ap()
    dbg = {}
    if STAGE != "full":
        for name, shape in [("kb", (128, T * NP)), ("qb", (128, T * NP)),
                            ("vb", (128, T * NP)), ("bb", (128, T * NP)),
                            ("pp", (H, T + 1)), ("ys", (128, T * NP)),
                            ("xm", (T, D)), ("e1", (128, 1024)),
                            ("kc", (128, 128)), ("pw1", (128, 1024)),
                            ("pw2", (128, 64)), ("u0", (128, 128)),
                            ("wr0", (128, 2048)), ("qk0", (16, 2048)),
                            ("v0", (128, 128)), ("b0", (128, 128))]:
            dbg[name] = nc.dram_tensor("dbg_" + name, list(shape), F32,
                                       kind="ExternalOutput").ap()

    def scratch(name, shape, dtype=F32):
        return nc.dram_tensor(name, list(shape), dtype, kind="Internal").ap()

    d_wt = {m: scratch("wt" + m, (D, D), BF16) for m in "qkvgb"}
    d_wto = scratch("wto", (D, D), BF16)
    d_wtf = scratch("wtf", (D, D), BF16)
    d_wat = scratch("wat", (D, H), BF16)
    d_pd = scratch("pd", (H, T + 32))
    d_pcs = scratch("pcs", (H, 160))
    d_rk = scratch("rk", (H, T))
    d_rq = scratch("rq", (H, T))
    d_rr = scratch("rr", (H, T))
    d_wd = scratch("wd", (2, NCH, H, C, C))
    d_qk = scratch("qk", (2, NCH, H, C, C))

    ctxs = []

    def sbuf(name, shape, dtype=F32):
        t = nc.sbuf_tensor(name, list(shape), dtype)
        ctxs.append(t)
        return t.__enter__()

    def psum(name, shape, dtype=F32):
        t = nc.psum_tensor(name, list(shape), dtype)
        ctxs.append(t)
        return t.__enter__()

    s_idf = sbuf("idf", (128, 128))
    s_idb = sbuf("idb", (128, 128), BF16)
    s_ntril = sbuf("ntril", (128, H * C))
    s_ttri = sbuf("ttri", (128, H * C))
    s_esel = sbuf("esel", (128, 2))
    s_e2t = sbuf("e2t", (2, 128))
    s_mka = sbuf("mka", (128, 1), mybir.dt.int32)
    s_mkb = sbuf("mkb", (128, 1), mybir.dt.int32)
    s_dt = sbuf("dt", (H, 1))
    s_ealog = sbuf("ealog", (H, 1))
    s_nwcol = sbuf("nwcol", (128, 1))
    s_borep = sbuf("borep", (128, D))
    s_bfrep = sbuf("bfrep", (128, D))
    s_ones = sbuf("ones", (H, SBT))

    s_wld = sbuf("wld", (128, D))
    s_wbf = sbuf("wbf", (128, D), BF16)
    s_wtt = sbuf("wtt", (128, D), BF16)

    s_x = sbuf("sx", (128, 4 * D))
    s_tb = sbuf("tb", (128, D), BF16)
    s_lt = sbuf("lt", (128, 8 * SBT), BF16)
    s_kb = sbuf("kb", (128, NP * SBT), BF16)
    s_qb = sbuf("qb", (128, NP * SBT), BF16)
    s_vb = sbuf("vb", (128, NP * SBT), BF16)
    s_bb = sbuf("bb", (128, NP * SBT), BF16)
    s_gb = sbuf("gb", (128, NP * SBT), BF16)
    s_wrhs = sbuf("wrhs", (128, 2 * 1024), BF16)

    s_pp = sbuf("pp", (H, T + 32))
    s_ppt = sbuf("ppt", (128, NWIN * H))
    s_at = sbuf("at", (H, SBT))
    s_at2 = sbuf("at2", (H, SBT))
    s_pcs = sbuf("spcs", (H, 40))

    s_sc = sbuf("sc", (128, H * WIN))
    s_wdst = sbuf("wdst", (128, H * C))
    s_mt = sbuf("mt", (128, H * C))
    s_prow = sbuf("prow", (128, H * C))
    s_qkb = sbuf("qkb", (16, 2 * NCH * H * C))
    s_wrep = sbuf("wrep", (128, 2 * C * C * NP))
    s_e1 = sbuf("e1", (128, NP * WIN))
    s_e2 = sbuf("e2", (128, NP * WIN))
    s_pw1 = sbuf("pw1", (128, NP * WIN))
    s_pw2 = sbuf("pw2", (128, NP * WIN))
    s_scr = sbuf("scr", (128, NP * WIN))
    s_rr2 = sbuf("rr2", (2, NP * WIN))
    s_rrep = sbuf("rrep", (128, NP * WIN))

    s_u = sbuf("u", (128, NP * C))
    s_ptm = sbuf("ptm", (128, NP * C))
    s_kc = sbuf("kc", (128, NP * C))
    s_qc = sbuf("qc", (128, NP * C))
    s_k2 = sbuf("k2", (128, NP * C))
    s_ut = sbuf("ut", (16, NP * 128))
    s_ktc = sbuf("ktc", (16, NP * 128))
    s_s = sbuf("s", (128, NP * DH))
    s_ys = sbuf("ys", (128, NP * WIN))
    s_qtmp = s_ys
    s_ctx = sbuf("ctx", (128, NP * WIN), BF16)

    s_ao = sbuf("ao", (128, D))
    s_q2 = sbuf("q2", (128, D))
    s_q3 = sbuf("q3", (128, 32))
    s_q4 = sbuf("q4", (128, 32))
    s_qs = s_scr
    s_q1x = s_pw1
    s_mfac = s_pw2
    s_q1 = s_wld
    s_msk = sbuf("msk", (128, D), mybir.dt.uint8)
    s_xd = sbuf("xd", (128, D), BF16)
    s_st = sbuf("st", (128, 8))

    p_pr = psum("ppr", (128, 512))
    p_sc = psum("psc", (128, 512))
    p_tp = psum("ptp", (128, 1024), BF16)
    p_tr = psum("ptr", (16, 512))
    p_kv = psum("pkv", (128, 512))
    p_ys = psum("pys", (128, 512))
    p_su = psum("psu", (128, 512))
    p_sc2 = psum("psc2", (128, 512))

    def dma(q, out, in_, rd, wr, tag=""):
        P.add(q, lambda e, o=out, i=in_: e.dma_start(out=o, in_=i),
              reads=list(rd), writes=list(wr), dma=True, tag=tag)

    def VE(fn, rd, wr, tag=""):
        P.add("vector", fn, reads=list(rd), writes=list(wr), tag=tag)

    def SE(fn, rd, wr, tag=""):
        P.add("scalar", fn, reads=list(rd), writes=list(wr), tag=tag)

    def TE(fn, rd, wr, tag=""):
        P.add("tensor", fn, reads=list(rd), writes=list(wr), tag=tag)

    def GE(fn, rd, wr, tag=""):
        P.add("gpsimd", fn, reads=list(rd), writes=list(wr), tag=tag)

    dma("sync", s_idf[:], d_idf, [], ["idf"])
    dma("sync", s_idb[:], d_idb, [], ["idb"])
    dma("sync", s_ntril[:], d_ntril, [], ["ntril"])
    dma("sync", s_ttri[:], d_ttri, [], ["ttri"])
    dma("sync", s_esel[:], d_esel, [], ["esel"])
    dma("sync", s_e2t[:], d_e2t, [], ["e2t"])
    dma("sync", s_mka[:], d_mska, [], ["mka"])
    dma("sync", s_mkb[:], d_mskb, [], ["mkb"])
    dma("sync", s_dt[:], d_dt, [], ["dt"])
    dma("sync", s_ealog[:], d_alog, [], ["ealog"])
    dma("sync", s_nwcol[:], d_nw.unsqueeze(0).unsqueeze(2).broadcast_to((2, 64, 1)),
        [], ["nwcol"])
    dma("sync", s_borep[:], d_bo.unsqueeze(0).broadcast_to((128, D)), [], ["borep"])
    dma("sync", s_bfrep[:], d_bf.unsqueeze(0).broadcast_to((128, D)), [], ["bfrep"])
    GE(lambda e: e.memset(s_ones[:], 1.0), [], ["ones"])
    GE(lambda e: e.memset(s_pp[:], 0.0), [], ["pp"])
    GE(lambda e: e.memset(s_s[:], 0.0), [], ["S"])
    GE(lambda e: e.memset(s_ys[:], 0.0), [], ["ys"])
    SE(lambda e: e.activation(out=s_ealog[:], in_=s_ealog[:], func=AF.Exp),
       ["ealog"], ["ealog"])
    VE(lambda e: e.tensor_scalar_mul(s_ealog[:], s_ealog[:], -1.0),
       ["ealog"], ["ealog"])

    def emit_quantize(src, dst, key, wkey):
        v3 = src.rearrange("p (a b) -> p a b", b=32)
        VE(lambda e: e.tensor_reduce(out=s_q3[:], in_=v3, axis=AX.X, op=ALU.max,
                                     apply_absolute_value=True), [key], ["q3"])
        VE(lambda e: e.tensor_scalar(out=s_q3[:], in0=s_q3[:], scalar1=1.0 / FP6_MAX,
                                     scalar2=1e-30, op0=ALU.mult, op1=ALU.max),
           ["q3"], ["q3"])
        VE(lambda e: e.reciprocal(s_q4[:], s_q3[:]), ["q3"], ["q4"])
        rb = s_q4[:].unsqueeze(2).broadcast_to((128, 32, 32))
        VE(lambda e: e.tensor_tensor(s_q2[:].rearrange("p (a b) -> p a b", b=32),
                                     v3, rb, ALU.mult), [key, "q4"], ["q2"])
        SE(lambda e: e.activation(out=s_qs[:], in_=s_q2[:], func=AF.Sign),
           ["q2"], ["scr"])
        SE(lambda e: e.activation(out=s_ao[:], in_=s_q2[:], func=AF.Abs),
           ["q2"], ["ao2"])
        VE(lambda e: e.tensor_scalar(out=s_msk[:], in0=s_ao[:], scalar1=4.0,
                                     scalar2=None, op0=ALU.is_ge),
           ["ao2"], ["msk"])
        VE(lambda e: e.scalar_tensor_tensor(out=s_mfac[:], in0=s_qs[:],
                                            scalar=0.125, in1=s_q2[:],
                                            op0=ALU.mult, op1=ALU.add),
           ["scr", "q2"], ["pw2"])
        VE(lambda e: e.scalar_tensor_tensor(out=s_qtmp[:], in0=s_qs[:],
                                            scalar=0.25, in1=s_q2[:],
                                            op0=ALU.mult, op1=ALU.add),
           ["scr", "q2", "ys"], ["qtmp"])
        VE(lambda e: e.select(out=s_mfac[:], mask=s_msk[:], on_true=s_qtmp[:],
                              on_false=s_mfac[:], add_drain=True),
           ["msk", "pw2", "qtmp"], ["pw2"])
        VE(lambda e: e.tensor_scalar(out=s_qtmp[:].bitcast(mybir.dt.int32),
                                     in0=s_mfac[:].bitcast(mybir.dt.int32),
                                     scalar1=s_mkb[:], scalar2=None,
                                     op0=ALU.bitwise_and),
           ["pw2", "mkb", "ys"], ["qtmp"])
        VE(lambda e: e.scalar_tensor_tensor(out=s_q1x[:], in0=s_qs[:],
                                            scalar=2.0625, in1=s_q2[:],
                                            op0=ALU.mult, op1=ALU.add),
           ["scr", "q2"], ["pw1"])
        VE(lambda e: e.tensor_scalar(out=s_mfac[:].bitcast(mybir.dt.int32),
                                     in0=s_q1x[:].bitcast(mybir.dt.int32),
                                     scalar1=s_mka[:], scalar2=None,
                                     op0=ALU.bitwise_and), ["pw1", "mka"], ["pw2"])
        VE(lambda e: e.scalar_tensor_tensor(out=s_q1x[:], in0=s_qs[:],
                                            scalar=-2.0, in1=s_mfac[:],
                                            op0=ALU.mult, op1=ALU.add),
           ["scr", "pw2"], ["pw1"])
        VE(lambda e: e.tensor_scalar(out=s_msk[:], in0=s_ao[:], scalar1=2.0,
                                     scalar2=None, op0=ALU.is_ge),
           ["ao2"], ["msk"])
        VE(lambda e: e.select(out=s_q1x[:], mask=s_msk[:], on_true=s_qtmp[:],
                              on_false=s_q1x[:], add_drain=True),
           ["msk", "pw1", "qtmp"], ["pw1"])
        sb3 = s_q3[:].unsqueeze(2).broadcast_to((128, 32, 32))
        wkeys = wkey if isinstance(wkey, list) else [wkey]
        VE(lambda e, dst=dst: e.tensor_tensor(dst.rearrange("p (a b) -> p a b", b=32),
                                              s_q1x[:].rearrange("p (a b) -> p a b", b=32),
                                              sb3, ALU.mult), ["pw1", "q3"], wkeys)

    def prep_weight(d_src, d_dst, quantize=False, tagn=""):
        for ob in range(8):
            if quantize:
                dma("sync", s_wld[:], d_src[ob * 128:(ob + 1) * 128, :],
                    [], ["wld"], tag="wload")
                emit_quantize(s_wld[:], s_wbf[:], "wld", ["wbfA", "wbfB"])
            else:
                for hb in range(2):
                    dma("gpsimd", s_wbf[:, hb * 512:(hb + 1) * 512],
                        d_src[ob * 128:(ob + 1) * 128, hb * 512:(hb + 1) * 512],
                        [], ["wbf" + "AB"[hb]], tag="wload")
            for fb in range(8):
                if quantize:
                    bank, bkey = (p_kv, "pkv") if fb % 2 == 0 else (p_sc2, "psc2")
                else:
                    bank, bkey = [(p_pr, "ppr"), (p_kv, "pkv"), (p_ys, "pys"),
                                  (p_su, "psu")][fb % 4]
                hk = "wbf" + "AB"[fb // 4]
                wk = "wtt" + "AB"[fb // 4]
                TE(lambda e, fb=fb, bank=bank: e.matmul(
                    bank[:, 0:128],
                    s_wbf[:, fb * 128:(fb + 1) * 128], s_idb[:],
                    start=True, stop=True),
                   [hk, "idb", bkey], [bkey + "w"])
                if fb % 2 == 0:
                    SE(lambda e, fb=fb, bank=bank: e.copy(
                        s_wtt[:, fb * 128:(fb + 1) * 128], bank[:, 0:128]),
                       [bkey + "w"], [bkey, wk])
                else:
                    VE(lambda e, fb=fb, bank=bank: e.tensor_copy(
                        s_wtt[:, fb * 128:(fb + 1) * 128], bank[:, 0:128]),
                       [bkey + "w"], [bkey, wk])
            for hb in range(2):
                dst = d_dst.rearrange("(fb f) o -> f fb o", f=128)[
                    :, hb * 4:(hb + 1) * 4, ob * 128:(ob + 1) * 128]
                src = s_wtt[:, hb * 512:(hb + 1) * 512].rearrange(
                    "f (fb o) -> f fb o", o=128)
                dma("sync", dst, src, ["wtt" + "AB"[hb]], ["dw" + tagn],
                    tag="wstore")

    def prep_weight_alt_iter(d_src, d_dst, ob):
        for hb in range(2):
            dma("gpsimd", s_xd[:, hb * 512:(hb + 1) * 512],
                d_src[ob * 128:(ob + 1) * 128, hb * 512:(hb + 1) * 512],
                [], ["xd" + "AB"[hb]], tag="wload")
        for fb in range(8):
            slot = fb % 4
            hk = "xd" + "AB"[fb // 4]
            wk = "tb" + "AB"[fb // 4]
            TE(lambda e, fb=fb, slot=slot: e.transpose(
                p_tp[:, slot * 128:(slot + 1) * 128],
                s_xd[:, fb * 128:(fb + 1) * 128], s_idb[:]),
               [hk, "idb", "ptp"], ["ptpw"])
            SE(lambda e, fb=fb, slot=slot: e.copy(
                s_tb[:, fb * 128:(fb + 1) * 128],
                p_tp[:, slot * 128:(slot + 1) * 128]),
               ["ptpw"], ["ptp", wk])
        for hb in range(2):
            dst = d_dst.rearrange("(fb f) o -> f fb o", f=128)[
                :, hb * 4:(hb + 1) * 4, ob * 128:(ob + 1) * 128]
            srcs = s_tb[:, hb * 512:(hb + 1) * 512].rearrange(
                "f (fb o) -> f fb o", o=128)
            dma("sync", dst, srcs, ["tb" + "AB"[hb]], ["dwo"], tag="wstore")

    def prep_weight_quant_iter(d_src, d_dst, ob):
        dma("sync", s_wld[:], d_src[ob * 128:(ob + 1) * 128, :],
            [], ["wld"], tag="wload")
        emit_quantize(s_wld[:], s_wbf[:], "wld", ["wbfA", "wbfB"])
        for fb in range(8):
            bank, bkey = (p_kv, "pkv") if fb % 2 == 0 else (p_sc2, "psc2")
            hk = "wbf" + "AB"[fb // 4]
            wk = "wtt" + "AB"[fb // 4]
            TE(lambda e, fb=fb, bank=bank: e.matmul(
                bank[:, 0:128],
                s_wbf[:, fb * 128:(fb + 1) * 128], s_idb[:],
                start=True, stop=True),
               [hk, "idb", bkey], [bkey + "w"])
            if fb % 2 == 0:
                SE(lambda e, fb=fb, bank=bank: e.copy(
                    s_wtt[:, fb * 128:(fb + 1) * 128], bank[:, 0:128]),
                   [bkey + "w"], [bkey, wk])
            else:
                VE(lambda e, fb=fb, bank=bank: e.tensor_copy(
                    s_wtt[:, fb * 128:(fb + 1) * 128], bank[:, 0:128]),
                   [bkey + "w"], [bkey, wk])
        for hb in range(2):
            dst = d_dst.rearrange("(fb f) o -> f fb o", f=128)[
                :, hb * 4:(hb + 1) * 4, ob * 128:(ob + 1) * 128]
            srcs = s_wtt[:, hb * 512:(hb + 1) * 512].rearrange(
                "f (fb o) -> f fb o", o=128)
            dma("sync", dst, srcs, ["wtt" + "AB"[hb]], ["dwf"], tag="wstore")

    def emit_preamble():
        proj_x_part(0)
        for m in "qkvgb":
            prep_weight(d_w[m], d_wt[m], tagn=m)
        for ob in range(8):
            prep_weight_alt_iter(d_wo, d_wto, ob)
            prep_weight_quant_iter(d_wf, d_wtf, ob)
        dma("sync", s_wld[0:16, :], d_wa, [], ["wld"])
        VE(lambda e: e.tensor_copy(s_wbf[0:16, :], s_wld[0:16, :]), ["wld"],
           ["wbfA", "wbfB"])
        for fb in range(8):
            slot = fb % 4
            TE(lambda e, fb=fb, slot=slot: e.transpose(
                p_tp[:, slot * 128:slot * 128 + 16],
                s_wbf[0:16, fb * 128:(fb + 1) * 128], s_idb[0:16, 0:16]),
               ["wbfA", "wbfB", "idb", "ptp"], ["ptpw"])
            SE(lambda e, fb=fb, slot=slot: e.copy(
                s_wtt[:, fb * 16:(fb + 1) * 16], p_tp[:, slot * 128:slot * 128 + 16]),
               ["ptpw"], ["ptp", "wttA"])
        dma("sync", d_wat.rearrange("(fb f) h -> f fb h", f=128),
            s_wtt[:, 0:128].rearrange("f (fb h) -> f fb h", h=16), ["wttA"],
            ["wttA", "dwa"])
        proj_mm_part(0, hook=lambda: (norms_window(0, 0),
                                      scores_and_masks(0, 0)))

    def ln_stats(xap, key):
        SE(lambda e: e.activation(out=s_q2[:], in_=xap, func=AF.Square,
                                  accum_out=s_st[:, 1:2]), [key], ["q2", "st"])
        VE(lambda e: e.tensor_reduce(out=s_st[:, 0:1], in_=xap, axis=AX.X,
                                     op=ALU.add, negate=True), [key, "st"], ["st"])
        VE(lambda e: e.tensor_scalar_mul(s_st[:, 0:1], s_st[:, 0:1], 1.0 / D),
           ["st"], ["st"])
        VE(lambda e: e.tensor_tensor(s_st[:, 2:3], s_st[:, 0:1], s_st[:, 0:1],
                                     ALU.mult), ["st"], ["st"])
        VE(lambda e: e.tensor_scalar(out=s_st[:, 1:2], in0=s_st[:, 1:2],
                                     scalar1=1.0 / D, scalar2=None, op0=ALU.mult),
           ["st"], ["st"])
        VE(lambda e: e.tensor_tensor(s_st[:, 1:2], s_st[:, 1:2], s_st[:, 2:3],
                                     ALU.subtract), ["st"], ["st"])
        VE(lambda e: e.tensor_scalar(out=s_st[:, 1:2], in0=s_st[:, 1:2],
                                     scalar1=EPS_LN, scalar2=None, op0=ALU.add),
           ["st"], ["st"])
        SE(lambda e: e.activation(out=s_st[:, 1:2], in_=s_st[:, 1:2], func=AF.Sqrt),
           ["st"], ["st"])
        VE(lambda e: e.reciprocal(s_st[:, 1:2], s_st[:, 1:2]), ["st"], ["st"])

    def proj_x_part(sb):
        xsrc = d_x.rearrange("(n p) d -> p n d", p=128)[:, sb * 4:(sb + 1) * 4, :]
        dma("sync", s_x[:].rearrange("p (n d) -> p n d", d=D), xsrc,
            ["x"], ["x"], tag="xload")
        for wi in range(4):
            xt = s_x[:, wi * D:(wi + 1) * D]
            ln_stats(xt, "x")
            VE(lambda e, xt=xt: e.tensor_scalar(out=s_tb[:], in0=xt,
                                                scalar1=s_st[:, 0:1],
                                                scalar2=s_st[:, 1:2],
                                                op0=ALU.add, op1=ALU.mult),
               ["x", "st"], ["tb"])
            for fb in range(8):
                bank, bkey = (p_sc, "psc") if fb % 2 == 0 else (p_sc2, "psc2")
                TE(lambda e, fb=fb, bank=bank: e.matmul(
                    bank[:, 0:128],
                    s_tb[:, fb * 128:(fb + 1) * 128], s_idb[:],
                    start=True, stop=True),
                   ["tb", "idb", bkey], [bkey + "w"])
                if fb % 2 == 0:
                    SE(lambda e, fb=fb, bank=bank, wi=wi: e.copy(
                        s_lt[:, fb * SBT + wi * 128: fb * SBT + (wi + 1) * 128],
                        bank[:, 0:128]),
                       [bkey + "w"], [bkey, "lt"])
                else:
                    VE(lambda e, fb=fb, bank=bank, wi=wi: e.tensor_copy(
                        s_lt[:, fb * SBT + wi * 128: fb * SBT + (wi + 1) * 128],
                        bank[:, 0:128]),
                       [bkey + "w"], [bkey, "lt"])
    def proj_mm_part(sb, hook=None):
        src = d_wat.rearrange("(fb f) h -> f fb h", f=128)
        dma("sync", s_wrhs[:, 0:128].rearrange("f (fb h) -> f fb h", h=16), src,
            ["ws0", "dwa"], ["ws0"])
        for fb in range(8):
            TE(lambda e, fb=fb: e.matmul(
                p_pr[0:16, :], s_wrhs[:, fb * 16:(fb + 1) * 16],
                s_lt[:, fb * SBT:(fb + 1) * SBT],
                start=(fb == 0), stop=(fb == 7)),
               ["ws0", "lt", "ppr"] if fb == 0 else ["ws0", "lt"],
               ["pprw"] if fb == 7 else [])
        SE(lambda e: e.activation(out=s_at[:], in_=p_pr[0:16, :], func=AF.Identity,
                                  bias=s_dt[:]), ["pprw", "dt"], ["at", "ppr"])
        SE(lambda e: e.activation(out=s_at2[:], in_=s_at[:], func=AF.Abs),
           ["at"], ["at2"])
        SE(lambda e: e.activation(out=s_at2[:], in_=s_at2[:], func=AF.Exp,
                                  scale=-1.0), ["at2"], ["at2"])
        SE(lambda e: e.activation(out=s_at2[:], in_=s_at2[:], func=AF.Ln,
                                  bias=1.0), ["at2"], ["at2"])
        VE(lambda e: e.tensor_scalar_max(s_at[:], s_at[:], 0.0), ["at"], ["at"])
        VE(lambda e: e.tensor_tensor(s_at[:], s_at[:], s_at2[:], ALU.add),
           ["at", "at2"], ["at"])
        VE(lambda e: e.tensor_scalar_mul(s_at[:], s_at[:], s_ealog[:]),
           ["at", "ealog"], ["at"])
        VE(lambda e, sb=sb: e.tensor_tensor_scan(
            out=s_pp[:, 1 + sb * SBT: 1 + (sb + 1) * SBT], data0=s_ones[:],
            data1=s_at[:], initial=s_pp[:, sb * SBT:sb * SBT + 1],
            op0=ALU.mult, op1=ALU.add), ["at", "ones", "pp"], ["pp"])
        dma("sync", d_pd[:, sb * SBT: 1 + (sb + 1) * SBT],
            s_pp[:, sb * SBT: 1 + (sb + 1) * SBT], ["pp"], ["pd"])
        pcsv = s_pp[0:16, :].rearrange("h (b i) -> h b i", i=16)[
            :, sb * 32:sb * 32 + 33, 0]
        VE(lambda e, pcsv=pcsv, sb=sb: e.tensor_copy(s_pcs[0:16, 0:33], pcsv),
           ["pp"], ["spcs"])
        dma("sync", d_pcs[:, sb * 32:sb * 32 + 33], s_pcs[0:16, 0:33],
            ["spcs"], ["pcs"])
        for wi in range(4):
            w = sb * 4 + wi
            TE(lambda e, w=w: e.transpose(
                p_sc[:, 0:16], s_pp[0:16, 1 + w * WIN: 1 + (w + 1) * WIN],
                s_idf[0:16, 0:16]),
               ["pp", "idf", "psc"], ["pscw"])
            SE(lambda e, w=w: e.copy(s_ppt[:, w * H:(w + 1) * H], p_sc[:, 0:16]),
               ["pscw"], ["psc", "ppt"])
        projs = [("q", s_qb, AF.Identity), ("k", s_kb, AF.Identity),
                 ("v", s_vb, AF.Identity), ("g", s_gb, AF.Silu),
                 ("b", s_bb, AF.Sigmoid)]
        for m, dstt, act in projs:
            for oc in range(8):
                par = oc % 2
                wsl = s_wrhs[:, par * 1024:(par + 1) * 1024]
                src = d_wt[m].rearrange("(fb f) o -> f fb o", f=128)[
                    :, :, oc * 128:(oc + 1) * 128]
                skeys = [f"ws{par * 2}", f"ws{par * 2 + 1}"]
                dma("sync" if par == 0 else "gpsimd",
                    wsl.rearrange("f (fb o) -> f fb o", o=128), src,
                    skeys + ["dw" + m], skeys, tag="wstream")
                for fb in range(8):
                    sk = f"ws{par * 2 + fb // 4}"
                    TE(lambda e, fb=fb, wsl=wsl: e.matmul(
                        p_pr[:], wsl[:, fb * 128:(fb + 1) * 128],
                        s_lt[:, fb * SBT:(fb + 1) * SBT],
                        start=(fb == 0), stop=(fb == 7)),
                       [sk, "lt", "ppr"] if fb == 0 else [sk, "lt"],
                       ["pprw"] if fb == 7 else [])
                SE(lambda e, dstt=dstt, oc=oc, act=act: e.activation(
                    out=dstt[:, oc * SBT:(oc + 1) * SBT], in_=p_pr[:], func=act),
                   ["pprw"], ["ppr", m + "b"])
            if m == "k" and hook is not None:
                hook()

    def win_slice(tile, wi, p0=0, p1=128):
        return tile[p0:p1, :].rearrange("p (h t) -> p h t", h=NP)[
            :, :, wi * 128:(wi + 1) * 128]

    def norms_window(sb, wi):
        w = sb * 4 + wi
        for which, tile, drn, scale in [("k", s_kb, d_rk, 1.0),
                                        ("q", s_qb, d_rq, 64.0)]:
            ksl = win_slice(tile, wi)
            SE(lambda e, ksl=ksl: e.activation(
                out=s_scr[:].rearrange("p (h t) -> p h t", h=NP), in_=ksl,
                func=AF.Square), [which + "b"], ["scr"])
            for half in range(2):
                TE(lambda e, half=half: e.matmul(
                    p_tr[0:2, 0:512], s_esel[:],
                    s_scr[:, half * 512:(half + 1) * 512], start=True, stop=True),
                   ["scr", "esel", "ptr"], ["ptrw"])
                SE(lambda e, scale=scale, half=half: e.activation(
                    out=s_rr2[:, half * 512:(half + 1) * 512], in_=p_tr[0:2, 0:512],
                    func=AF.Sqrt, scale=scale),
                   ["ptrw"], ["ptr", "rr2"])
            VE(lambda e: e.reciprocal(s_rr2[:], s_rr2[:]), ["rr2"], ["rr2"])
            for half, bank, bkey in ((0, p_kv, "pkv"), (1, p_ys, "pys")):
                TE(lambda e, half=half, bank=bank: e.matmul(
                    bank[:, 0:512], s_e2t[0:2, :],
                    s_rr2[0:2, half * 512:(half + 1) * 512],
                    start=True, stop=True),
                   ["rr2", "e2t", bkey], [bkey + "w"])
                if half == 0:
                    SE(lambda e, half=half, bank=bank: e.copy(
                        s_rrep[:, half * 512:(half + 1) * 512], bank[:, 0:512]),
                       [bkey + "w"], [bkey, "rrep"])
                else:
                    VE(lambda e, half=half, bank=bank: e.tensor_copy(
                        s_rrep[:, half * 512:(half + 1) * 512], bank[:, 0:512]),
                       [bkey + "w"], [bkey, "rrep"])
            if which == "k":
                VE(lambda e, ksl=ksl: e.tensor_tensor(
                    s_pw1[:].rearrange("p (h t) -> p h t", h=NP), ksl,
                    s_rrep[:].rearrange("p (h t) -> p h t", h=NP), ALU.mult),
                   [which + "b", "rrep"], ["pw1"])
                VE(lambda e, ksl=ksl: e.tensor_copy(
                    ksl, s_pw1[:].rearrange("p (h t) -> p h t", h=NP)),
                   ["pw1"], [which + "b"])
            else:
                VE(lambda e, ksl=ksl: e.tensor_tensor(
                    ksl, ksl, s_rrep[:].rearrange("p (h t) -> p h t", h=NP),
                    ALU.mult),
                   [which + "b", "rrep"], [which + "b"])

    def _score_mms(wi, incl, h0, h1):
        for h in range(h0, h1):
            hp, h01 = h // 2, h % 2
            bank, bkey = (p_sc, "psc") if h % 2 == 0 else (p_sc2, "psc2")
            slot = (h // 2) % 4
            if incl:
                lhs = win_slice(s_kb, wi, h01 * 64, (h01 + 1) * 64)[:, hp]
                rhs = win_slice(s_qb, wi, h01 * 64, (h01 + 1) * 64)[:, hp]
                rkeys = ["kb", "qb", bkey]
            else:
                lhs = s_pw1[h01 * 64:(h01 + 1) * 64,
                            hp * 128:(hp + 1) * 128]
                rhs = lhs
                rkeys = ["pw1", bkey]
            TE(lambda e, lhs=lhs, rhs=rhs, slot=slot, bank=bank: e.matmul(
                bank[:, (slot) * 128:(slot + 1) * 128], lhs, rhs,
                start=True, stop=True),
               rkeys, [bkey + "w"])
            SE(lambda e, h=h, slot=slot, bank=bank: e.copy(
                s_sc[:, h * 128:(h + 1) * 128],
                bank[:, slot * 128:(slot + 1) * 128]),
               [bkey + "w"], [bkey, "sc"])

    def _score_extract(w, with_prow):
        for k in range(NCH):
            srcx = s_sc[16 * k:16 * (k + 1), :].rearrange(
                "p (h i) -> p h i", h=H)[:, :, 16 * k:16 * (k + 1)]
            dstx = s_wdst[16 * k:16 * (k + 1), :].rearrange(
                "p (h i) -> p h i", h=H)
            dma("gpsimd", dstx, srcx, ["sc"], ["wdst"], tag="diagx")
        if with_prow:
            for c_ in range(NCH):
                pr = d_pd[:, 1 + w * WIN + 16 * c_:1 + w * WIN + 16 * (c_ + 1)] \
                    .unsqueeze(0).broadcast_to((16, H, 16))
                dma("gpsimd", s_prow[16 * c_:16 * (c_ + 1), :], pr,
                    ["pd"], ["prow"], tag="prow")

    def _score_maskdump(w, incl):
        par = w % 2
        dumpd = d_qk if incl else d_wd
        dkey = ("dqk" if incl else "dwd") + str(par)
        pcol = s_ppt[:, w * H:(w + 1) * H].unsqueeze(2).broadcast_to((128, H, 16))
        if incl:
            GE(lambda e, pcol=pcol: e.tensor_tensor(
                s_mt[:].rearrange("p (h i) -> p h i", h=H),
                s_prow[:].rearrange("p (h i) -> p h i", h=H), pcol,
                ALU.subtract),
               ["prow", "ppt"], ["mt"])
        else:
            GE(lambda e, pcol=pcol: e.tensor_tensor(
                s_mt[:].rearrange("p (h i) -> p h i", h=H), pcol,
                s_prow[:].rearrange("p (h i) -> p h i", h=H),
                ALU.subtract),
               ["prow", "ppt"], ["mt"])
        GE(lambda e: e.tensor_scalar_min(s_mt[:], s_mt[:], 0.0), ["mt"], ["mt"])
        SE(lambda e: e.activation(out=s_mt[:], in_=s_mt[:], func=AF.Exp),
           ["mt"], ["mt"])
        GE(lambda e: e.tensor_tensor(s_wdst[:], s_wdst[:], s_mt[:], ALU.mult),
           ["mt", "wdst"], ["wdst"])
        mk = s_ttri if incl else s_ntril
        mkk = "ttri" if incl else "ntril"
        GE(lambda e, mk=mk: e.tensor_tensor(s_wdst[:], s_wdst[:], mk[:],
                                            ALU.mult),
           ["wdst", mkk], ["wdst"])
        for c_ in range(NCH):
            dstd = dumpd[par, c_].rearrange("h j i -> j h i")
            dma("gpsimd", dstd,
                s_wdst[16 * c_:16 * (c_ + 1), :].rearrange(
                    "p (h i) -> p h i", h=H),
                ["wdst"], [dkey], tag="wdump")

    def _score_qkload(w):
        par = w % 2
        srcq = d_qk[par].rearrange("c h i j -> i c h j")
        dma("sync", s_qkb[0:16, par * (NCH * H * C):(par + 1) * (NCH * H * C)]
            .rearrange("i (c h j) -> i c h j", c=NCH, h=H),
            srcq, [f"dqk{par}"], [f"qkb{par}"], tag="qkload")

    def scores_pieces(sb, wi):
        w = sb * 4 + wi
        return [
            lambda: _score_mms(wi, False, 0, 8),
            lambda: _score_mms(wi, False, 8, 16),
            lambda: _score_extract(w, True),
            lambda: _score_maskdump(w, False),
            lambda: _score_mms(wi, True, 0, 8),
            lambda: _score_mms(wi, True, 8, 16),
            lambda: (_score_extract(w, False), _score_maskdump(w, True)),
            lambda: _score_qkload(w),
        ]

    def scores_and_masks(sb, wi):
        for piece in scores_pieces(sb, wi):
            piece()
    def proj_superblock(sb):
        proj_x_part(sb)
        proj_mm_part(sb, hook=lambda: (norms_window(sb, 0),
                                       scores_and_masks(sb, 0)))

    def ebuild_window(sb, wi):
        w = sb * 4 + wi
        par = w % 2
        for h01 in range(2):
            pt = d_pd.rearrange("(hp h2) t -> h2 hp t", h2=2)[h01] \
                     .unsqueeze(0).broadcast_to((64, NP, T + 32))
            dma("sync", s_ao[h01 * 64:(h01 + 1) * 64, :],
                pt[:, :, 1 + w * WIN:1 + (w + 1) * WIN], ["pd"], ["ao2"])
        for h01 in range(2):
            pcv = d_pcs.rearrange("(hp h2) b -> h2 hp b", h2=2)[
                h01, :, w * NCH:(w + 1) * NCH]
            pcsb = pcv.unsqueeze(0).broadcast_to((64, NP, NCH))
            dma("sync", s_pw2[h01 * 64:(h01 + 1) * 64, 0:64], pcsb,
                ["pcs"], ["pw2"])
        pw2b = s_pw2[:, 0:64].rearrange("p (h c) -> p h c", h=NP) \
            .unsqueeze(3).broadcast_to((128, NP, NCH, 16))
        GE(lambda e, pw2b=pw2b: e.tensor_tensor(
            s_e1[:].rearrange("p (h c i) -> p h c i", h=NP, c=NCH),
            s_ao[:].rearrange("p (h c i) -> p h c i", h=NP, c=NCH),
            pw2b, ALU.subtract),
           ["ao2", "pw2"], ["e1"])
        SE(lambda e, par=par: e.activation(
            out=s_e1[:],
            in_=s_e1[:], func=AF.Exp),
           ["e1"], ["e1"])
        for h01 in range(2):
            pev = d_pcs.rearrange("(hp h2) b -> h2 hp b", h2=2)[
                h01, :, w * NCH + 1:(w + 1) * NCH + 1]
            pceb = pev.unsqueeze(0).broadcast_to((64, NP, NCH))
            dma("sync", s_pw2[h01 * 64:(h01 + 1) * 64, 0:64], pceb,
                ["pcs", "e1"], ["pw2"])
        pw2c = s_pw2[:, 0:64].rearrange("p (h c) -> p h c", h=NP) \
            .unsqueeze(3).broadcast_to((128, NP, NCH, 16))
        GE(lambda e, pw2c=pw2c: e.tensor_tensor(
            s_e2[:].rearrange("p (h c i) -> p h c i", h=NP, c=NCH),
            pw2c,
            s_ao[:].rearrange("p (h c i) -> p h c i", h=NP, c=NCH),
            ALU.subtract),
           ["ao2", "pw2"], ["e2"])
        SE(lambda e, par=par: e.activation(
            out=s_e2[:],
            in_=s_e2[:], func=AF.Exp),
           ["e2"], ["e2"])

    def wrep_load(w, k):
        par, cp = w % 2, k % 2
        for h01 in range(2):
            srcw = d_wd[par, k].rearrange("(hp h2) j i -> h2 hp (j i)", h2=2)[h01] \
                               .unsqueeze(0).broadcast_to((64, NP, C * C))
            dma("sync", s_wrep[h01 * 64:(h01 + 1) * 64,
                               cp * 2048:(cp + 1) * 2048],
                srcw, [f"dwd{par}"], [f"wrep{cp}"], tag="wrepload")
        wv = s_wrep[:, cp * 2048:(cp + 1) * 2048].rearrange(
            "p (h j i) -> p h j i", h=NP, j=C)
        bb3 = win_slice(s_bb, w % 4)[:, :, k * C:(k + 1) * C] \
            .unsqueeze(3).broadcast_to((128, NP, C, C))
        VE(lambda e, wv=wv, bb3=bb3: e.tensor_tensor(wv, wv, bb3, ALU.mult),
           [f"wrep{cp}", "bb"], [f"wrep{cp}"], tag="wrepscale")

    def chunk_builds(sb, wi, k):
        e1sl = s_e1[:].rearrange("p (h t) -> p h t", h=NP)[:, :, k * C:(k + 1) * C]
        e2sl = s_e2[:].rearrange("p (h t) -> p h t", h=NP)[:, :, k * C:(k + 1) * C]
        kbsl = win_slice(s_kb, wi)[:, :, k * C:(k + 1) * C]
        qbsl = win_slice(s_qb, wi)[:, :, k * C:(k + 1) * C]
        VE(lambda e: e.tensor_tensor(s_kc[:].rearrange("p (h t) -> p h t", h=NP),
                                     kbsl, e1sl, ALU.mult),
           ["kb", "e1"], ["kc"])
        VE(lambda e: e.tensor_tensor(s_qc[:].rearrange("p (h t) -> p h t", h=NP),
                                     qbsl, e1sl, ALU.mult),
           ["qb", "e1"], ["qc"])
        VE(lambda e: e.tensor_tensor(s_k2[:].rearrange("p (h t) -> p h t", h=NP),
                                     kbsl, e2sl, ALU.mult),
           ["kb", "e2"], ["k2"])

    def chunk_steps(sb, wi, k, piece=None):
        w = sb * 4 + wi
        par, cp = w % 2, k % 2
        e1v = s_e1[:].rearrange(
            "p (h t) -> p h t", h=NP)
        vbsl = win_slice(s_vb, wi)[:, :, k * C:(k + 1) * C]
        bbsl = win_slice(s_bb, wi)[:, :, k * C:(k + 1) * C]
        uv = s_u[:].rearrange("p (h t) -> p h t", h=NP)
        for hp in range(NP):
            for h01 in range(2):
                b0 = h01 * 64
                lhs = s_s[b0:b0 + 64, hp * 64:(hp + 1) * 64]
                last = (hp == NP - 1 and h01 == 1)
                first = (hp == 0)
                TE(lambda e, lhs=lhs, hp=hp, b0=b0, first=first: e.matmul(
                    p_kv[b0:b0 + 64, hp * C:(hp + 1) * C], lhs,
                    s_kc[b0:b0 + 64, hp * C:(hp + 1) * C], start=first,
                    stop=True, skip_group_check=True),
                   ["S", "kc", "pkv"], ["pkvw"] if last else [])
                TE(lambda e, lhs=lhs, hp=hp, b0=b0, first=first: e.matmul(
                    p_ys[b0:b0 + 64, hp * C:(hp + 1) * C], lhs,
                    s_qc[b0:b0 + 64, hp * C:(hp + 1) * C], start=first,
                    stop=True, skip_group_check=True),
                   ["S", "qc", "pys"], [])
        for rnd in range(2):
            for hpi in range(4):
                hp = rnd * 4 + hpi
                TE(lambda e, hp=hp, hpi=hpi: e.transpose(
                    p_tr[0:16, hpi * 128:(hpi + 1) * 128],
                    s_k2[:, hp * C:(hp + 1) * C],
                    s_idf[:]), ["k2", "idf", "ptr"],
                   ["ptrw"] if hpi == 3 else [])
            SE(lambda e, rnd=rnd: e.copy(
                s_ktc[0:16, rnd * 512:(rnd + 1) * 512], p_tr[0:16, 0:512]),
               ["ptrw"], ["ptr", "ktc"])
        if piece is not None:
            piece()
        gcol = e1v[:, :, k * C + C - 1].unsqueeze(2).broadcast_to((128, NP, 64))
        sv = s_s[:].rearrange("p (h e) -> p h e", h=NP)
        VE(lambda e, gcol=gcol, sv=sv: e.tensor_tensor(sv, sv, gcol, ALU.mult),
           ["S", "e1"], ["S"])
        VE(lambda e: e.tensor_tensor(
            uv, vbsl, p_kv[:, 0:128].rearrange("p (h t) -> p h t", h=NP),
            ALU.subtract),
           ["vb", "pkvw"], ["u", "pkv"])
        VE(lambda e: e.tensor_tensor(uv, uv, bbsl, ALU.mult),
           ["u", "bb"], ["u"])
        if k + 1 < NCH:
            chunk_builds(sb, wi, k + 1)
        for m in range(C - 1):
            ucol = uv[:, :, m]
            width = C - 1 - m
            wsl = s_wrep[:, cp * 2048:(cp + 1) * 2048].rearrange(
                "p (h j i) -> p h j i", h=NP, j=C)[:, :, m + 1:, m]
            ub = ucol.unsqueeze(2).broadcast_to((128, NP, width))
            ptv = s_ptm[:].rearrange("p (h t) -> p h t", h=NP)[:, :, 0:width]
            VE(lambda e, wsl=wsl, ub=ub, ptv=ptv: e.tensor_tensor(
                ptv, wsl, ub, ALU.mult), ["u", f"wrep{cp}"], ["ptm"],
               tag="slv_p")
            usl = uv[:, :, m + 1:]
            VE(lambda e, usl=usl, ptv=ptv: e.tensor_tensor(
                usl, usl, ptv, ALU.add), ["u", "ptm"], ["u"], tag="slv_u")
        for rnd in range(2):
            for hpi in range(4):
                hp = rnd * 4 + hpi
                TE(lambda e, hp=hp, hpi=hpi: e.transpose(
                    p_tr[0:16, hpi * 128:(hpi + 1) * 128],
                    s_u[:, hp * C:(hp + 1) * C],
                    s_idf[:]), ["u", "idf", "ptr"],
                   ["ptrw"] if hpi == 3 else [])
            if rnd == 0:
                SE(lambda e, rnd=rnd: e.copy(
                    s_ut[0:16, rnd * 512:(rnd + 1) * 512], p_tr[0:16, 0:512]),
                   ["ptrw"], ["ptr", "ut"])
            else:
                VE(lambda e, rnd=rnd: e.tensor_copy(
                    s_ut[0:16, rnd * 512:(rnd + 1) * 512], p_tr[0:16, 0:512]),
                   ["ptrw"], ["ptr", "ut"])
        for hp in range(NP):
            for h01 in range(2):
                b0 = h01 * 64
                last = (hp == NP - 1 and h01 == 1)
                first = (hp == 0)
                TE(lambda e, hp=hp, b0=b0, first=first: e.matmul(
                    p_su[b0:b0 + 64, hp * 64:(hp + 1) * 64],
                    s_ktc[0:16, hp * 128 + b0:hp * 128 + b0 + 64],
                    s_ut[0:16, hp * 128 + b0:hp * 128 + b0 + 64],
                    start=first, stop=True, skip_group_check=True),
                   ["ktc", "ut", "psu"], ["psuw"] if last else [])
        VE(lambda e, sv=sv: e.tensor_tensor(
            sv, sv, p_su[:].rearrange("p (h e) -> p h e", h=NP), ALU.add),
           ["S", "psuw"], ["S", "psu"])
        qkoff = par * (NCH * H * C) + k * (H * C)
        for h in range(H):
            hp, h01 = h // 2, h % 2
            b0 = h01 * 64
            TE(lambda e, hp=hp, b0=b0, h=h, qkoff=qkoff: e.matmul(
                p_ys[b0:b0 + 64, hp * C:(hp + 1) * C],
                s_ut[0:16, hp * 128 + b0:hp * 128 + b0 + 64],
                s_qkb[0:16, qkoff + h * C:qkoff + (h + 1) * C],
                start=False, stop=True, skip_group_check=True),
               ["ut", f"qkb{par}", "pys"], ["pysw"] if h == H - 1 else [])
        yssl = s_ys[:].rearrange("p (h t) -> p h t", h=NP)[:, :, k * C:(k + 1) * C]
        SE(lambda e, yssl=yssl: e.copy(
            yssl, p_ys[:, 0:128].rearrange("p (h t) -> p h t", h=NP)),
           ["pysw"], ["pys", "ys"])

    def window_post(sb, wi):
        w = sb * 4 + wi
        SE(lambda e: e.activation(out=s_scr[:], in_=s_ys[:], func=AF.Square),
           ["ys"], ["scr"])
        for half in range(2):
            TE(lambda e, half=half: e.matmul(
                p_tr[0:2, 0:512], s_esel[:],
                s_scr[:, half * 512:(half + 1) * 512], start=True, stop=True),
               ["scr", "esel", "ptr"], ["ptrw"])
            VE(lambda e, half=half: e.tensor_scalar(
                out=s_rr2[:, half * 512:(half + 1) * 512], in0=p_tr[0:2, 0:512],
                scalar1=1.0 / DH, scalar2=EPS_RMS, op0=ALU.mult, op1=ALU.add),
               ["ptrw"], ["ptr", "rr2"])
        SE(lambda e: e.activation(out=s_rr2[:], in_=s_rr2[:], func=AF.Sqrt),
           ["rr2"], ["rr2"])
        VE(lambda e: e.reciprocal(s_rr2[:], s_rr2[:]), ["rr2"], ["rr2"])
        for half, bank, bkey in ((0, p_kv, "pkv"), (1, p_ys, "pys")):
            TE(lambda e, half=half, bank=bank: e.matmul(
                bank[:, 0:512], s_e2t[0:2, :],
                s_rr2[0:2, half * 512:(half + 1) * 512],
                start=True, stop=True),
               ["rr2", "e2t", bkey], [bkey + "w"])
            if half == 0:
                SE(lambda e, half=half, bank=bank: e.copy(
                    s_rrep[:, half * 512:(half + 1) * 512], bank[:, 0:512]),
                   [bkey + "w"], [bkey, "rrep"])
            else:
                VE(lambda e, half=half, bank=bank: e.tensor_copy(
                    s_rrep[:, half * 512:(half + 1) * 512], bank[:, 0:512]),
                   [bkey + "w"], [bkey, "rrep"])
        VE(lambda e: e.tensor_tensor(s_ys[:], s_ys[:], s_rrep[:], ALU.mult),
           ["ys", "rrep"], ["ys"])
        gsl = win_slice(s_gb, wi)
        VE(lambda e, gsl=gsl, wi=wi: e.scalar_tensor_tensor(
            out=s_lt[:, wi * D:(wi + 1) * D].rearrange("p (h t) -> p h t", h=NP),
            in0=s_ys[:].rearrange("p (h t) -> p h t", h=NP),
            scalar=s_nwcol[:], in1=gsl, op0=ALU.mult, op1=ALU.mult),
           ["ys", "nwcol", "gb"], ["lt"])

    OUT_BANKS = [(p_pr, "ppr"), (p_kv, "pkv"), (p_ys, "pys"), (p_su, "psu")]
    STAGING = [(s_ao, "ao2"), (s_scr, "scr"), (s_e1, "e1"), (s_e2, "e2")]

    def out_ln_residual(wi):
        stag, skey = STAGING[wi]
        ln_stats(stag[:], skey)
        VE(lambda e, stag=stag: e.tensor_scalar(
            out=stag[:], in0=stag[:], scalar1=s_st[:, 0:1],
            scalar2=s_st[:, 1:2], op0=ALU.add, op1=ALU.mult),
           [skey, "st"], [skey])
        xt = s_x[:, wi * D:(wi + 1) * D]
        VE(lambda e, xt=xt, stag=stag: e.tensor_tensor(
            xt, xt, stag[:], ALU.add), ["x", skey], ["x"])

    def attn_out_batch(sb):
        for cc in range(4):
            wsl = s_wrhs[:, cc * 512:cc * 512 + 512]
            dma("sync" if cc < 2 else "gpsimd", wsl,
                d_wto[cc * 128:(cc + 1) * 128, 0:512],
                [f"ws{cc}", "dwo"], [f"ws{cc}"], tag="wostream")
        for oh in range(2):
            for cc in range(8):
                par2 = cc % 4
                wsl = s_wrhs[:, par2 * 512:par2 * 512 + 512]
                if not (oh == 0 and cc < 4):
                    dma("sync" if par2 < 2 else "gpsimd", wsl,
                        d_wto[cc * 128:(cc + 1) * 128, oh * 512:(oh + 1) * 512],
                        [f"ws{par2}", "dwo"], [f"ws{par2}"], tag="wostream")
                for wi in range(4):
                    bank, bkey = OUT_BANKS[wi]
                    TE(lambda e, cc=cc, wsl=wsl, wi=wi, bank=bank: e.matmul(
                        bank[:], s_lt[:, wi * D + cc * 128:wi * D + (cc + 1) * 128],
                        wsl, start=(cc == 0), stop=(cc == 7)),
                       [f"ws{par2}", "lt", bkey] if cc == 0
                       else [f"ws{par2}", "lt"],
                       [bkey + "w"] if cc == 7 else [])
            for wi in range(4):
                stag, skey = STAGING[wi]
                bank, bkey = OUT_BANKS[wi]
                VE(lambda e, oh=oh, stag=stag, bank=bank: e.tensor_tensor(
                    stag[:, oh * 512:(oh + 1) * 512], bank[:],
                    s_borep[:, oh * 512:(oh + 1) * 512], ALU.add),
                   [bkey + "w", "borep"], [bkey, skey])
        for wi in range(4):
            out_ln_residual(wi)

    def ffn_superblock(sb):
        for fb in range(4):
            wsl = s_wrhs[:, fb * 512:fb * 512 + 512]
            dma("sync" if fb < 2 else "gpsimd", wsl,
                d_wtf[fb * 128:(fb + 1) * 128, 0:512],
                [f"ws{fb}", "dwf"], [f"ws{fb}"], tag="wfstream")
        for wi in range(4):
            xt = s_x[:, wi * D:(wi + 1) * D]
            ln_stats(xt, "x")
            VE(lambda e, xt=xt: e.tensor_scalar(out=s_q1[:], in0=xt,
                                                scalar1=s_st[:, 0:1],
                                                scalar2=s_st[:, 1:2],
                                                op0=ALU.add, op1=ALU.mult),
               ["x", "st"], ["wld"])
            emit_quantize(s_q1[:], s_xd[:], "wld", "xd")
            for fb in range(8):
                bank, bkey = (p_sc, "psc") if fb % 2 == 0 else (p_sc2, "psc2")
                TE(lambda e, fb=fb, bank=bank: e.matmul(
                    bank[:, 0:128],
                    s_xd[:, fb * 128:(fb + 1) * 128], s_idb[:],
                    start=True, stop=True),
                   ["xd", "idb", bkey], [bkey + "w"])
                if fb % 2 == 0:
                    SE(lambda e, fb=fb, bank=bank, wi=wi: e.copy(
                        s_lt[:, fb * SBT + wi * 128: fb * SBT + (wi + 1) * 128],
                        bank[:, 0:128]),
                       [bkey + "w"], [bkey, "lt"])
                else:
                    VE(lambda e, fb=fb, bank=bank, wi=wi: e.tensor_copy(
                        s_lt[:, fb * SBT + wi * 128: fb * SBT + (wi + 1) * 128],
                        bank[:, 0:128]),
                       [bkey + "w"], [bkey, "lt"])
        for oh in range(2):
            for fb in range(8):
                par2 = fb % 4
                wsl = s_wrhs[:, par2 * 512:par2 * 512 + 512]
                if not (oh == 0 and fb < 4):
                    dma("sync" if par2 < 2 else "gpsimd", wsl,
                        d_wtf[fb * 128:(fb + 1) * 128, oh * 512:(oh + 1) * 512],
                        [f"ws{par2}", "dwf"], [f"ws{par2}"], tag="wfstream")
                for wi in range(4):
                    bank, bkey = OUT_BANKS[wi]
                    TE(lambda e, fb=fb, wi=wi, wsl=wsl, bank=bank: e.matmul(
                        bank[:],
                        s_lt[:, fb * SBT + wi * 128:fb * SBT + (wi + 1) * 128],
                        wsl, start=(fb == 0), stop=(fb == 7)),
                       [f"ws{par2}", "lt", bkey] if fb == 0
                       else [f"ws{par2}", "lt"],
                       [bkey + "w"] if fb == 7 else [])
            for wi in range(4):
                stag, skey = STAGING[wi]
                bank, bkey = OUT_BANKS[wi]
                VE(lambda e, oh=oh, stag=stag, bank=bank: e.tensor_tensor(
                    stag[:, oh * 512:(oh + 1) * 512], bank[:],
                    s_bfrep[:, oh * 512:(oh + 1) * 512], ALU.add),
                   [bkey + "w", "bfrep"], [bkey, skey])
        for wi in range(4):
            out_ln_residual(wi)
        dstq = d_out.rearrange("(n p) d -> p n d", p=128)[:, sb * 4:(sb + 1) * 4, :]
        dma("sync", dstq, s_x[:].rearrange("p (n d) -> p n d", d=D), ["x"],
            ["xout"], tag="xstore")

    emit_preamble()
    for sb in range(NSB):
        if sb > 0:
            proj_superblock(sb)
        if STAGE == "proj":
            for key, tile, dd in [("kb", s_kb, dbg["kb"]), ("qb", s_qb, dbg["qb"]),
                                  ("vb", s_vb, dbg["vb"]), ("bb", s_bb, dbg["bb"])]:
                dv = dd.rearrange("p (h t) -> p h t", h=NP)[
                    :, :, sb * SBT:(sb + 1) * SBT]
                for hp in range(NP):
                    VE(lambda e, tile=tile, hp=hp: e.tensor_copy(
                        s_q1[:, 0:512], tile[:, hp * SBT:(hp + 1) * SBT]),
                       [key], ["wld"])
                    dma("sync", dv[:, hp, :], s_q1[:, 0:512], ["wld"], ["dbg"])
            continue
        for wi in range(WPS):
            if sb * 4 + wi >= NWINLIM:
                continue
            if wi == 0:
                wrep_load(sb * 4, 0)
            pieces = []
            if wi + 1 < WPS and sb * 4 + wi + 1 < NWINLIM:
                norms_window(sb, wi + 1)
                pieces = scores_pieces(sb, wi + 1)
            ebuild_window(sb, wi)
            chunk_builds(sb, wi, 0)
            if STAGE == "scan" and sb == 0 and wi == 0:
                dma("sync", dbg["e1"], s_e1[:], ["e1"], ["dbg"])
                dma("sync", dbg["pw1"], s_ao[:], ["ao2"], ["dbg"])
                dma("sync", dbg["pw2"], s_pw2[:, 0:64], ["pw2"], ["dbg"])
                dma("gpsimd", dbg["kc"], win_slice(s_kb, wi)[:, :, 0:16],
                    ["kb"], ["dbg"])
            for k in range(NCH):
                if k >= NCHLIM:
                    continue
                if k + 1 < NCH:
                    wrep_load(sb * 4 + wi, k + 1)
                elif wi + 1 < WPS and sb * 4 + wi + 1 < NWINLIM:
                    wrep_load(sb * 4 + wi + 1, 0)
                chunk_steps(sb, wi, k,
                            piece=pieces[k] if k < len(pieces) else None)
                if STAGE == "scan" and sb == 0 and wi == 0 and k == 0:
                    dma("sync", dbg["u0"], s_u[:], ["u"], ["dbg"])
                    dma("gpsimd", dbg["v0"], win_slice(s_vb, wi)[:, :, 0:16],
                        ["vb"], ["dbg"])
                    dma("gpsimd", dbg["b0"], win_slice(s_bb, wi)[:, :, 0:16],
                        ["bb"], ["dbg"])
                    dma("sync", dbg["wr0"], s_wrep[:, 0:2048], ["wrep0"], ["dbg"])
                    dma("sync", dbg["qk0"], s_qkb[0:16, 0:2048], ["qkb0"], ["dbg"])
            if STAGE == "scan":
                w = sb * 4 + wi
                dv = dbg["ys"].rearrange("p (h t) -> p h t", h=NP)[
                    :, :, w * WIN:(w + 1) * WIN]
                dma("sync", dv, s_ys[:].rearrange("p (h t) -> p h t", h=NP),
                    ["ys"], ["dbg"])
                continue
            window_post(sb, wi)
        if STAGE == "scan":
            continue
        attn_out_batch(sb)
        if STAGE == "attn":
            dv = dbg["xm"].rearrange("(n p) d -> p n d", p=128)[
                :, sb * 4:(sb + 1) * 4, :]
            dma("sync", dv, s_x[:].rearrange("p (n d) -> p n d", d=D), ["x"],
                ["dbg"])
            continue
        ffn_superblock(sb)
    if STAGE == "proj":
        dma("sync", dbg["pp"], s_pp[:, 0:T + 1], ["pp"], ["dbg"])

    with nc.Block() as block:
        sems = {}
        for e in ENGINES:
            t = nc.semaphore(f"sem_{e}")
            ctxs.append(t)
            sems[e] = t.__enter__()
        for semk in P.dq_counts:
            t = nc.semaphore(semk)
            ctxs.append(t)
            sems[semk] = t.__enter__()
        final_counts = dict(P.counts)
        final_dq = dict(P.dq_counts)

        waited = {}
        for st_ in P.steps:
            for de, dv_ in st_["waits"].items():
                waited.setdefault(de, set()).add(dv_)
        for oe in ENGINES:
            waited.setdefault(oe, set()).add(P.counts[oe])
        cnt = {e: 0 for e in ENGINES}
        keep = []
        kept_cum = {e: 0 for e in ENGINES}
        remap = {e: {0: 0} for e in ENGINES}
        for st_ in P.steps:
            semk = st_["sem"]
            if semk in ENGINES:
                prev = cnt[semk]
                cur = prev + 1
                cnt[semk] = cur
                need = any(prev < w <= cur for w in waited.get(semk, ()))
                keep.append(need)
                if need:
                    kept_cum[semk] += 1
                remap[semk][cur] = kept_cum[semk]
            else:
                keep.append(True)

        def mapv(semk, v):
            if semk in ENGINES:
                return remap[semk][v]
            return v

        def make_stream(ename):
            def f(eng):
                last_wait = {}
                idx = 0
                for sti, st_ in enumerate(P.steps):
                    if st_["engine"] != ename:
                        continue
                    for de, dv_ in sorted(st_["waits"].items()):
                        v = mapv(de, dv_)
                        if v <= 0:
                            continue
                        if last_wait.get(de, -1) >= v:
                            continue
                        eng.wait_ge(sems[de], v)
                        last_wait[de] = v
                    inst = st_["fn"](eng)
                    if inst is None:
                        inst = eng.nop()
                    if keep[sti]:
                        inst.then_inc(sems[st_["sem"]], st_["inc"])
                if ename == "sync":
                    for oe in ENGINES:
                        if oe != "sync":
                            v = mapv(oe, final_counts[oe])
                            if v > 0 and last_wait.get(oe, -1) < v:
                                eng.wait_ge(sems[oe], v)
                    for semk, v in final_dq.items():
                        if last_wait.get(semk, -1) < v:
                            eng.wait_ge(sems[semk], v)
            return f

        block.sync(make_stream("sync"))
        block.tensor(make_stream("tensor"))
        block.vector(make_stream("vector"))
        block.scalar(make_stream("scalar"))
        block.gpsimd(make_stream("gpsimd"))

    for t in reversed(ctxs):
        t.__exit__(None, None, None)
    return nc


_NC_CACHE = {}


def kernel(L0_attn_x, L0_attn_Wq, L0_attn_Wk, L0_attn_Wv, L0_attn_Wg, L0_attn_Wb,
           L0_attn_Wa, L0_attn_dt_bias, L0_attn_A_log, L0_attn_norm_w, L0_attn_Wo,
           L0_attn_bo, L0_ffn_weight, L0_ffn_bias):
    if "prog" not in _NC_CACHE:
        _NC_CACHE["prog"] = build_program()
    nc = _NC_CACHE["prog"]
    consts = build_consts()
    x = np.ascontiguousarray(np.asarray(L0_attn_x, dtype=np.float32))
    base = dict(
        Wq=np.ascontiguousarray(np.asarray(L0_attn_Wq, np.float32)),
        Wk=np.ascontiguousarray(np.asarray(L0_attn_Wk, np.float32)),
        Wv=np.ascontiguousarray(np.asarray(L0_attn_Wv, np.float32)),
        Wg=np.ascontiguousarray(np.asarray(L0_attn_Wg, np.float32)),
        Wb=np.ascontiguousarray(np.asarray(L0_attn_Wb, np.float32)),
        Wa=np.ascontiguousarray(np.asarray(L0_attn_Wa, np.float32)),
        dt_bias=np.ascontiguousarray(np.asarray(L0_attn_dt_bias, np.float32)
                                     .reshape(H, 1)),
        A_log=np.ascontiguousarray(np.asarray(L0_attn_A_log, np.float32)
                                   .reshape(H, 1)),
        norm_w=np.ascontiguousarray(np.asarray(L0_attn_norm_w, np.float32)),
        Wo=np.ascontiguousarray(np.asarray(L0_attn_Wo, np.float32)),
        bo=np.ascontiguousarray(np.asarray(L0_attn_bo, np.float32)),
        Wf=np.ascontiguousarray(np.asarray(L0_ffn_weight, np.float32)),
        bfv=np.ascontiguousarray(np.asarray(L0_ffn_bias, np.float32)),
        ident_f32=consts["ident_f32"],
        ident_bf16=consts["ident_bf16"],
        negtril=consts["negtril"],
        triincl=consts["triincl"],
        esel2=consts["esel2"],
        esel2t=consts["esel2t"],
        mska=consts["mska"],
        mskb=consts["mskb"],
    )
    in_maps = []
    for b in range(NCORES):
        m = dict(base)
        m["x"] = np.ascontiguousarray(x[b])
        in_maps.append(m)
    res = run_bass_kernel_spmd(nc, in_maps, core_ids=list(range(NCORES)))
    out = np.stack([np.asarray(res.results[b]["out"]) for b in range(NCORES)],
                   axis=0)
    return out.astype(np.float32)


# version 15 (final)
# speedup vs baseline: 1.0028x; 1.0028x over previous
import os
import numpy as np
import ml_dtypes

import concourse.bass as bass
import concourse.mybir as mybir
from concourse.bass_utils import run_bass_kernel_spmd

F32 = mybir.dt.float32
BF16 = mybir.dt.bfloat16
AF = mybir.ActivationFunctionType
ALU = mybir.AluOpType
AX = mybir.AxisListType

NCORES = 8
T = 2048
D = 1024
H = 16
DH = 64
NP = 8
C = 16
WIN = 128
NWIN = T // WIN
NCH = WIN // C
SBT = 512
NSB = T // SBT
WPS = SBT // WIN
EPS_LN = 1e-5
EPS_RMS = 1e-6
FP6_MAX = 7.5

STAGE = os.environ.get("BASS_STAGE", "full")
NWINLIM = int(os.environ.get("BASS_NWINLIM", "999"))
NCHLIM = int(os.environ.get("BASS_NCHLIM", "999"))

ENGINES = ("sync", "tensor", "vector", "scalar", "gpsimd")


DMA_SLOTS = {"sync": 10, "gpsimd": 6}


class Plan:

    def __init__(self):
        self.steps = []
        self.counts = {e: 0 for e in ENGINES}
        self.dq_counts = {}
        self.dma_idx = {q: 0 for q in DMA_SLOTS}
        self.slot_prev = {}
        self.last_write = {}
        self.readers = {}

    def add(self, engine, fn, reads=(), writes=(), dma=False, tag=""):
        deps = {}

        def upd(ev):
            e, v = ev
            if v > deps.get(e, -1):
                deps[e] = v

        for r in reads:
            lw = self.last_write.get(r)
            if lw:
                upd(lw)
        for r in writes:
            lw = self.last_write.get(r)
            if lw:
                upd(lw)
            for rd in self.readers.get(r, ()):
                upd(rd)
        if dma:
            q = engine
            slot = self.dma_idx[q] % DMA_SLOTS[q]
            self.dma_idx[q] += 1
            semk = f"dq_{q}_{slot}"
            prevres = self.slot_prev.get((q, slot))
            if prevres is not None:
                upd((semk, self.dq_counts.get(semk, 0)))
                for r in prevres:
                    for rd in self.readers.get(r, ()):
                        upd(rd)
            tgt = self.dq_counts.get(semk, 0) + 16
            self.dq_counts[semk] = tgt
            self.steps.append(dict(engine=engine, fn=fn, waits=dict(deps),
                                   sem=semk, inc=16, tag=tag))
            sig = (semk, tgt)
            self.slot_prev[(q, slot)] = list(writes) + list(reads)
        else:
            after = self.counts[engine] + 1
            self.counts[engine] = after
            self.steps.append(dict(engine=engine, fn=fn, waits=dict(deps),
                                   sem=engine, inc=1, tag=tag))
            sig = (engine, after)
        for r in writes:
            self.last_write[r] = sig
            self.readers[r] = []
        for r in reads:
            self.readers.setdefault(r, []).append(sig)


def build_consts():
    c = {}
    c["ident_f32"] = np.eye(128, dtype=np.float32)
    c["ident_bf16"] = np.eye(128, dtype=ml_dtypes.bfloat16)
    j = np.arange(C)
    negtril = -(j[:, None] > j[None, :]).astype(np.float32)
    c["negtril"] = np.ascontiguousarray(
        np.broadcast_to(negtril[None, :, None, :], (NCH, C, H, C)).reshape(128, H * C))
    triincl = (j[:, None] <= j[None, :]).astype(np.float32)
    c["triincl"] = np.ascontiguousarray(
        np.broadcast_to(triincl[None, :, None, :], (NCH, C, H, C)).reshape(128, H * C))
    e = np.zeros((128, 2), dtype=np.float32)
    e[0:64, 0] = 1.0
    e[64:128, 1] = 1.0
    c["esel2"] = e
    e2t = np.zeros((2, 128), dtype=np.float32)
    e2t[0, 0:64] = 1.0
    e2t[1, 64:128] = 1.0
    c["esel2t"] = e2t
    c["mska"] = np.full((128, 1), ~((1 << 19) - 1), np.int32)
    c["mskb"] = np.full((128, 1), ~((1 << 20) - 1), np.int32)
    return c


def build_program():
    nc = bass.Bass("TRN2", target_bir_lowering=False)
    P = Plan()

    def inp(name, shape, dtype=F32):
        return nc.dram_tensor(name, list(shape), dtype, kind="ExternalInput").ap()

    d_x = inp("x", (T, D))
    d_w = {m: inp("W" + m, (D, D)) for m in "qkvgb"}
    d_wa = inp("Wa", (H, D))
    d_dt = inp("dt_bias", (H, 1))
    d_alog = inp("A_log", (H, 1))
    d_nw = inp("norm_w", (DH,))
    d_wo = inp("Wo", (D, D))
    d_bo = inp("bo", (D,))
    d_wf = inp("Wf", (D, D))
    d_bf = inp("bfv", (D,))
    d_idf = inp("ident_f32", (128, 128))
    d_idb = inp("ident_bf16", (128, 128), BF16)
    d_ntril = inp("negtril", (128, H * C))
    d_ttri = inp("triincl", (128, H * C))
    d_esel = inp("esel2", (128, 2))
    d_e2t = inp("esel2t", (2, 128))
    d_mska = inp("mska", (128, 1), mybir.dt.int32)
    d_mskb = inp("mskb", (128, 1), mybir.dt.int32)

    d_out = nc.dram_tensor("out", [T, D], F32, kind="ExternalOutput").ap()
    dbg = {}
    if STAGE != "full":
        for name, shape in [("kb", (128, T * NP)), ("qb", (128, T * NP)),
                            ("vb", (128, T * NP)), ("bb", (128, T * NP)),
                            ("pp", (H, T + 1)), ("ys", (128, T * NP)),
                            ("xm", (T, D)), ("e1", (128, 1024)),
                            ("kc", (128, 128)), ("pw1", (128, 1024)),
                            ("pw2", (128, 64)), ("u0", (128, 128)),
                            ("wr0", (128, 2048)), ("qk0", (16, 2048)),
                            ("v0", (128, 128)), ("b0", (128, 128))]:
            dbg[name] = nc.dram_tensor("dbg_" + name, list(shape), F32,
                                       kind="ExternalOutput").ap()

    def scratch(name, shape, dtype=F32):
        return nc.dram_tensor(name, list(shape), dtype, kind="Internal").ap()

    d_wt = {m: scratch("wt" + m, (D, D), BF16) for m in "qkvgb"}
    d_wto = scratch("wto", (D, D), BF16)
    d_wtf = scratch("wtf", (D, D), BF16)
    d_wat = scratch("wat", (D, H), BF16)
    d_pd = scratch("pd", (H, T + 32))
    d_pcs = scratch("pcs", (H, 160))
    d_rk = scratch("rk", (H, T))
    d_rq = scratch("rq", (H, T))
    d_rr = scratch("rr", (H, T))
    d_wd = scratch("wd", (2, NCH, H, C, C))
    d_qk = scratch("qk", (2, NCH, H, C, C))

    ctxs = []

    def sbuf(name, shape, dtype=F32):
        t = nc.sbuf_tensor(name, list(shape), dtype)
        ctxs.append(t)
        return t.__enter__()

    def psum(name, shape, dtype=F32):
        t = nc.psum_tensor(name, list(shape), dtype)
        ctxs.append(t)
        return t.__enter__()

    s_idf = sbuf("idf", (128, 128))
    s_idb = sbuf("idb", (128, 128), BF16)
    s_ntril = sbuf("ntril", (128, H * C))
    s_ttri = sbuf("ttri", (128, H * C))
    s_esel = sbuf("esel", (128, 2))
    s_e2t = sbuf("e2t", (2, 128))
    s_mka = sbuf("mka", (128, 1), mybir.dt.int32)
    s_mkb = sbuf("mkb", (128, 1), mybir.dt.int32)
    s_dt = sbuf("dt", (H, 1))
    s_ealog = sbuf("ealog", (H, 1))
    s_nwcol = sbuf("nwcol", (128, 1))
    s_borep = sbuf("borep", (128, D))
    s_bfrep = sbuf("bfrep", (128, D))
    s_ones = sbuf("ones", (H, SBT))

    s_wld = sbuf("wld", (128, D))
    s_wbf = sbuf("wbf", (128, D), BF16)
    s_wtt = sbuf("wtt", (128, D), BF16)

    s_x = sbuf("sx", (128, 4 * D))
    s_tb = sbuf("tb", (128, D), BF16)
    s_lt = sbuf("lt", (128, 8 * SBT), BF16)
    s_kb = sbuf("kb", (128, NP * SBT), BF16)
    s_qb = sbuf("qb", (128, NP * SBT), BF16)
    s_vb = sbuf("vb", (128, NP * SBT), BF16)
    s_bb = sbuf("bb", (128, NP * SBT), BF16)
    s_gb = sbuf("gb", (128, NP * SBT), BF16)
    s_wrhs = sbuf("wrhs", (128, 2 * 1024), BF16)

    s_pp = sbuf("pp", (H, T + 32))
    s_ppt = sbuf("ppt", (128, NWIN * H))
    s_at = sbuf("at", (H, SBT))
    s_at2 = sbuf("at2", (H, SBT))
    s_pcs = sbuf("spcs", (H, 40))

    s_sc = sbuf("sc", (128, H * WIN))
    s_wdst = sbuf("wdst", (128, H * C))
    s_mt = sbuf("mt", (128, H * C))
    s_prow = sbuf("prow", (128, H * C))
    s_qkb = sbuf("qkb", (16, 2 * NCH * H * C))
    s_wrep = sbuf("wrep", (128, 2 * C * C * NP))
    s_e1 = sbuf("e1", (128, NP * WIN))
    s_e2 = sbuf("e2", (128, NP * WIN))
    s_pw1 = sbuf("pw1", (128, NP * WIN))
    s_pw2 = sbuf("pw2", (128, NP * WIN))
    s_scr = sbuf("scr", (128, NP * WIN))
    s_rr2 = sbuf("rr2", (2, NP * WIN))
    s_rrep = sbuf("rrep", (128, NP * WIN))

    s_u = sbuf("u", (128, NP * C))
    s_ptm = sbuf("ptm", (128, NP * C))
    s_kc = sbuf("kc", (128, NP * C))
    s_qc = sbuf("qc", (128, NP * C))
    s_k2 = sbuf("k2", (128, NP * C))
    s_ut = sbuf("ut", (16, NP * 128))
    s_ktc = sbuf("ktc", (16, NP * 128))
    s_s = sbuf("s", (128, NP * DH))
    s_ys = sbuf("ys", (128, NP * WIN))
    s_qtmp = s_ys
    s_ctx = sbuf("ctx", (128, NP * WIN), BF16)

    s_ao = sbuf("ao", (128, D))
    s_q2 = sbuf("q2", (128, D))
    s_q3 = sbuf("q3", (128, 32))
    s_q4 = sbuf("q4", (128, 32))
    s_qs = s_scr
    s_q1x = s_pw1
    s_mfac = s_pw2
    s_q1 = s_wld
    s_msk = sbuf("msk", (128, D), mybir.dt.uint8)
    s_xd = sbuf("xd", (128, D), BF16)
    s_st = sbuf("st", (128, 8))

    p_pr = psum("ppr", (128, 512))
    p_sc = psum("psc", (128, 512))
    p_tp = psum("ptp", (128, 1024), BF16)
    p_tr = psum("ptr", (16, 512))
    p_kv = psum("pkv", (128, 512))
    p_ys = psum("pys", (128, 512))
    p_su = psum("psu", (128, 512))
    p_sc2 = psum("psc2", (128, 512))

    def dma(q, out, in_, rd, wr, tag=""):
        P.add(q, lambda e, o=out, i=in_: e.dma_start(out=o, in_=i),
              reads=list(rd), writes=list(wr), dma=True, tag=tag)

    def VE(fn, rd, wr, tag=""):
        P.add("vector", fn, reads=list(rd), writes=list(wr), tag=tag)

    def SE(fn, rd, wr, tag=""):
        P.add("scalar", fn, reads=list(rd), writes=list(wr), tag=tag)

    def TE(fn, rd, wr, tag=""):
        P.add("tensor", fn, reads=list(rd), writes=list(wr), tag=tag)

    def GE(fn, rd, wr, tag=""):
        P.add("gpsimd", fn, reads=list(rd), writes=list(wr), tag=tag)

    dma("sync", s_idf[:], d_idf, [], ["idf"])
    dma("sync", s_idb[:], d_idb, [], ["idb"])
    dma("sync", s_ntril[:], d_ntril, [], ["ntril"])
    dma("sync", s_ttri[:], d_ttri, [], ["ttri"])
    dma("sync", s_esel[:], d_esel, [], ["esel"])
    dma("sync", s_e2t[:], d_e2t, [], ["e2t"])
    dma("sync", s_mka[:], d_mska, [], ["mka"])
    dma("sync", s_mkb[:], d_mskb, [], ["mkb"])
    dma("sync", s_dt[:], d_dt, [], ["dt"])
    dma("sync", s_ealog[:], d_alog, [], ["ealog"])
    dma("sync", s_nwcol[:], d_nw.unsqueeze(0).unsqueeze(2).broadcast_to((2, 64, 1)),
        [], ["nwcol"])
    dma("sync", s_borep[:], d_bo.unsqueeze(0).broadcast_to((128, D)), [], ["borep"])
    dma("sync", s_bfrep[:], d_bf.unsqueeze(0).broadcast_to((128, D)), [], ["bfrep"])
    GE(lambda e: e.memset(s_ones[:], 1.0), [], ["ones"])
    GE(lambda e: e.memset(s_pp[:], 0.0), [], ["pp"])
    GE(lambda e: e.memset(s_s[:], 0.0), [], ["S"])
    GE(lambda e: e.memset(s_ys[:], 0.0), [], ["ys"])
    SE(lambda e: e.activation(out=s_ealog[:], in_=s_ealog[:], func=AF.Exp),
       ["ealog"], ["ealog"])
    VE(lambda e: e.tensor_scalar_mul(s_ealog[:], s_ealog[:], -1.0),
       ["ealog"], ["ealog"])

    def emit_quantize(src, dst, key, wkey):
        v3 = src.rearrange("p (a b) -> p a b", b=32)
        VE(lambda e: e.tensor_reduce(out=s_q3[:], in_=v3, axis=AX.X, op=ALU.max,
                                     apply_absolute_value=True), [key], ["q3"])
        VE(lambda e: e.tensor_scalar(out=s_q3[:], in0=s_q3[:], scalar1=1.0 / FP6_MAX,
                                     scalar2=1e-30, op0=ALU.mult, op1=ALU.max),
           ["q3"], ["q3"])
        VE(lambda e: e.reciprocal(s_q4[:], s_q3[:]), ["q3"], ["q4"])
        rb = s_q4[:].unsqueeze(2).broadcast_to((128, 32, 32))
        VE(lambda e: e.tensor_tensor(s_q2[:].rearrange("p (a b) -> p a b", b=32),
                                     v3, rb, ALU.mult), [key, "q4"], ["q2"])
        SE(lambda e: e.activation(out=s_qs[:], in_=s_q2[:], func=AF.Sign),
           ["q2"], ["scr"])
        SE(lambda e: e.activation(out=s_ao[:], in_=s_q2[:], func=AF.Abs),
           ["q2"], ["ao2"])
        VE(lambda e: e.tensor_scalar(out=s_msk[:], in0=s_ao[:], scalar1=4.0,
                                     scalar2=None, op0=ALU.is_ge),
           ["ao2"], ["msk"])
        VE(lambda e: e.scalar_tensor_tensor(out=s_mfac[:], in0=s_qs[:],
                                            scalar=0.125, in1=s_q2[:],
                                            op0=ALU.mult, op1=ALU.add),
           ["scr", "q2"], ["pw2"])
        VE(lambda e: e.scalar_tensor_tensor(out=s_qtmp[:], in0=s_qs[:],
                                            scalar=0.25, in1=s_q2[:],
                                            op0=ALU.mult, op1=ALU.add),
           ["scr", "q2", "ys"], ["qtmp"])
        VE(lambda e: e.select(out=s_mfac[:], mask=s_msk[:], on_true=s_qtmp[:],
                              on_false=s_mfac[:], add_drain=True),
           ["msk", "pw2", "qtmp"], ["pw2"])
        VE(lambda e: e.tensor_scalar(out=s_qtmp[:].bitcast(mybir.dt.int32),
                                     in0=s_mfac[:].bitcast(mybir.dt.int32),
                                     scalar1=s_mkb[:], scalar2=None,
                                     op0=ALU.bitwise_and),
           ["pw2", "mkb", "ys"], ["qtmp"])
        VE(lambda e: e.scalar_tensor_tensor(out=s_q1x[:], in0=s_qs[:],
                                            scalar=2.0625, in1=s_q2[:],
                                            op0=ALU.mult, op1=ALU.add),
           ["scr", "q2"], ["pw1"])
        VE(lambda e: e.tensor_scalar(out=s_mfac[:].bitcast(mybir.dt.int32),
                                     in0=s_q1x[:].bitcast(mybir.dt.int32),
                                     scalar1=s_mka[:], scalar2=None,
                                     op0=ALU.bitwise_and), ["pw1", "mka"], ["pw2"])
        VE(lambda e: e.scalar_tensor_tensor(out=s_q1x[:], in0=s_qs[:],
                                            scalar=-2.0, in1=s_mfac[:],
                                            op0=ALU.mult, op1=ALU.add),
           ["scr", "pw2"], ["pw1"])
        VE(lambda e: e.tensor_scalar(out=s_msk[:], in0=s_ao[:], scalar1=2.0,
                                     scalar2=None, op0=ALU.is_ge),
           ["ao2"], ["msk"])
        VE(lambda e: e.select(out=s_q1x[:], mask=s_msk[:], on_true=s_qtmp[:],
                              on_false=s_q1x[:], add_drain=True),
           ["msk", "pw1", "qtmp"], ["pw1"])
        sb3 = s_q3[:].unsqueeze(2).broadcast_to((128, 32, 32))
        wkeys = wkey if isinstance(wkey, list) else [wkey]
        VE(lambda e, dst=dst: e.tensor_tensor(dst.rearrange("p (a b) -> p a b", b=32),
                                              s_q1x[:].rearrange("p (a b) -> p a b", b=32),
                                              sb3, ALU.mult), ["pw1", "q3"], wkeys)

    def prep_weight(d_src, d_dst, quantize=False, tagn=""):
        for ob in range(8):
            if quantize:
                dma("sync", s_wld[:], d_src[ob * 128:(ob + 1) * 128, :],
                    [], ["wld"], tag="wload")
                emit_quantize(s_wld[:], s_wbf[:], "wld", ["wbfA", "wbfB"])
            else:
                for hb in range(2):
                    dma("gpsimd", s_wbf[:, hb * 512:(hb + 1) * 512],
                        d_src[ob * 128:(ob + 1) * 128, hb * 512:(hb + 1) * 512],
                        [], ["wbf" + "AB"[hb]], tag="wload")
            for fb in range(8):
                if quantize:
                    bank, bkey = (p_kv, "pkv") if fb % 2 == 0 else (p_sc2, "psc2")
                else:
                    bank, bkey = [(p_pr, "ppr"), (p_kv, "pkv"), (p_ys, "pys"),
                                  (p_su, "psu")][fb % 4]
                hk = "wbf" + "AB"[fb // 4]
                wk = "wtt" + "AB"[fb // 4]
                TE(lambda e, fb=fb, bank=bank: e.matmul(
                    bank[:, 0:128],
                    s_wbf[:, fb * 128:(fb + 1) * 128], s_idb[:],
                    start=True, stop=True),
                   [hk, "idb", bkey], [bkey + "w"])
                if fb % 2 == 0:
                    SE(lambda e, fb=fb, bank=bank: e.copy(
                        s_wtt[:, fb * 128:(fb + 1) * 128], bank[:, 0:128]),
                       [bkey + "w"], [bkey, wk])
                else:
                    VE(lambda e, fb=fb, bank=bank: e.tensor_copy(
                        s_wtt[:, fb * 128:(fb + 1) * 128], bank[:, 0:128]),
                       [bkey + "w"], [bkey, wk])
            for hb in range(2):
                dst = d_dst.rearrange("(fb f) o -> f fb o", f=128)[
                    :, hb * 4:(hb + 1) * 4, ob * 128:(ob + 1) * 128]
                src = s_wtt[:, hb * 512:(hb + 1) * 512].rearrange(
                    "f (fb o) -> f fb o", o=128)
                dma("sync", dst, src, ["wtt" + "AB"[hb]], ["dw" + tagn],
                    tag="wstore")

    def prep_weight_alt_iter(d_src, d_dst, ob):
        for hb in range(2):
            dma("gpsimd", s_xd[:, hb * 512:(hb + 1) * 512],
                d_src[ob * 128:(ob + 1) * 128, hb * 512:(hb + 1) * 512],
                [], ["xd" + "AB"[hb]], tag="wload")
        for fb in range(8):
            slot = fb % 4
            hk = "xd" + "AB"[fb // 4]
            wk = "tb" + "AB"[fb // 4]
            TE(lambda e, fb=fb, slot=slot: e.transpose(
                p_tp[:, slot * 128:(slot + 1) * 128],
                s_xd[:, fb * 128:(fb + 1) * 128], s_idb[:]),
               [hk, "idb", "ptp"], ["ptpw"])
            SE(lambda e, fb=fb, slot=slot: e.copy(
                s_tb[:, fb * 128:(fb + 1) * 128],
                p_tp[:, slot * 128:(slot + 1) * 128]),
               ["ptpw"], ["ptp", wk])
        for hb in range(2):
            dst = d_dst.rearrange("(fb f) o -> f fb o", f=128)[
                :, hb * 4:(hb + 1) * 4, ob * 128:(ob + 1) * 128]
            srcs = s_tb[:, hb * 512:(hb + 1) * 512].rearrange(
                "f (fb o) -> f fb o", o=128)
            dma("sync", dst, srcs, ["tb" + "AB"[hb]], ["dwo"], tag="wstore")

    def prep_weight_quant_iter(d_src, d_dst, ob):
        dma("sync", s_wld[:], d_src[ob * 128:(ob + 1) * 128, :],
            [], ["wld"], tag="wload")
        emit_quantize(s_wld[:], s_wbf[:], "wld", ["wbfA", "wbfB"])
        for fb in range(8):
            bank, bkey = (p_kv, "pkv") if fb % 2 == 0 else (p_sc2, "psc2")
            hk = "wbf" + "AB"[fb // 4]
            wk = "wtt" + "AB"[fb // 4]
            TE(lambda e, fb=fb, bank=bank: e.matmul(
                bank[:, 0:128],
                s_wbf[:, fb * 128:(fb + 1) * 128], s_idb[:],
                start=True, stop=True),
               [hk, "idb", bkey], [bkey + "w"])
            if fb % 2 == 0:
                SE(lambda e, fb=fb, bank=bank: e.copy(
                    s_wtt[:, fb * 128:(fb + 1) * 128], bank[:, 0:128]),
                   [bkey + "w"], [bkey, wk])
            else:
                VE(lambda e, fb=fb, bank=bank: e.tensor_copy(
                    s_wtt[:, fb * 128:(fb + 1) * 128], bank[:, 0:128]),
                   [bkey + "w"], [bkey, wk])
        for hb in range(2):
            dst = d_dst.rearrange("(fb f) o -> f fb o", f=128)[
                :, hb * 4:(hb + 1) * 4, ob * 128:(ob + 1) * 128]
            srcs = s_wtt[:, hb * 512:(hb + 1) * 512].rearrange(
                "f (fb o) -> f fb o", o=128)
            dma("sync", dst, srcs, ["wtt" + "AB"[hb]], ["dwf"], tag="wstore")

    def emit_preamble():
        proj_x_part(0)
        for m in "qkvgb":
            prep_weight(d_w[m], d_wt[m], tagn=m)
        for ob in range(8):
            prep_weight_alt_iter(d_wo, d_wto, ob)
            prep_weight_quant_iter(d_wf, d_wtf, ob)
        dma("sync", s_wld[0:16, :], d_wa, [], ["wld"])
        VE(lambda e: e.tensor_copy(s_wbf[0:16, :], s_wld[0:16, :]), ["wld"],
           ["wbfA", "wbfB"])
        for fb in range(8):
            slot = fb % 4
            TE(lambda e, fb=fb, slot=slot: e.transpose(
                p_tp[:, slot * 128:slot * 128 + 16],
                s_wbf[0:16, fb * 128:(fb + 1) * 128], s_idb[0:16, 0:16]),
               ["wbfA", "wbfB", "idb", "ptp"], ["ptpw"])
            SE(lambda e, fb=fb, slot=slot: e.copy(
                s_wtt[:, fb * 16:(fb + 1) * 16], p_tp[:, slot * 128:slot * 128 + 16]),
               ["ptpw"], ["ptp", "wttA"])
        dma("sync", d_wat.rearrange("(fb f) h -> f fb h", f=128),
            s_wtt[:, 0:128].rearrange("f (fb h) -> f fb h", h=16), ["wttA"],
            ["wttA", "dwa"])
        proj_mm_part(0)

    def ln_stats(xap, key):
        SE(lambda e: e.activation(out=s_q2[:], in_=xap, func=AF.Square,
                                  accum_out=s_st[:, 1:2]), [key], ["q2", "st"])
        VE(lambda e: e.tensor_reduce(out=s_st[:, 0:1], in_=xap, axis=AX.X,
                                     op=ALU.add, negate=True), [key, "st"], ["st"])
        VE(lambda e: e.tensor_scalar_mul(s_st[:, 0:1], s_st[:, 0:1], 1.0 / D),
           ["st"], ["st"])
        VE(lambda e: e.tensor_tensor(s_st[:, 2:3], s_st[:, 0:1], s_st[:, 0:1],
                                     ALU.mult), ["st"], ["st"])
        VE(lambda e: e.tensor_scalar(out=s_st[:, 1:2], in0=s_st[:, 1:2],
                                     scalar1=1.0 / D, scalar2=None, op0=ALU.mult),
           ["st"], ["st"])
        VE(lambda e: e.tensor_tensor(s_st[:, 1:2], s_st[:, 1:2], s_st[:, 2:3],
                                     ALU.subtract), ["st"], ["st"])
        VE(lambda e: e.tensor_scalar(out=s_st[:, 1:2], in0=s_st[:, 1:2],
                                     scalar1=EPS_LN, scalar2=None, op0=ALU.add),
           ["st"], ["st"])
        SE(lambda e: e.activation(out=s_st[:, 1:2], in_=s_st[:, 1:2], func=AF.Sqrt),
           ["st"], ["st"])
        VE(lambda e: e.reciprocal(s_st[:, 1:2], s_st[:, 1:2]), ["st"], ["st"])

    def proj_x_part(sb):
        xsrc = d_x.rearrange("(n p) d -> p n d", p=128)[:, sb * 4:(sb + 1) * 4, :]
        dma("sync", s_x[:].rearrange("p (n d) -> p n d", d=D), xsrc,
            ["x"], ["x"], tag="xload")
        for wi in range(4):
            xt = s_x[:, wi * D:(wi + 1) * D]
            ln_stats(xt, "x")
            VE(lambda e, xt=xt: e.tensor_scalar(out=s_tb[:], in0=xt,
                                                scalar1=s_st[:, 0:1],
                                                scalar2=s_st[:, 1:2],
                                                op0=ALU.add, op1=ALU.mult),
               ["x", "st"], ["tb"])
            for fb in range(8):
                bank, bkey = (p_sc, "psc") if fb % 2 == 0 else (p_sc2, "psc2")
                TE(lambda e, fb=fb, bank=bank: e.matmul(
                    bank[:, 0:128],
                    s_tb[:, fb * 128:(fb + 1) * 128], s_idb[:],
                    start=True, stop=True),
                   ["tb", "idb", bkey], [bkey + "w"])
                if fb % 2 == 0:
                    SE(lambda e, fb=fb, bank=bank, wi=wi: e.copy(
                        s_lt[:, fb * SBT + wi * 128: fb * SBT + (wi + 1) * 128],
                        bank[:, 0:128]),
                       [bkey + "w"], [bkey, "lt"])
                else:
                    VE(lambda e, fb=fb, bank=bank, wi=wi: e.tensor_copy(
                        s_lt[:, fb * SBT + wi * 128: fb * SBT + (wi + 1) * 128],
                        bank[:, 0:128]),
                       [bkey + "w"], [bkey, "lt"])
    def proj_mm_part(sb, hook=None):
        src = d_wat.rearrange("(fb f) h -> f fb h", f=128)
        dma("sync", s_wrhs[:, 0:128].rearrange("f (fb h) -> f fb h", h=16), src,
            ["ws0", "dwa"], ["ws0"])
        for fb in range(8):
            TE(lambda e, fb=fb: e.matmul(
                p_pr[0:16, :], s_wrhs[:, fb * 16:(fb + 1) * 16],
                s_lt[:, fb * SBT:(fb + 1) * SBT],
                start=(fb == 0), stop=(fb == 7)),
               ["ws0", "lt", "ppr"] if fb == 0 else ["ws0", "lt"],
               ["pprw"] if fb == 7 else [])
        SE(lambda e: e.activation(out=s_at[:], in_=p_pr[0:16, :], func=AF.Identity,
                                  bias=s_dt[:]), ["pprw", "dt"], ["at", "ppr"])
        SE(lambda e: e.activation(out=s_at2[:], in_=s_at[:], func=AF.Abs),
           ["at"], ["at2"])
        SE(lambda e: e.activation(out=s_at2[:], in_=s_at2[:], func=AF.Exp,
                                  scale=-1.0), ["at2"], ["at2"])
        SE(lambda e: e.activation(out=s_at2[:], in_=s_at2[:], func=AF.Ln,
                                  bias=1.0), ["at2"], ["at2"])
        VE(lambda e: e.tensor_scalar_max(s_at[:], s_at[:], 0.0), ["at"], ["at"])
        VE(lambda e: e.tensor_tensor(s_at[:], s_at[:], s_at2[:], ALU.add),
           ["at", "at2"], ["at"])
        VE(lambda e: e.tensor_scalar_mul(s_at[:], s_at[:], s_ealog[:]),
           ["at", "ealog"], ["at"])
        VE(lambda e, sb=sb: e.tensor_tensor_scan(
            out=s_pp[:, 1 + sb * SBT: 1 + (sb + 1) * SBT], data0=s_ones[:],
            data1=s_at[:], initial=s_pp[:, sb * SBT:sb * SBT + 1],
            op0=ALU.mult, op1=ALU.add), ["at", "ones", "pp"], ["pp"])
        dma("sync", d_pd[:, sb * SBT: 1 + (sb + 1) * SBT],
            s_pp[:, sb * SBT: 1 + (sb + 1) * SBT], ["pp"], ["pd"])
        pcsv = s_pp[0:16, :].rearrange("h (b i) -> h b i", i=16)[
            :, sb * 32:sb * 32 + 33, 0]
        VE(lambda e, pcsv=pcsv, sb=sb: e.tensor_copy(s_pcs[0:16, 0:33], pcsv),
           ["pp"], ["spcs"])
        dma("sync", d_pcs[:, sb * 32:sb * 32 + 33], s_pcs[0:16, 0:33],
            ["spcs"], ["pcs"])
        for wi in range(4):
            w = sb * 4 + wi
            TE(lambda e, w=w: e.transpose(
                p_sc[:, 0:16], s_pp[0:16, 1 + w * WIN: 1 + (w + 1) * WIN],
                s_idf[0:16, 0:16]),
               ["pp", "idf", "psc"], ["pscw"])
            SE(lambda e, w=w: e.copy(s_ppt[:, w * H:(w + 1) * H], p_sc[:, 0:16]),
               ["pscw"], ["psc", "ppt"])
        projs = [("q", s_qb, AF.Identity), ("k", s_kb, AF.Identity),
                 ("v", s_vb, AF.Identity), ("g", s_gb, AF.Silu),
                 ("b", s_bb, AF.Sigmoid)]
        for m, dstt, act in projs:
            for oc in range(8):
                par = oc % 2
                wsl = s_wrhs[:, par * 1024:(par + 1) * 1024]
                src = d_wt[m].rearrange("(fb f) o -> f fb o", f=128)[
                    :, :, oc * 128:(oc + 1) * 128]
                skeys = [f"ws{par * 2}", f"ws{par * 2 + 1}"]
                dma("sync" if par == 0 else "gpsimd",
                    wsl.rearrange("f (fb o) -> f fb o", o=128), src,
                    skeys + ["dw" + m], skeys, tag="wstream")
                for fb in range(8):
                    sk = f"ws{par * 2 + fb // 4}"
                    TE(lambda e, fb=fb, wsl=wsl: e.matmul(
                        p_pr[:], wsl[:, fb * 128:(fb + 1) * 128],
                        s_lt[:, fb * SBT:(fb + 1) * SBT],
                        start=(fb == 0), stop=(fb == 7)),
                       [sk, "lt", "ppr"] if fb == 0 else [sk, "lt"],
                       ["pprw"] if fb == 7 else [])
                SE(lambda e, dstt=dstt, oc=oc, act=act: e.activation(
                    out=dstt[:, oc * SBT:(oc + 1) * SBT], in_=p_pr[:], func=act),
                   ["pprw"], ["ppr", m + "b"])
            if m == "k" and hook is not None:
                hook()

    def win_slice(tile, wi, p0=0, p1=128):
        return tile[p0:p1, :].rearrange("p (h t) -> p h t", h=NP)[
            :, :, wi * 128:(wi + 1) * 128]

    def norms_window(sb, wi):
        w = sb * 4 + wi
        for which, tile, drn, scale in [("k", s_kb, d_rk, 1.0),
                                        ("q", s_qb, d_rq, 64.0)]:
            ksl = win_slice(tile, wi)
            SE(lambda e, ksl=ksl: e.activation(
                out=s_scr[:].rearrange("p (h t) -> p h t", h=NP), in_=ksl,
                func=AF.Square), [which + "b"], ["scr"])
            for half in range(2):
                TE(lambda e, half=half: e.matmul(
                    p_tr[0:2, 0:512], s_esel[:],
                    s_scr[:, half * 512:(half + 1) * 512], start=True, stop=True),
                   ["scr", "esel", "ptr"], ["ptrw"])
                SE(lambda e, scale=scale, half=half: e.activation(
                    out=s_rr2[:, half * 512:(half + 1) * 512], in_=p_tr[0:2, 0:512],
                    func=AF.Sqrt, scale=scale),
                   ["ptrw"], ["ptr", "rr2"])
            VE(lambda e: e.reciprocal(s_rr2[:], s_rr2[:]), ["rr2"], ["rr2"])
            for half, bank, bkey in ((0, p_kv, "pkv"), (1, p_ys, "pys")):
                TE(lambda e, half=half, bank=bank: e.matmul(
                    bank[:, 0:512], s_e2t[0:2, :],
                    s_rr2[0:2, half * 512:(half + 1) * 512],
                    start=True, stop=True),
                   ["rr2", "e2t", bkey], [bkey + "w"])
                if half == 0:
                    SE(lambda e, half=half, bank=bank: e.copy(
                        s_rrep[:, half * 512:(half + 1) * 512], bank[:, 0:512]),
                       [bkey + "w"], [bkey, "rrep"])
                else:
                    VE(lambda e, half=half, bank=bank: e.tensor_copy(
                        s_rrep[:, half * 512:(half + 1) * 512], bank[:, 0:512]),
                       [bkey + "w"], [bkey, "rrep"])
            if which == "k":
                VE(lambda e, ksl=ksl: e.tensor_tensor(
                    s_pw1[:].rearrange("p (h t) -> p h t", h=NP), ksl,
                    s_rrep[:].rearrange("p (h t) -> p h t", h=NP), ALU.mult),
                   [which + "b", "rrep"], ["pw1"])
                VE(lambda e, ksl=ksl: e.tensor_copy(
                    ksl, s_pw1[:].rearrange("p (h t) -> p h t", h=NP)),
                   ["pw1"], [which + "b"])
            else:
                VE(lambda e, ksl=ksl: e.tensor_tensor(
                    ksl, ksl, s_rrep[:].rearrange("p (h t) -> p h t", h=NP),
                    ALU.mult),
                   [which + "b", "rrep"], [which + "b"])

    def _score_mms(wi, incl, h0, h1):
        for h in range(h0, h1):
            hp, h01 = h // 2, h % 2
            bank, bkey = (p_sc, "psc") if h % 2 == 0 else (p_sc2, "psc2")
            slot = (h // 2) % 4
            if incl:
                lhs = win_slice(s_kb, wi, h01 * 64, (h01 + 1) * 64)[:, hp]
                rhs = win_slice(s_qb, wi, h01 * 64, (h01 + 1) * 64)[:, hp]
                rkeys = ["kb", "qb", bkey]
            else:
                lhs = s_pw1[h01 * 64:(h01 + 1) * 64,
                            hp * 128:(hp + 1) * 128]
                rhs = lhs
                rkeys = ["pw1", bkey]
            TE(lambda e, lhs=lhs, rhs=rhs, slot=slot, bank=bank: e.matmul(
                bank[:, (slot) * 128:(slot + 1) * 128], lhs, rhs,
                start=True, stop=True),
               rkeys, [bkey + "w"])
            SE(lambda e, h=h, slot=slot, bank=bank: e.copy(
                s_sc[:, h * 128:(h + 1) * 128],
                bank[:, slot * 128:(slot + 1) * 128]),
               [bkey + "w"], [bkey, "sc"])

    def _score_extract(w, with_prow):
        for k in range(NCH):
            srcx = s_sc[16 * k:16 * (k + 1), :].rearrange(
                "p (h i) -> p h i", h=H)[:, :, 16 * k:16 * (k + 1)]
            dstx = s_wdst[16 * k:16 * (k + 1), :].rearrange(
                "p (h i) -> p h i", h=H)
            dma("gpsimd", dstx, srcx, ["sc"], ["wdst"], tag="diagx")
        if with_prow:
            for c_ in range(NCH):
                pr = d_pd[:, 1 + w * WIN + 16 * c_:1 + w * WIN + 16 * (c_ + 1)] \
                    .unsqueeze(0).broadcast_to((16, H, 16))
                dma("gpsimd", s_prow[16 * c_:16 * (c_ + 1), :], pr,
                    ["pd"], ["prow"], tag="prow")

    def _score_maskdump(w, incl):
        par = w % 2
        dumpd = d_qk if incl else d_wd
        dkey = ("dqk" if incl else "dwd") + str(par)
        pcol = s_ppt[:, w * H:(w + 1) * H].unsqueeze(2).broadcast_to((128, H, 16))
        if incl:
            GE(lambda e, pcol=pcol: e.tensor_tensor(
                s_mt[:].rearrange("p (h i) -> p h i", h=H),
                s_prow[:].rearrange("p (h i) -> p h i", h=H), pcol,
                ALU.subtract),
               ["prow", "ppt"], ["mt"])
        else:
            GE(lambda e, pcol=pcol: e.tensor_tensor(
                s_mt[:].rearrange("p (h i) -> p h i", h=H), pcol,
                s_prow[:].rearrange("p (h i) -> p h i", h=H),
                ALU.subtract),
               ["prow", "ppt"], ["mt"])
        GE(lambda e: e.tensor_scalar_min(s_mt[:], s_mt[:], 0.0), ["mt"], ["mt"])
        SE(lambda e: e.activation(out=s_mt[:], in_=s_mt[:], func=AF.Exp),
           ["mt"], ["mt"])
        GE(lambda e: e.tensor_tensor(s_wdst[:], s_wdst[:], s_mt[:], ALU.mult),
           ["mt", "wdst"], ["wdst"])
        mk = s_ttri if incl else s_ntril
        mkk = "ttri" if incl else "ntril"
        GE(lambda e, mk=mk: e.tensor_tensor(s_wdst[:], s_wdst[:], mk[:],
                                            ALU.mult),
           ["wdst", mkk], ["wdst"])
        for c_ in range(NCH):
            dstd = dumpd[par, c_].rearrange("h j i -> j h i")
            dma("gpsimd", dstd,
                s_wdst[16 * c_:16 * (c_ + 1), :].rearrange(
                    "p (h i) -> p h i", h=H),
                ["wdst"], [dkey], tag="wdump")

    def _score_qkload(w):
        par = w % 2
        srcq = d_qk[par].rearrange("c h i j -> i c h j")
        dma("sync", s_qkb[0:16, par * (NCH * H * C):(par + 1) * (NCH * H * C)]
            .rearrange("i (c h j) -> i c h j", c=NCH, h=H),
            srcq, [f"dqk{par}"], [f"qkb{par}"], tag="qkload")

    def scores_pieces(sb, wi):
        w = sb * 4 + wi
        return [
            lambda: _score_mms(wi, False, 0, 8),
            lambda: _score_mms(wi, False, 8, 16),
            lambda: _score_extract(w, True),
            lambda: _score_maskdump(w, False),
            lambda: _score_mms(wi, True, 0, 8),
            lambda: _score_mms(wi, True, 8, 16),
            lambda: (_score_extract(w, False), _score_maskdump(w, True)),
            lambda: _score_qkload(w),
        ]

    def scores_and_masks(sb, wi):
        for piece in scores_pieces(sb, wi):
            piece()
    def proj_superblock(sb):
        proj_x_part(sb)
        proj_mm_part(sb)

    def ebuild_window(sb, wi):
        w = sb * 4 + wi
        par = w % 2
        for h01 in range(2):
            pt = d_pd.rearrange("(hp h2) t -> h2 hp t", h2=2)[h01] \
                     .unsqueeze(0).broadcast_to((64, NP, T + 32))
            dma("sync", s_ao[h01 * 64:(h01 + 1) * 64, :],
                pt[:, :, 1 + w * WIN:1 + (w + 1) * WIN], ["pd"], ["ao2"])
        for h01 in range(2):
            pcv = d_pcs.rearrange("(hp h2) b -> h2 hp b", h2=2)[
                h01, :, w * NCH:(w + 1) * NCH]
            pcsb = pcv.unsqueeze(0).broadcast_to((64, NP, NCH))
            dma("sync", s_pw2[h01 * 64:(h01 + 1) * 64, 0:64], pcsb,
                ["pcs"], ["pw2"])
        pw2b = s_pw2[:, 0:64].rearrange("p (h c) -> p h c", h=NP) \
            .unsqueeze(3).broadcast_to((128, NP, NCH, 16))
        GE(lambda e, pw2b=pw2b: e.tensor_tensor(
            s_e1[:].rearrange("p (h c i) -> p h c i", h=NP, c=NCH),
            s_ao[:].rearrange("p (h c i) -> p h c i", h=NP, c=NCH),
            pw2b, ALU.subtract),
           ["ao2", "pw2"], ["e1"])
        SE(lambda e, par=par: e.activation(
            out=s_e1[:],
            in_=s_e1[:], func=AF.Exp),
           ["e1"], ["e1"])
        for h01 in range(2):
            pev = d_pcs.rearrange("(hp h2) b -> h2 hp b", h2=2)[
                h01, :, w * NCH + 1:(w + 1) * NCH + 1]
            pceb = pev.unsqueeze(0).broadcast_to((64, NP, NCH))
            dma("sync", s_pw2[h01 * 64:(h01 + 1) * 64, 0:64], pceb,
                ["pcs", "e1"], ["pw2"])
        pw2c = s_pw2[:, 0:64].rearrange("p (h c) -> p h c", h=NP) \
            .unsqueeze(3).broadcast_to((128, NP, NCH, 16))
        GE(lambda e, pw2c=pw2c: e.tensor_tensor(
            s_e2[:].rearrange("p (h c i) -> p h c i", h=NP, c=NCH),
            pw2c,
            s_ao[:].rearrange("p (h c i) -> p h c i", h=NP, c=NCH),
            ALU.subtract),
           ["ao2", "pw2"], ["e2"])
        SE(lambda e, par=par: e.activation(
            out=s_e2[:],
            in_=s_e2[:], func=AF.Exp),
           ["e2"], ["e2"])

    def wrep_load(w, k):
        par, cp = w % 2, k % 2
        for h01 in range(2):
            srcw = d_wd[par, k].rearrange("(hp h2) j i -> h2 hp (j i)", h2=2)[h01] \
                               .unsqueeze(0).broadcast_to((64, NP, C * C))
            dma("sync", s_wrep[h01 * 64:(h01 + 1) * 64,
                               cp * 2048:(cp + 1) * 2048],
                srcw, [f"dwd{par}"], [f"wrep{cp}"], tag="wrepload")
        wv = s_wrep[:, cp * 2048:(cp + 1) * 2048].rearrange(
            "p (h j i) -> p h j i", h=NP, j=C)
        bb3 = win_slice(s_bb, w % 4)[:, :, k * C:(k + 1) * C] \
            .unsqueeze(3).broadcast_to((128, NP, C, C))
        VE(lambda e, wv=wv, bb3=bb3: e.tensor_tensor(wv, wv, bb3, ALU.mult),
           [f"wrep{cp}", "bb"], [f"wrep{cp}"], tag="wrepscale")

    def chunk_builds(sb, wi, k):
        e1sl = s_e1[:].rearrange("p (h t) -> p h t", h=NP)[:, :, k * C:(k + 1) * C]
        e2sl = s_e2[:].rearrange("p (h t) -> p h t", h=NP)[:, :, k * C:(k + 1) * C]
        kbsl = win_slice(s_kb, wi)[:, :, k * C:(k + 1) * C]
        qbsl = win_slice(s_qb, wi)[:, :, k * C:(k + 1) * C]
        VE(lambda e: e.tensor_tensor(s_kc[:].rearrange("p (h t) -> p h t", h=NP),
                                     kbsl, e1sl, ALU.mult),
           ["kb", "e1"], ["kc"])
        VE(lambda e: e.tensor_tensor(s_qc[:].rearrange("p (h t) -> p h t", h=NP),
                                     qbsl, e1sl, ALU.mult),
           ["qb", "e1"], ["qc"])
        VE(lambda e: e.tensor_tensor(s_k2[:].rearrange("p (h t) -> p h t", h=NP),
                                     kbsl, e2sl, ALU.mult),
           ["kb", "e2"], ["k2"])

    def chunk_steps(sb, wi, k, piece=None):
        w = sb * 4 + wi
        par, cp = w % 2, k % 2
        e1v = s_e1[:].rearrange(
            "p (h t) -> p h t", h=NP)
        vbsl = win_slice(s_vb, wi)[:, :, k * C:(k + 1) * C]
        bbsl = win_slice(s_bb, wi)[:, :, k * C:(k + 1) * C]
        uv = s_u[:].rearrange("p (h t) -> p h t", h=NP)
        for hp in range(NP):
            for h01 in range(2):
                b0 = h01 * 64
                lhs = s_s[b0:b0 + 64, hp * 64:(hp + 1) * 64]
                last = (hp == NP - 1 and h01 == 1)
                first = (hp == 0)
                TE(lambda e, lhs=lhs, hp=hp, b0=b0, first=first: e.matmul(
                    p_kv[b0:b0 + 64, hp * C:(hp + 1) * C], lhs,
                    s_kc[b0:b0 + 64, hp * C:(hp + 1) * C], start=first,
                    stop=True, skip_group_check=True),
                   ["S", "kc", "pkv"], ["pkvw"] if last else [])
                TE(lambda e, lhs=lhs, hp=hp, b0=b0, first=first: e.matmul(
                    p_ys[b0:b0 + 64, hp * C:(hp + 1) * C], lhs,
                    s_qc[b0:b0 + 64, hp * C:(hp + 1) * C], start=first,
                    stop=True, skip_group_check=True),
                   ["S", "qc", "pys"], [])
        for rnd in range(2):
            for hpi in range(4):
                hp = rnd * 4 + hpi
                TE(lambda e, hp=hp, hpi=hpi: e.transpose(
                    p_tr[0:16, hpi * 128:(hpi + 1) * 128],
                    s_k2[:, hp * C:(hp + 1) * C],
                    s_idf[:]), ["k2", "idf", "ptr"],
                   ["ptrw"] if hpi == 3 else [])
            SE(lambda e, rnd=rnd: e.copy(
                s_ktc[0:16, rnd * 512:(rnd + 1) * 512], p_tr[0:16, 0:512]),
               ["ptrw"], ["ptr", "ktc"])
        if piece is not None:
            piece()
        gcol = e1v[:, :, k * C + C - 1].unsqueeze(2).broadcast_to((128, NP, 64))
        sv = s_s[:].rearrange("p (h e) -> p h e", h=NP)
        VE(lambda e, gcol=gcol, sv=sv: e.tensor_tensor(sv, sv, gcol, ALU.mult),
           ["S", "e1"], ["S"])
        VE(lambda e: e.tensor_tensor(
            uv, vbsl, p_kv[:, 0:128].rearrange("p (h t) -> p h t", h=NP),
            ALU.subtract),
           ["vb", "pkvw"], ["u", "pkv"])
        VE(lambda e: e.tensor_tensor(uv, uv, bbsl, ALU.mult),
           ["u", "bb"], ["u"])
        if k + 1 < NCH:
            chunk_builds(sb, wi, k + 1)
        for m in range(C - 1):
            ucol = uv[:, :, m]
            width = C - 1 - m
            wsl = s_wrep[:, cp * 2048:(cp + 1) * 2048].rearrange(
                "p (h j i) -> p h j i", h=NP, j=C)[:, :, m + 1:, m]
            ub = ucol.unsqueeze(2).broadcast_to((128, NP, width))
            ptv = s_ptm[:].rearrange("p (h t) -> p h t", h=NP)[:, :, 0:width]
            VE(lambda e, wsl=wsl, ub=ub, ptv=ptv: e.tensor_tensor(
                ptv, wsl, ub, ALU.mult), ["u", f"wrep{cp}"], ["ptm"],
               tag="slv_p")
            usl = uv[:, :, m + 1:]
            VE(lambda e, usl=usl, ptv=ptv: e.tensor_tensor(
                usl, usl, ptv, ALU.add), ["u", "ptm"], ["u"], tag="slv_u")
        for rnd in range(2):
            for hpi in range(4):
                hp = rnd * 4 + hpi
                TE(lambda e, hp=hp, hpi=hpi: e.transpose(
                    p_tr[0:16, hpi * 128:(hpi + 1) * 128],
                    s_u[:, hp * C:(hp + 1) * C],
                    s_idf[:]), ["u", "idf", "ptr"],
                   ["ptrw"] if hpi == 3 else [])
            SE(lambda e, rnd=rnd: e.copy(
                s_ut[0:16, rnd * 512:(rnd + 1) * 512], p_tr[0:16, 0:512]),
               ["ptrw"], ["ptr", "ut"])
        for hp in range(NP):
            for h01 in range(2):
                b0 = h01 * 64
                last = (hp == NP - 1 and h01 == 1)
                first = (hp == 0)
                TE(lambda e, hp=hp, b0=b0, first=first: e.matmul(
                    p_su[b0:b0 + 64, hp * 64:(hp + 1) * 64],
                    s_ktc[0:16, hp * 128 + b0:hp * 128 + b0 + 64],
                    s_ut[0:16, hp * 128 + b0:hp * 128 + b0 + 64],
                    start=first, stop=True, skip_group_check=True),
                   ["ktc", "ut", "psu"], ["psuw"] if last else [])
        VE(lambda e, sv=sv: e.tensor_tensor(
            sv, sv, p_su[:].rearrange("p (h e) -> p h e", h=NP), ALU.add),
           ["S", "psuw"], ["S", "psu"])
        qkoff = par * (NCH * H * C) + k * (H * C)
        for h in range(H):
            hp, h01 = h // 2, h % 2
            b0 = h01 * 64
            TE(lambda e, hp=hp, b0=b0, h=h, qkoff=qkoff: e.matmul(
                p_ys[b0:b0 + 64, hp * C:(hp + 1) * C],
                s_ut[0:16, hp * 128 + b0:hp * 128 + b0 + 64],
                s_qkb[0:16, qkoff + h * C:qkoff + (h + 1) * C],
                start=False, stop=True, skip_group_check=True),
               ["ut", f"qkb{par}", "pys"], ["pysw"] if h == H - 1 else [])
        yssl = s_ys[:].rearrange("p (h t) -> p h t", h=NP)[:, :, k * C:(k + 1) * C]
        SE(lambda e, yssl=yssl: e.copy(
            yssl, p_ys[:, 0:128].rearrange("p (h t) -> p h t", h=NP)),
           ["pysw"], ["pys", "ys"])

    def window_post(sb, wi):
        w = sb * 4 + wi
        SE(lambda e: e.activation(out=s_scr[:], in_=s_ys[:], func=AF.Square),
           ["ys"], ["scr"])
        for half in range(2):
            TE(lambda e, half=half: e.matmul(
                p_tr[0:2, 0:512], s_esel[:],
                s_scr[:, half * 512:(half + 1) * 512], start=True, stop=True),
               ["scr", "esel", "ptr"], ["ptrw"])
            VE(lambda e, half=half: e.tensor_scalar(
                out=s_rr2[:, half * 512:(half + 1) * 512], in0=p_tr[0:2, 0:512],
                scalar1=1.0 / DH, scalar2=EPS_RMS, op0=ALU.mult, op1=ALU.add),
               ["ptrw"], ["ptr", "rr2"])
        SE(lambda e: e.activation(out=s_rr2[:], in_=s_rr2[:], func=AF.Sqrt),
           ["rr2"], ["rr2"])
        VE(lambda e: e.reciprocal(s_rr2[:], s_rr2[:]), ["rr2"], ["rr2"])
        for half, bank, bkey in ((0, p_kv, "pkv"), (1, p_ys, "pys")):
            TE(lambda e, half=half, bank=bank: e.matmul(
                bank[:, 0:512], s_e2t[0:2, :],
                s_rr2[0:2, half * 512:(half + 1) * 512],
                start=True, stop=True),
               ["rr2", "e2t", bkey], [bkey + "w"])
            if half == 0:
                SE(lambda e, half=half, bank=bank: e.copy(
                    s_rrep[:, half * 512:(half + 1) * 512], bank[:, 0:512]),
                   [bkey + "w"], [bkey, "rrep"])
            else:
                VE(lambda e, half=half, bank=bank: e.tensor_copy(
                    s_rrep[:, half * 512:(half + 1) * 512], bank[:, 0:512]),
                   [bkey + "w"], [bkey, "rrep"])
        VE(lambda e: e.tensor_tensor(s_ys[:], s_ys[:], s_rrep[:], ALU.mult),
           ["ys", "rrep"], ["ys"])
        gsl = win_slice(s_gb, wi)
        VE(lambda e, gsl=gsl, wi=wi: e.scalar_tensor_tensor(
            out=s_lt[:, wi * D:(wi + 1) * D].rearrange("p (h t) -> p h t", h=NP),
            in0=s_ys[:].rearrange("p (h t) -> p h t", h=NP),
            scalar=s_nwcol[:], in1=gsl, op0=ALU.mult, op1=ALU.mult),
           ["ys", "nwcol", "gb"], ["lt"])

    OUT_BANKS = [(p_pr, "ppr"), (p_kv, "pkv"), (p_ys, "pys"), (p_su, "psu")]
    STAGING = [(s_ao, "ao2"), (s_scr, "scr"), (s_e1, "e1"), (s_e2, "e2")]

    def out_ln_residual(wi):
        stag, skey = STAGING[wi]
        ln_stats(stag[:], skey)
        VE(lambda e, stag=stag: e.tensor_scalar(
            out=stag[:], in0=stag[:], scalar1=s_st[:, 0:1],
            scalar2=s_st[:, 1:2], op0=ALU.add, op1=ALU.mult),
           [skey, "st"], [skey])
        xt = s_x[:, wi * D:(wi + 1) * D]
        VE(lambda e, xt=xt, stag=stag: e.tensor_tensor(
            xt, xt, stag[:], ALU.add), ["x", skey], ["x"])

    def attn_out_batch(sb):
        for cc in range(4):
            wsl = s_wrhs[:, cc * 512:cc * 512 + 512]
            dma("sync" if cc < 2 else "gpsimd", wsl,
                d_wto[cc * 128:(cc + 1) * 128, 0:512],
                [f"ws{cc}", "dwo"], [f"ws{cc}"], tag="wostream")
        for oh in range(2):
            for cc in range(8):
                par2 = cc % 4
                wsl = s_wrhs[:, par2 * 512:par2 * 512 + 512]
                if not (oh == 0 and cc < 4):
                    dma("sync" if par2 < 2 else "gpsimd", wsl,
                        d_wto[cc * 128:(cc + 1) * 128, oh * 512:(oh + 1) * 512],
                        [f"ws{par2}", "dwo"], [f"ws{par2}"], tag="wostream")
                for wi in range(4):
                    bank, bkey = OUT_BANKS[wi]
                    TE(lambda e, cc=cc, wsl=wsl, wi=wi, bank=bank: e.matmul(
                        bank[:], s_lt[:, wi * D + cc * 128:wi * D + (cc + 1) * 128],
                        wsl, start=(cc == 0), stop=(cc == 7)),
                       [f"ws{par2}", "lt", bkey] if cc == 0
                       else [f"ws{par2}", "lt"],
                       [bkey + "w"] if cc == 7 else [])
            for wi in range(4):
                stag, skey = STAGING[wi]
                bank, bkey = OUT_BANKS[wi]
                VE(lambda e, oh=oh, stag=stag, bank=bank: e.tensor_tensor(
                    stag[:, oh * 512:(oh + 1) * 512], bank[:],
                    s_borep[:, oh * 512:(oh + 1) * 512], ALU.add),
                   [bkey + "w", "borep"], [bkey, skey])
        for wi in range(4):
            out_ln_residual(wi)

    def ffn_superblock(sb):
        for fb in range(4):
            wsl = s_wrhs[:, fb * 512:fb * 512 + 512]
            dma("sync" if fb < 2 else "gpsimd", wsl,
                d_wtf[fb * 128:(fb + 1) * 128, 0:512],
                [f"ws{fb}", "dwf"], [f"ws{fb}"], tag="wfstream")
        for wi in range(4):
            xt = s_x[:, wi * D:(wi + 1) * D]
            ln_stats(xt, "x")
            VE(lambda e, xt=xt: e.tensor_scalar(out=s_q1[:], in0=xt,
                                                scalar1=s_st[:, 0:1],
                                                scalar2=s_st[:, 1:2],
                                                op0=ALU.add, op1=ALU.mult),
               ["x", "st"], ["wld"])
            emit_quantize(s_q1[:], s_xd[:], "wld", "xd")
            for fb in range(8):
                bank, bkey = (p_sc, "psc") if fb % 2 == 0 else (p_sc2, "psc2")
                TE(lambda e, fb=fb, bank=bank: e.matmul(
                    bank[:, 0:128],
                    s_xd[:, fb * 128:(fb + 1) * 128], s_idb[:],
                    start=True, stop=True),
                   ["xd", "idb", bkey], [bkey + "w"])
                if fb % 2 == 0:
                    SE(lambda e, fb=fb, bank=bank, wi=wi: e.copy(
                        s_lt[:, fb * SBT + wi * 128: fb * SBT + (wi + 1) * 128],
                        bank[:, 0:128]),
                       [bkey + "w"], [bkey, "lt"])
                else:
                    VE(lambda e, fb=fb, bank=bank, wi=wi: e.tensor_copy(
                        s_lt[:, fb * SBT + wi * 128: fb * SBT + (wi + 1) * 128],
                        bank[:, 0:128]),
                       [bkey + "w"], [bkey, "lt"])
        for oh in range(2):
            for fb in range(8):
                par2 = fb % 4
                wsl = s_wrhs[:, par2 * 512:par2 * 512 + 512]
                if not (oh == 0 and fb < 4):
                    dma("sync" if par2 < 2 else "gpsimd", wsl,
                        d_wtf[fb * 128:(fb + 1) * 128, oh * 512:(oh + 1) * 512],
                        [f"ws{par2}", "dwf"], [f"ws{par2}"], tag="wfstream")
                for wi in range(4):
                    bank, bkey = OUT_BANKS[wi]
                    TE(lambda e, fb=fb, wi=wi, wsl=wsl, bank=bank: e.matmul(
                        bank[:],
                        s_lt[:, fb * SBT + wi * 128:fb * SBT + (wi + 1) * 128],
                        wsl, start=(fb == 0), stop=(fb == 7)),
                       [f"ws{par2}", "lt", bkey] if fb == 0
                       else [f"ws{par2}", "lt"],
                       [bkey + "w"] if fb == 7 else [])
            for wi in range(4):
                stag, skey = STAGING[wi]
                bank, bkey = OUT_BANKS[wi]
                VE(lambda e, oh=oh, stag=stag, bank=bank: e.tensor_tensor(
                    stag[:, oh * 512:(oh + 1) * 512], bank[:],
                    s_bfrep[:, oh * 512:(oh + 1) * 512], ALU.add),
                   [bkey + "w", "bfrep"], [bkey, skey])
        for wi in range(4):
            out_ln_residual(wi)
        dstq = d_out.rearrange("(n p) d -> p n d", p=128)[:, sb * 4:(sb + 1) * 4, :]
        dma("sync", dstq, s_x[:].rearrange("p (n d) -> p n d", d=D), ["x"],
            ["xout"], tag="xstore")

    emit_preamble()
    for sb in range(NSB):
        if sb > 0:
            proj_superblock(sb)
        if STAGE == "proj":
            for key, tile, dd in [("kb", s_kb, dbg["kb"]), ("qb", s_qb, dbg["qb"]),
                                  ("vb", s_vb, dbg["vb"]), ("bb", s_bb, dbg["bb"])]:
                dv = dd.rearrange("p (h t) -> p h t", h=NP)[
                    :, :, sb * SBT:(sb + 1) * SBT]
                for hp in range(NP):
                    VE(lambda e, tile=tile, hp=hp: e.tensor_copy(
                        s_q1[:, 0:512], tile[:, hp * SBT:(hp + 1) * SBT]),
                       [key], ["wld"])
                    dma("sync", dv[:, hp, :], s_q1[:, 0:512], ["wld"], ["dbg"])
            continue
        for wi in range(WPS):
            if sb * 4 + wi >= NWINLIM:
                continue
            if wi == 0:
                norms_window(sb, 0)
                scores_and_masks(sb, 0)
                wrep_load(sb * 4, 0)
            pieces = []
            if wi + 1 < WPS and sb * 4 + wi + 1 < NWINLIM:
                norms_window(sb, wi + 1)
                pieces = scores_pieces(sb, wi + 1)
            ebuild_window(sb, wi)
            chunk_builds(sb, wi, 0)
            if STAGE == "scan" and sb == 0 and wi == 0:
                dma("sync", dbg["e1"], s_e1[:], ["e1"], ["dbg"])
                dma("sync", dbg["pw1"], s_ao[:], ["ao2"], ["dbg"])
                dma("sync", dbg["pw2"], s_pw2[:, 0:64], ["pw2"], ["dbg"])
                dma("gpsimd", dbg["kc"], win_slice(s_kb, wi)[:, :, 0:16],
                    ["kb"], ["dbg"])
            for k in range(NCH):
                if k >= NCHLIM:
                    continue
                if k + 1 < NCH:
                    wrep_load(sb * 4 + wi, k + 1)
                elif wi + 1 < WPS and sb * 4 + wi + 1 < NWINLIM:
                    wrep_load(sb * 4 + wi + 1, 0)
                chunk_steps(sb, wi, k,
                            piece=pieces[k] if k < len(pieces) else None)
                if STAGE == "scan" and sb == 0 and wi == 0 and k == 0:
                    dma("sync", dbg["u0"], s_u[:], ["u"], ["dbg"])
                    dma("gpsimd", dbg["v0"], win_slice(s_vb, wi)[:, :, 0:16],
                        ["vb"], ["dbg"])
                    dma("gpsimd", dbg["b0"], win_slice(s_bb, wi)[:, :, 0:16],
                        ["bb"], ["dbg"])
                    dma("sync", dbg["wr0"], s_wrep[:, 0:2048], ["wrep0"], ["dbg"])
                    dma("sync", dbg["qk0"], s_qkb[0:16, 0:2048], ["qkb0"], ["dbg"])
            if STAGE == "scan":
                w = sb * 4 + wi
                dv = dbg["ys"].rearrange("p (h t) -> p h t", h=NP)[
                    :, :, w * WIN:(w + 1) * WIN]
                dma("sync", dv, s_ys[:].rearrange("p (h t) -> p h t", h=NP),
                    ["ys"], ["dbg"])
                continue
            window_post(sb, wi)
        if STAGE == "scan":
            continue
        attn_out_batch(sb)
        if STAGE == "attn":
            dv = dbg["xm"].rearrange("(n p) d -> p n d", p=128)[
                :, sb * 4:(sb + 1) * 4, :]
            dma("sync", dv, s_x[:].rearrange("p (n d) -> p n d", d=D), ["x"],
                ["dbg"])
            continue
        ffn_superblock(sb)
    if STAGE == "proj":
        dma("sync", dbg["pp"], s_pp[:, 0:T + 1], ["pp"], ["dbg"])

    with nc.Block() as block:
        sems = {}
        for e in ENGINES:
            t = nc.semaphore(f"sem_{e}")
            ctxs.append(t)
            sems[e] = t.__enter__()
        for semk in P.dq_counts:
            t = nc.semaphore(semk)
            ctxs.append(t)
            sems[semk] = t.__enter__()
        final_counts = dict(P.counts)
        final_dq = dict(P.dq_counts)

        waited = {}
        for st_ in P.steps:
            for de, dv_ in st_["waits"].items():
                waited.setdefault(de, set()).add(dv_)
        for oe in ENGINES:
            waited.setdefault(oe, set()).add(P.counts[oe])
        cnt = {e: 0 for e in ENGINES}
        keep = []
        kept_cum = {e: 0 for e in ENGINES}
        remap = {e: {0: 0} for e in ENGINES}
        for st_ in P.steps:
            semk = st_["sem"]
            if semk in ENGINES:
                prev = cnt[semk]
                cur = prev + 1
                cnt[semk] = cur
                need = any(prev < w <= cur for w in waited.get(semk, ()))
                keep.append(need)
                if need:
                    kept_cum[semk] += 1
                remap[semk][cur] = kept_cum[semk]
            else:
                keep.append(True)

        def mapv(semk, v):
            if semk in ENGINES:
                return remap[semk][v]
            return v

        def make_stream(ename):
            def f(eng):
                last_wait = {}
                idx = 0
                for sti, st_ in enumerate(P.steps):
                    if st_["engine"] != ename:
                        continue
                    for de, dv_ in sorted(st_["waits"].items()):
                        v = mapv(de, dv_)
                        if v <= 0:
                            continue
                        if last_wait.get(de, -1) >= v:
                            continue
                        eng.wait_ge(sems[de], v)
                        last_wait[de] = v
                    inst = st_["fn"](eng)
                    if inst is None:
                        inst = eng.nop()
                    if keep[sti]:
                        inst.then_inc(sems[st_["sem"]], st_["inc"])
                if ename == "sync":
                    for oe in ENGINES:
                        if oe != "sync":
                            v = mapv(oe, final_counts[oe])
                            if v > 0 and last_wait.get(oe, -1) < v:
                                eng.wait_ge(sems[oe], v)
                    for semk, v in final_dq.items():
                        if last_wait.get(semk, -1) < v:
                            eng.wait_ge(sems[semk], v)
            return f

        block.sync(make_stream("sync"))
        block.tensor(make_stream("tensor"))
        block.vector(make_stream("vector"))
        block.scalar(make_stream("scalar"))
        block.gpsimd(make_stream("gpsimd"))

    for t in reversed(ctxs):
        t.__exit__(None, None, None)
    return nc


_NC_CACHE = {}


def kernel(L0_attn_x, L0_attn_Wq, L0_attn_Wk, L0_attn_Wv, L0_attn_Wg, L0_attn_Wb,
           L0_attn_Wa, L0_attn_dt_bias, L0_attn_A_log, L0_attn_norm_w, L0_attn_Wo,
           L0_attn_bo, L0_ffn_weight, L0_ffn_bias):
    if "prog" not in _NC_CACHE:
        _NC_CACHE["prog"] = build_program()
    nc = _NC_CACHE["prog"]
    consts = build_consts()
    x = np.ascontiguousarray(np.asarray(L0_attn_x, dtype=np.float32))
    base = dict(
        Wq=np.ascontiguousarray(np.asarray(L0_attn_Wq, np.float32)),
        Wk=np.ascontiguousarray(np.asarray(L0_attn_Wk, np.float32)),
        Wv=np.ascontiguousarray(np.asarray(L0_attn_Wv, np.float32)),
        Wg=np.ascontiguousarray(np.asarray(L0_attn_Wg, np.float32)),
        Wb=np.ascontiguousarray(np.asarray(L0_attn_Wb, np.float32)),
        Wa=np.ascontiguousarray(np.asarray(L0_attn_Wa, np.float32)),
        dt_bias=np.ascontiguousarray(np.asarray(L0_attn_dt_bias, np.float32)
                                     .reshape(H, 1)),
        A_log=np.ascontiguousarray(np.asarray(L0_attn_A_log, np.float32)
                                   .reshape(H, 1)),
        norm_w=np.ascontiguousarray(np.asarray(L0_attn_norm_w, np.float32)),
        Wo=np.ascontiguousarray(np.asarray(L0_attn_Wo, np.float32)),
        bo=np.ascontiguousarray(np.asarray(L0_attn_bo, np.float32)),
        Wf=np.ascontiguousarray(np.asarray(L0_ffn_weight, np.float32)),
        bfv=np.ascontiguousarray(np.asarray(L0_ffn_bias, np.float32)),
        ident_f32=consts["ident_f32"],
        ident_bf16=consts["ident_bf16"],
        negtril=consts["negtril"],
        triincl=consts["triincl"],
        esel2=consts["esel2"],
        esel2t=consts["esel2t"],
        mska=consts["mska"],
        mskb=consts["mskb"],
    )
    in_maps = []
    for b in range(NCORES):
        m = dict(base)
        m["x"] = np.ascontiguousarray(x[b])
        in_maps.append(m)
    res = run_bass_kernel_spmd(nc, in_maps, core_ids=list(range(NCORES)))
    out = np.stack([np.asarray(res.results[b]["out"]) for b in range(NCORES)],
                   axis=0)
    return out.astype(np.float32)


# version 16
# speedup vs baseline: 1.0097x; 1.0069x over previous
import os
import numpy as np
import ml_dtypes

import concourse.bass as bass
import concourse.mybir as mybir
from concourse.bass_utils import run_bass_kernel_spmd

F32 = mybir.dt.float32
BF16 = mybir.dt.bfloat16
AF = mybir.ActivationFunctionType
ALU = mybir.AluOpType
AX = mybir.AxisListType

NCORES = 8
T = 2048
D = 1024
H = 16
DH = 64
NP = 8
C = 16
WIN = 128
NWIN = T // WIN
NCH = WIN // C
SBT = 512
NSB = T // SBT
WPS = SBT // WIN
EPS_LN = 1e-5
EPS_RMS = 1e-6
FP6_MAX = 7.5

STAGE = os.environ.get("BASS_STAGE", "full")
NWINLIM = int(os.environ.get("BASS_NWINLIM", "999"))
NCHLIM = int(os.environ.get("BASS_NCHLIM", "999"))

ENGINES = ("sync", "tensor", "vector", "scalar", "gpsimd")


DMA_SLOTS = {"sync": 10, "gpsimd": 6}


class Plan:

    def __init__(self):
        self.steps = []
        self.counts = {e: 0 for e in ENGINES}
        self.dq_counts = {}
        self.dma_idx = {q: 0 for q in DMA_SLOTS}
        self.slot_prev = {}
        self.last_write = {}
        self.readers = {}

    def add(self, engine, fn, reads=(), writes=(), dma=False, tag=""):
        deps = {}

        def upd(ev):
            e, v = ev
            if v > deps.get(e, -1):
                deps[e] = v

        for r in reads:
            lw = self.last_write.get(r)
            if lw:
                upd(lw)
        for r in writes:
            lw = self.last_write.get(r)
            if lw:
                upd(lw)
            for rd in self.readers.get(r, ()):
                upd(rd)
        if dma:
            q = engine
            slot = self.dma_idx[q] % DMA_SLOTS[q]
            self.dma_idx[q] += 1
            semk = f"dq_{q}_{slot}"
            prevres = self.slot_prev.get((q, slot))
            if prevres is not None:
                upd((semk, self.dq_counts.get(semk, 0)))
                for r in prevres:
                    for rd in self.readers.get(r, ()):
                        upd(rd)
            tgt = self.dq_counts.get(semk, 0) + 16
            self.dq_counts[semk] = tgt
            self.steps.append(dict(engine=engine, fn=fn, waits=dict(deps),
                                   sem=semk, inc=16, tag=tag))
            sig = (semk, tgt)
            self.slot_prev[(q, slot)] = list(writes) + list(reads)
        else:
            after = self.counts[engine] + 1
            self.counts[engine] = after
            self.steps.append(dict(engine=engine, fn=fn, waits=dict(deps),
                                   sem=engine, inc=1, tag=tag))
            sig = (engine, after)
        for r in writes:
            self.last_write[r] = sig
            self.readers[r] = []
        for r in reads:
            self.readers.setdefault(r, []).append(sig)


def build_consts():
    c = {}
    c["ident_f32"] = np.eye(128, dtype=np.float32)
    c["ident_bf16"] = np.eye(128, dtype=ml_dtypes.bfloat16)
    j = np.arange(C)
    negtril = -(j[:, None] > j[None, :]).astype(np.float32)
    c["negtril"] = np.ascontiguousarray(
        np.broadcast_to(negtril[None, :, None, :], (NCH, C, H, C)).reshape(128, H * C))
    triincl = (j[:, None] <= j[None, :]).astype(np.float32)
    c["triincl"] = np.ascontiguousarray(
        np.broadcast_to(triincl[None, :, None, :], (NCH, C, H, C)).reshape(128, H * C))
    ntriu = -(j[None, :] > j[:, None]).astype(np.float32)
    c["ntriu"] = np.ascontiguousarray(
        np.broadcast_to(ntriu[None, :, None, :], (NCH, C, H, C)).reshape(128, H * C))
    e = np.zeros((128, 2), dtype=np.float32)
    e[0:64, 0] = 1.0
    e[64:128, 1] = 1.0
    c["esel2"] = e
    e2t = np.zeros((2, 128), dtype=np.float32)
    e2t[0, 0:64] = 1.0
    e2t[1, 64:128] = 1.0
    c["esel2t"] = e2t
    c["mska"] = np.full((128, 1), ~((1 << 19) - 1), np.int32)
    c["mskb"] = np.full((128, 1), ~((1 << 20) - 1), np.int32)
    return c


def build_program():
    nc = bass.Bass("TRN2", target_bir_lowering=False)
    P = Plan()

    def inp(name, shape, dtype=F32):
        return nc.dram_tensor(name, list(shape), dtype, kind="ExternalInput").ap()

    d_x = inp("x", (T, D))
    d_w = {m: inp("W" + m, (D, D)) for m in "qkvgb"}
    d_wa = inp("Wa", (H, D))
    d_dt = inp("dt_bias", (H, 1))
    d_alog = inp("A_log", (H, 1))
    d_nw = inp("norm_w", (DH,))
    d_wo = inp("Wo", (D, D))
    d_bo = inp("bo", (D,))
    d_wf = inp("Wf", (D, D))
    d_bf = inp("bfv", (D,))
    d_idf = inp("ident_f32", (128, 128))
    d_idb = inp("ident_bf16", (128, 128), BF16)
    d_ntril = inp("negtril", (128, H * C))
    d_ttri = inp("triincl", (128, H * C))
    d_ntriu = inp("ntriu", (128, H * C))
    d_esel = inp("esel2", (128, 2))
    d_e2t = inp("esel2t", (2, 128))
    d_mska = inp("mska", (128, 1), mybir.dt.int32)
    d_mskb = inp("mskb", (128, 1), mybir.dt.int32)

    d_out = nc.dram_tensor("out", [T, D], F32, kind="ExternalOutput").ap()
    dbg = {}
    if STAGE != "full":
        for name, shape in [("kb", (128, T * NP)), ("qb", (128, T * NP)),
                            ("vb", (128, T * NP)), ("bb", (128, T * NP)),
                            ("pp", (H, T + 1)), ("ys", (128, T * NP)),
                            ("xm", (T, D)), ("e1", (128, 1024)),
                            ("kc", (128, 128)), ("pw1", (128, 1024)),
                            ("pw2", (128, 64)), ("u0", (128, 128)),
                            ("wr0", (128, 2048)), ("qk0", (16, 2048)),
                            ("v0", (128, 128)), ("b0", (128, 128))]:
            dbg[name] = nc.dram_tensor("dbg_" + name, list(shape), F32,
                                       kind="ExternalOutput").ap()

    def scratch(name, shape, dtype=F32):
        return nc.dram_tensor(name, list(shape), dtype, kind="Internal").ap()

    d_wt = {m: scratch("wt" + m, (D, D), BF16) for m in "qkvgb"}
    d_wto = scratch("wto", (D, D), BF16)
    d_wtf = scratch("wtf", (D, D), BF16)
    d_wat = scratch("wat", (D, H), BF16)
    d_pd = scratch("pd", (H, T + 32))
    d_pcs = scratch("pcs", (H, 160))
    d_rk = scratch("rk", (H, T))
    d_rq = scratch("rq", (H, T))
    d_rr = scratch("rr", (H, T))
    d_wd = scratch("wd", (2, NCH, H, C, C))
    d_qk = scratch("qk", (2, NCH, H, C, C))

    ctxs = []

    def sbuf(name, shape, dtype=F32):
        t = nc.sbuf_tensor(name, list(shape), dtype)
        ctxs.append(t)
        return t.__enter__()

    def psum(name, shape, dtype=F32):
        t = nc.psum_tensor(name, list(shape), dtype)
        ctxs.append(t)
        return t.__enter__()

    s_idf = sbuf("idf", (128, 128))
    s_idb = sbuf("idb", (128, 128), BF16)
    s_ntril = sbuf("ntril", (128, H * C))
    s_ttri = sbuf("ttri", (128, H * C))
    s_ntriu = sbuf("sntriu", (128, H * C))
    s_esel = sbuf("esel", (128, 2))
    s_e2t = sbuf("e2t", (2, 128))
    s_mka = sbuf("mka", (128, 1), mybir.dt.int32)
    s_mkb = sbuf("mkb", (128, 1), mybir.dt.int32)
    s_dt = sbuf("dt", (H, 1))
    s_ealog = sbuf("ealog", (H, 1))
    s_nwcol = sbuf("nwcol", (128, 1))
    s_borep = sbuf("borep", (128, D))
    s_bfrep = sbuf("bfrep", (128, D))
    s_ones = sbuf("ones", (H, SBT))

    s_wld = sbuf("wld", (128, D))
    s_wbf = sbuf("wbf", (128, D), BF16)
    s_wtt = sbuf("wtt", (128, D), BF16)

    s_x = sbuf("sx", (128, 4 * D))
    s_tb = sbuf("tb", (128, D), BF16)
    s_lt = sbuf("lt", (128, 8 * SBT), BF16)
    s_kb = sbuf("kb", (128, NP * SBT), BF16)
    s_qb = sbuf("qb", (128, NP * SBT), BF16)
    s_vb = sbuf("vb", (128, NP * SBT), BF16)
    s_bb = sbuf("bb", (128, NP * SBT), BF16)
    s_gb = sbuf("gb", (128, NP * SBT), BF16)
    s_wrhs = sbuf("wrhs", (128, 2 * 1024), BF16)

    s_pp = sbuf("pp", (H, T + 32))
    s_ppt = sbuf("ppt", (128, NWIN * H))
    s_at = sbuf("at", (H, SBT))
    s_at2 = sbuf("at2", (H, SBT))
    s_pcs = sbuf("spcs", (H, 40))

    s_sc = sbuf("sc", (128, H * WIN))
    s_wdst = sbuf("wdst", (128, H * C))
    s_mt = sbuf("mt", (128, H * C))
    s_prow = sbuf("prow", (128, H * C))
    s_qkb = sbuf("qkb", (16, 2 * NCH * H * C))
    s_wrep = sbuf("wrep", (128, 2 * C * C * NP))
    s_e1 = sbuf("e1", (128, NP * WIN))
    s_e2 = sbuf("e2", (128, NP * WIN))
    s_pw1 = sbuf("pw1", (128, NP * WIN))
    s_pw2 = sbuf("pw2", (128, NP * WIN))
    s_scr = sbuf("scr", (128, NP * WIN))
    s_rr2 = sbuf("rr2", (2, NP * WIN))
    s_rrep = sbuf("rrep", (128, NP * WIN))

    s_u = sbuf("u", (128, NP * C))
    s_ptm = sbuf("ptm", (128, NP * C))
    s_kc = sbuf("kc", (128, NP * C))
    s_qc = sbuf("qc", (128, NP * C))
    s_k2 = sbuf("k2", (128, NP * C))
    s_ut = sbuf("ut", (16, NP * 128))
    s_ktc = sbuf("ktc", (16, NP * 128))
    s_s = sbuf("s", (128, NP * DH))
    s_ys = sbuf("ys", (128, NP * WIN))
    s_qtmp = s_ys
    s_ctx = sbuf("ctx", (128, NP * WIN), BF16)

    s_ao = sbuf("ao", (128, D))
    s_q2 = sbuf("q2", (128, D))
    s_q3 = sbuf("q3", (128, 32))
    s_q4 = sbuf("q4", (128, 32))
    s_qs = s_scr
    s_q1x = s_pw1
    s_mfac = s_pw2
    s_q1 = s_wld
    s_msk = sbuf("msk", (128, D), mybir.dt.uint8)
    s_xd = sbuf("xd", (128, D), BF16)
    s_st = sbuf("st", (128, 8))

    p_pr = psum("ppr", (128, 512))
    p_sc = psum("psc", (128, 512))
    p_tp = psum("ptp", (128, 1024), BF16)
    p_tr = psum("ptr", (16, 512))
    p_kv = psum("pkv", (128, 512))
    p_ys = psum("pys", (128, 512))
    p_su = psum("psu", (128, 512))
    p_sc2 = psum("psc2", (128, 512))

    def dma(q, out, in_, rd, wr, tag=""):
        P.add(q, lambda e, o=out, i=in_: e.dma_start(out=o, in_=i),
              reads=list(rd), writes=list(wr), dma=True, tag=tag)

    def VE(fn, rd, wr, tag=""):
        P.add("vector", fn, reads=list(rd), writes=list(wr), tag=tag)

    def SE(fn, rd, wr, tag=""):
        P.add("scalar", fn, reads=list(rd), writes=list(wr), tag=tag)

    def TE(fn, rd, wr, tag=""):
        P.add("tensor", fn, reads=list(rd), writes=list(wr), tag=tag)

    def GE(fn, rd, wr, tag=""):
        P.add("gpsimd", fn, reads=list(rd), writes=list(wr), tag=tag)

    dma("sync", s_idf[:], d_idf, [], ["idf"])
    dma("sync", s_idb[:], d_idb, [], ["idb"])
    dma("sync", s_ntril[:], d_ntril, [], ["ntril"])
    dma("sync", s_ttri[:], d_ttri, [], ["ttri"])
    dma("sync", s_ntriu[:], d_ntriu, [], ["ntriu"])
    dma("sync", s_esel[:], d_esel, [], ["esel"])
    dma("sync", s_e2t[:], d_e2t, [], ["e2t"])
    dma("sync", s_mka[:], d_mska, [], ["mka"])
    dma("sync", s_mkb[:], d_mskb, [], ["mkb"])
    dma("sync", s_dt[:], d_dt, [], ["dt"])
    dma("sync", s_ealog[:], d_alog, [], ["ealog"])
    dma("sync", s_nwcol[:], d_nw.unsqueeze(0).unsqueeze(2).broadcast_to((2, 64, 1)),
        [], ["nwcol"])
    dma("sync", s_borep[:], d_bo.unsqueeze(0).broadcast_to((128, D)), [], ["borep"])
    dma("sync", s_bfrep[:], d_bf.unsqueeze(0).broadcast_to((128, D)), [], ["bfrep"])
    GE(lambda e: e.memset(s_ones[:], 1.0), [], ["ones"])
    GE(lambda e: e.memset(s_pp[:], 0.0), [], ["pp"])
    GE(lambda e: e.memset(s_s[:], 0.0), [], ["S"])
    GE(lambda e: e.memset(s_ys[:], 0.0), [], ["ys"])
    SE(lambda e: e.activation(out=s_ealog[:], in_=s_ealog[:], func=AF.Exp),
       ["ealog"], ["ealog"])
    VE(lambda e: e.tensor_scalar_mul(s_ealog[:], s_ealog[:], -1.0),
       ["ealog"], ["ealog"])

    def emit_quantize(src, dst, key, wkey):
        v3 = src.rearrange("p (a b) -> p a b", b=32)
        VE(lambda e: e.tensor_reduce(out=s_q3[:], in_=v3, axis=AX.X, op=ALU.max,
                                     apply_absolute_value=True), [key], ["q3"])
        VE(lambda e: e.tensor_scalar(out=s_q3[:], in0=s_q3[:], scalar1=1.0 / FP6_MAX,
                                     scalar2=1e-30, op0=ALU.mult, op1=ALU.max),
           ["q3"], ["q3"])
        VE(lambda e: e.reciprocal(s_q4[:], s_q3[:]), ["q3"], ["q4"])
        rb = s_q4[:].unsqueeze(2).broadcast_to((128, 32, 32))
        VE(lambda e: e.tensor_tensor(s_q2[:].rearrange("p (a b) -> p a b", b=32),
                                     v3, rb, ALU.mult), [key, "q4"], ["q2"])
        SE(lambda e: e.activation(out=s_qs[:], in_=s_q2[:], func=AF.Sign),
           ["q2"], ["scr"])
        SE(lambda e: e.activation(out=s_ao[:], in_=s_q2[:], func=AF.Abs),
           ["q2"], ["ao2"])
        VE(lambda e: e.tensor_scalar(out=s_msk[:], in0=s_ao[:], scalar1=4.0,
                                     scalar2=None, op0=ALU.is_ge),
           ["ao2"], ["msk"])
        VE(lambda e: e.scalar_tensor_tensor(out=s_mfac[:], in0=s_qs[:],
                                            scalar=0.125, in1=s_q2[:],
                                            op0=ALU.mult, op1=ALU.add),
           ["scr", "q2"], ["pw2"])
        VE(lambda e: e.scalar_tensor_tensor(out=s_qtmp[:], in0=s_qs[:],
                                            scalar=0.25, in1=s_q2[:],
                                            op0=ALU.mult, op1=ALU.add),
           ["scr", "q2", "ys"], ["qtmp"])
        VE(lambda e: e.select(out=s_mfac[:], mask=s_msk[:], on_true=s_qtmp[:],
                              on_false=s_mfac[:], add_drain=True),
           ["msk", "pw2", "qtmp"], ["pw2"])
        VE(lambda e: e.tensor_scalar(out=s_qtmp[:].bitcast(mybir.dt.int32),
                                     in0=s_mfac[:].bitcast(mybir.dt.int32),
                                     scalar1=s_mkb[:], scalar2=None,
                                     op0=ALU.bitwise_and),
           ["pw2", "mkb", "ys"], ["qtmp"])
        VE(lambda e: e.scalar_tensor_tensor(out=s_q1x[:], in0=s_qs[:],
                                            scalar=2.0625, in1=s_q2[:],
                                            op0=ALU.mult, op1=ALU.add),
           ["scr", "q2"], ["pw1"])
        VE(lambda e: e.tensor_scalar(out=s_mfac[:].bitcast(mybir.dt.int32),
                                     in0=s_q1x[:].bitcast(mybir.dt.int32),
                                     scalar1=s_mka[:], scalar2=None,
                                     op0=ALU.bitwise_and), ["pw1", "mka"], ["pw2"])
        VE(lambda e: e.scalar_tensor_tensor(out=s_q1x[:], in0=s_qs[:],
                                            scalar=-2.0, in1=s_mfac[:],
                                            op0=ALU.mult, op1=ALU.add),
           ["scr", "pw2"], ["pw1"])
        VE(lambda e: e.tensor_scalar(out=s_msk[:], in0=s_ao[:], scalar1=2.0,
                                     scalar2=None, op0=ALU.is_ge),
           ["ao2"], ["msk"])
        VE(lambda e: e.select(out=s_q1x[:], mask=s_msk[:], on_true=s_qtmp[:],
                              on_false=s_q1x[:], add_drain=True),
           ["msk", "pw1", "qtmp"], ["pw1"])
        sb3 = s_q3[:].unsqueeze(2).broadcast_to((128, 32, 32))
        wkeys = wkey if isinstance(wkey, list) else [wkey]
        VE(lambda e, dst=dst: e.tensor_tensor(dst.rearrange("p (a b) -> p a b", b=32),
                                              s_q1x[:].rearrange("p (a b) -> p a b", b=32),
                                              sb3, ALU.mult), ["pw1", "q3"], wkeys)

    def prep_weight(d_src, d_dst, quantize=False, tagn=""):
        for ob in range(8):
            if quantize:
                dma("sync", s_wld[:], d_src[ob * 128:(ob + 1) * 128, :],
                    [], ["wld"], tag="wload")
                emit_quantize(s_wld[:], s_wbf[:], "wld", ["wbfA", "wbfB"])
            else:
                for hb in range(2):
                    dma("gpsimd", s_wbf[:, hb * 512:(hb + 1) * 512],
                        d_src[ob * 128:(ob + 1) * 128, hb * 512:(hb + 1) * 512],
                        [], ["wbf" + "AB"[hb]], tag="wload")
            for fb in range(8):
                if quantize:
                    bank, bkey = (p_kv, "pkv") if fb % 2 == 0 else (p_sc2, "psc2")
                else:
                    bank, bkey = [(p_pr, "ppr"), (p_kv, "pkv"), (p_ys, "pys"),
                                  (p_su, "psu")][fb % 4]
                hk = "wbf" + "AB"[fb // 4]
                wk = "wtt" + "AB"[fb // 4]
                TE(lambda e, fb=fb, bank=bank: e.matmul(
                    bank[:, 0:128],
                    s_wbf[:, fb * 128:(fb + 1) * 128], s_idb[:],
                    start=True, stop=True),
                   [hk, "idb", bkey], [bkey + "w"])
                if fb % 2 == 0:
                    SE(lambda e, fb=fb, bank=bank: e.copy(
                        s_wtt[:, fb * 128:(fb + 1) * 128], bank[:, 0:128]),
                       [bkey + "w"], [bkey, wk])
                else:
                    VE(lambda e, fb=fb, bank=bank: e.tensor_copy(
                        s_wtt[:, fb * 128:(fb + 1) * 128], bank[:, 0:128]),
                       [bkey + "w"], [bkey, wk])
            for hb in range(2):
                dst = d_dst.rearrange("(fb f) o -> f fb o", f=128)[
                    :, hb * 4:(hb + 1) * 4, ob * 128:(ob + 1) * 128]
                src = s_wtt[:, hb * 512:(hb + 1) * 512].rearrange(
                    "f (fb o) -> f fb o", o=128)
                dma("sync", dst, src, ["wtt" + "AB"[hb]], ["dw" + tagn],
                    tag="wstore")

    def prep_weight_alt_iter(d_src, d_dst, ob):
        for hb in range(2):
            dma("gpsimd", s_xd[:, hb * 512:(hb + 1) * 512],
                d_src[ob * 128:(ob + 1) * 128, hb * 512:(hb + 1) * 512],
                [], ["xd" + "AB"[hb]], tag="wload")
        for fb in range(8):
            slot = fb % 4
            hk = "xd" + "AB"[fb // 4]
            wk = "tb" + "AB"[fb // 4]
            TE(lambda e, fb=fb, slot=slot: e.transpose(
                p_tp[:, slot * 128:(slot + 1) * 128],
                s_xd[:, fb * 128:(fb + 1) * 128], s_idb[:]),
               [hk, "idb", "ptp"], ["ptpw"])
            SE(lambda e, fb=fb, slot=slot: e.copy(
                s_tb[:, fb * 128:(fb + 1) * 128],
                p_tp[:, slot * 128:(slot + 1) * 128]),
               ["ptpw"], ["ptp", wk])
        for hb in range(2):
            dst = d_dst.rearrange("(fb f) o -> f fb o", f=128)[
                :, hb * 4:(hb + 1) * 4, ob * 128:(ob + 1) * 128]
            srcs = s_tb[:, hb * 512:(hb + 1) * 512].rearrange(
                "f (fb o) -> f fb o", o=128)
            dma("sync", dst, srcs, ["tb" + "AB"[hb]], ["dwo"], tag="wstore")

    def prep_weight_quant_iter(d_src, d_dst, ob):
        dma("sync", s_wld[:], d_src[ob * 128:(ob + 1) * 128, :],
            [], ["wld"], tag="wload")
        emit_quantize(s_wld[:], s_wbf[:], "wld", ["wbfA", "wbfB"])
        for fb in range(8):
            bank, bkey = (p_kv, "pkv") if fb % 2 == 0 else (p_sc2, "psc2")
            hk = "wbf" + "AB"[fb // 4]
            wk = "wtt" + "AB"[fb // 4]
            TE(lambda e, fb=fb, bank=bank: e.matmul(
                bank[:, 0:128],
                s_wbf[:, fb * 128:(fb + 1) * 128], s_idb[:],
                start=True, stop=True),
               [hk, "idb", bkey], [bkey + "w"])
            if fb % 2 == 0:
                SE(lambda e, fb=fb, bank=bank: e.copy(
                    s_wtt[:, fb * 128:(fb + 1) * 128], bank[:, 0:128]),
                   [bkey + "w"], [bkey, wk])
            else:
                VE(lambda e, fb=fb, bank=bank: e.tensor_copy(
                    s_wtt[:, fb * 128:(fb + 1) * 128], bank[:, 0:128]),
                   [bkey + "w"], [bkey, wk])
        for hb in range(2):
            dst = d_dst.rearrange("(fb f) o -> f fb o", f=128)[
                :, hb * 4:(hb + 1) * 4, ob * 128:(ob + 1) * 128]
            srcs = s_wtt[:, hb * 512:(hb + 1) * 512].rearrange(
                "f (fb o) -> f fb o", o=128)
            dma("sync", dst, srcs, ["wtt" + "AB"[hb]], ["dwf"], tag="wstore")

    def emit_preamble():
        proj_x_part(0)
        for m in "qkvgb":
            prep_weight(d_w[m], d_wt[m], tagn=m)
        for ob in range(8):
            prep_weight_alt_iter(d_wo, d_wto, ob)
            prep_weight_quant_iter(d_wf, d_wtf, ob)
        dma("sync", s_wld[0:16, :], d_wa, [], ["wld"])
        VE(lambda e: e.tensor_copy(s_wbf[0:16, :], s_wld[0:16, :]), ["wld"],
           ["wbfA", "wbfB"])
        for fb in range(8):
            slot = fb % 4
            TE(lambda e, fb=fb, slot=slot: e.transpose(
                p_tp[:, slot * 128:slot * 128 + 16],
                s_wbf[0:16, fb * 128:(fb + 1) * 128], s_idb[0:16, 0:16]),
               ["wbfA", "wbfB", "idb", "ptp"], ["ptpw"])
            SE(lambda e, fb=fb, slot=slot: e.copy(
                s_wtt[:, fb * 16:(fb + 1) * 16], p_tp[:, slot * 128:slot * 128 + 16]),
               ["ptpw"], ["ptp", "wttA"])
        dma("sync", d_wat.rearrange("(fb f) h -> f fb h", f=128),
            s_wtt[:, 0:128].rearrange("f (fb h) -> f fb h", h=16), ["wttA"],
            ["wttA", "dwa"])
        proj_mm_part(0)

    def ln_stats(xap, key):
        SE(lambda e: e.activation(out=s_q2[:], in_=xap, func=AF.Square,
                                  accum_out=s_st[:, 1:2]), [key], ["q2", "st"])
        VE(lambda e: e.tensor_reduce(out=s_st[:, 0:1], in_=xap, axis=AX.X,
                                     op=ALU.add, negate=True), [key, "st"], ["st"])
        VE(lambda e: e.tensor_scalar_mul(s_st[:, 0:1], s_st[:, 0:1], 1.0 / D),
           ["st"], ["st"])
        VE(lambda e: e.tensor_tensor(s_st[:, 2:3], s_st[:, 0:1], s_st[:, 0:1],
                                     ALU.mult), ["st"], ["st"])
        VE(lambda e: e.tensor_scalar(out=s_st[:, 1:2], in0=s_st[:, 1:2],
                                     scalar1=1.0 / D, scalar2=None, op0=ALU.mult),
           ["st"], ["st"])
        VE(lambda e: e.tensor_tensor(s_st[:, 1:2], s_st[:, 1:2], s_st[:, 2:3],
                                     ALU.subtract), ["st"], ["st"])
        VE(lambda e: e.tensor_scalar(out=s_st[:, 1:2], in0=s_st[:, 1:2],
                                     scalar1=EPS_LN, scalar2=None, op0=ALU.add),
           ["st"], ["st"])
        SE(lambda e: e.activation(out=s_st[:, 1:2], in_=s_st[:, 1:2], func=AF.Sqrt),
           ["st"], ["st"])
        VE(lambda e: e.reciprocal(s_st[:, 1:2], s_st[:, 1:2]), ["st"], ["st"])

    def proj_x_part(sb):
        xsrc = d_x.rearrange("(n p) d -> p n d", p=128)[:, sb * 4:(sb + 1) * 4, :]
        dma("sync", s_x[:].rearrange("p (n d) -> p n d", d=D), xsrc,
            ["x"], ["x"], tag="xload")
        for wi in range(4):
            xt = s_x[:, wi * D:(wi + 1) * D]
            ln_stats(xt, "x")
            VE(lambda e, xt=xt: e.tensor_scalar(out=s_tb[:], in0=xt,
                                                scalar1=s_st[:, 0:1],
                                                scalar2=s_st[:, 1:2],
                                                op0=ALU.add, op1=ALU.mult),
               ["x", "st"], ["tb"])
            for fb in range(8):
                bank, bkey = (p_sc, "psc") if fb % 2 == 0 else (p_sc2, "psc2")
                TE(lambda e, fb=fb, bank=bank: e.matmul(
                    bank[:, 0:128],
                    s_tb[:, fb * 128:(fb + 1) * 128], s_idb[:],
                    start=True, stop=True),
                   ["tb", "idb", bkey], [bkey + "w"])
                if fb % 2 == 0:
                    SE(lambda e, fb=fb, bank=bank, wi=wi: e.copy(
                        s_lt[:, fb * SBT + wi * 128: fb * SBT + (wi + 1) * 128],
                        bank[:, 0:128]),
                       [bkey + "w"], [bkey, "lt"])
                else:
                    VE(lambda e, fb=fb, bank=bank, wi=wi: e.tensor_copy(
                        s_lt[:, fb * SBT + wi * 128: fb * SBT + (wi + 1) * 128],
                        bank[:, 0:128]),
                       [bkey + "w"], [bkey, "lt"])
    def proj_mm_part(sb, hook=None):
        src = d_wat.rearrange("(fb f) h -> f fb h", f=128)
        dma("sync", s_wrhs[:, 0:128].rearrange("f (fb h) -> f fb h", h=16), src,
            ["ws0", "dwa"], ["ws0"])
        for fb in range(8):
            TE(lambda e, fb=fb: e.matmul(
                p_pr[0:16, :], s_wrhs[:, fb * 16:(fb + 1) * 16],
                s_lt[:, fb * SBT:(fb + 1) * SBT],
                start=(fb == 0), stop=(fb == 7)),
               ["ws0", "lt", "ppr"] if fb == 0 else ["ws0", "lt"],
               ["pprw"] if fb == 7 else [])
        SE(lambda e: e.activation(out=s_at[:], in_=p_pr[0:16, :], func=AF.Identity,
                                  bias=s_dt[:]), ["pprw", "dt"], ["at", "ppr"])
        SE(lambda e: e.activation(out=s_at2[:], in_=s_at[:], func=AF.Abs),
           ["at"], ["at2"])
        SE(lambda e: e.activation(out=s_at2[:], in_=s_at2[:], func=AF.Exp,
                                  scale=-1.0), ["at2"], ["at2"])
        SE(lambda e: e.activation(out=s_at2[:], in_=s_at2[:], func=AF.Ln,
                                  bias=1.0), ["at2"], ["at2"])
        VE(lambda e: e.tensor_scalar_max(s_at[:], s_at[:], 0.0), ["at"], ["at"])
        VE(lambda e: e.tensor_tensor(s_at[:], s_at[:], s_at2[:], ALU.add),
           ["at", "at2"], ["at"])
        VE(lambda e: e.tensor_scalar_mul(s_at[:], s_at[:], s_ealog[:]),
           ["at", "ealog"], ["at"])
        VE(lambda e, sb=sb: e.tensor_tensor_scan(
            out=s_pp[:, 1 + sb * SBT: 1 + (sb + 1) * SBT], data0=s_ones[:],
            data1=s_at[:], initial=s_pp[:, sb * SBT:sb * SBT + 1],
            op0=ALU.mult, op1=ALU.add), ["at", "ones", "pp"], ["pp"])
        dma("sync", d_pd[:, sb * SBT: 1 + (sb + 1) * SBT],
            s_pp[:, sb * SBT: 1 + (sb + 1) * SBT], ["pp"], ["pd"])
        pcsv = s_pp[0:16, :].rearrange("h (b i) -> h b i", i=16)[
            :, sb * 32:sb * 32 + 33, 0]
        VE(lambda e, pcsv=pcsv, sb=sb: e.tensor_copy(s_pcs[0:16, 0:33], pcsv),
           ["pp"], ["spcs"])
        dma("sync", d_pcs[:, sb * 32:sb * 32 + 33], s_pcs[0:16, 0:33],
            ["spcs"], ["pcs"])
        for wi in range(4):
            w = sb * 4 + wi
            TE(lambda e, w=w: e.transpose(
                p_sc[:, 0:16], s_pp[0:16, 1 + w * WIN: 1 + (w + 1) * WIN],
                s_idf[0:16, 0:16]),
               ["pp", "idf", "psc"], ["pscw"])
            SE(lambda e, w=w: e.copy(s_ppt[:, w * H:(w + 1) * H], p_sc[:, 0:16]),
               ["pscw"], ["psc", "ppt"])
        projs = [("q", s_qb, AF.Identity), ("k", s_kb, AF.Identity),
                 ("v", s_vb, AF.Identity), ("g", s_gb, AF.Silu),
                 ("b", s_bb, AF.Sigmoid)]
        for m, dstt, act in projs:
            for oc in range(8):
                par = oc % 2
                wsl = s_wrhs[:, par * 1024:(par + 1) * 1024]
                src = d_wt[m].rearrange("(fb f) o -> f fb o", f=128)[
                    :, :, oc * 128:(oc + 1) * 128]
                skeys = [f"ws{par * 2}", f"ws{par * 2 + 1}"]
                dma("sync" if par == 0 else "gpsimd",
                    wsl.rearrange("f (fb o) -> f fb o", o=128), src,
                    skeys + ["dw" + m], skeys, tag="wstream")
                for fb in range(8):
                    sk = f"ws{par * 2 + fb // 4}"
                    TE(lambda e, fb=fb, wsl=wsl: e.matmul(
                        p_pr[:], wsl[:, fb * 128:(fb + 1) * 128],
                        s_lt[:, fb * SBT:(fb + 1) * SBT],
                        start=(fb == 0), stop=(fb == 7)),
                       [sk, "lt", "ppr"] if fb == 0 else [sk, "lt"],
                       ["pprw"] if fb == 7 else [])
                SE(lambda e, dstt=dstt, oc=oc, act=act: e.activation(
                    out=dstt[:, oc * SBT:(oc + 1) * SBT], in_=p_pr[:], func=act),
                   ["pprw"], ["ppr", m + "b"])
            if m == "k" and hook is not None:
                hook()

    def win_slice(tile, wi, p0=0, p1=128):
        return tile[p0:p1, :].rearrange("p (h t) -> p h t", h=NP)[
            :, :, wi * 128:(wi + 1) * 128]

    def norms_window(sb, wi):
        w = sb * 4 + wi
        for which, tile, drn, scale in [("k", s_kb, d_rk, 1.0),
                                        ("q", s_qb, d_rq, 64.0)]:
            ksl = win_slice(tile, wi)
            SE(lambda e, ksl=ksl: e.activation(
                out=s_scr[:].rearrange("p (h t) -> p h t", h=NP), in_=ksl,
                func=AF.Square), [which + "b"], ["scr"])
            for half in range(2):
                TE(lambda e, half=half: e.matmul(
                    p_tr[0:2, 0:512], s_esel[:],
                    s_scr[:, half * 512:(half + 1) * 512], start=True, stop=True),
                   ["scr", "esel", "ptr"], ["ptrw"])
                SE(lambda e, scale=scale, half=half: e.activation(
                    out=s_rr2[:, half * 512:(half + 1) * 512], in_=p_tr[0:2, 0:512],
                    func=AF.Sqrt, scale=scale),
                   ["ptrw"], ["ptr", "rr2"])
            VE(lambda e: e.reciprocal(s_rr2[:], s_rr2[:]), ["rr2"], ["rr2"])
            for half, bank, bkey in ((0, p_kv, "pkv"), (1, p_ys, "pys")):
                TE(lambda e, half=half, bank=bank: e.matmul(
                    bank[:, 0:512], s_e2t[0:2, :],
                    s_rr2[0:2, half * 512:(half + 1) * 512],
                    start=True, stop=True),
                   ["rr2", "e2t", bkey], [bkey + "w"])
                if half == 0:
                    SE(lambda e, half=half, bank=bank: e.copy(
                        s_rrep[:, half * 512:(half + 1) * 512], bank[:, 0:512]),
                       [bkey + "w"], [bkey, "rrep"])
                else:
                    VE(lambda e, half=half, bank=bank: e.tensor_copy(
                        s_rrep[:, half * 512:(half + 1) * 512], bank[:, 0:512]),
                       [bkey + "w"], [bkey, "rrep"])
            if which == "k":
                VE(lambda e, ksl=ksl: e.tensor_tensor(
                    s_pw1[:].rearrange("p (h t) -> p h t", h=NP), ksl,
                    s_rrep[:].rearrange("p (h t) -> p h t", h=NP), ALU.mult),
                   [which + "b", "rrep"], ["pw1"])
                VE(lambda e, ksl=ksl: e.tensor_copy(
                    ksl, s_pw1[:].rearrange("p (h t) -> p h t", h=NP)),
                   ["pw1"], [which + "b"])
            else:
                VE(lambda e, ksl=ksl: e.tensor_tensor(
                    ksl, ksl, s_rrep[:].rearrange("p (h t) -> p h t", h=NP),
                    ALU.mult),
                   [which + "b", "rrep"], [which + "b"])

    def _score_mms(wi, incl, h0, h1):
        for h in range(h0, h1):
            hp, h01 = h // 2, h % 2
            bank, bkey = (p_sc, "psc") if h % 2 == 0 else (p_sc2, "psc2")
            slot = (h // 2) % 4
            if incl:
                lhs = win_slice(s_kb, wi, h01 * 64, (h01 + 1) * 64)[:, hp]
                rhs = win_slice(s_qb, wi, h01 * 64, (h01 + 1) * 64)[:, hp]
                rkeys = ["kb", "qb", bkey]
            else:
                lhs = s_pw1[h01 * 64:(h01 + 1) * 64,
                            hp * 128:(hp + 1) * 128]
                rhs = lhs
                rkeys = ["pw1", bkey]
            TE(lambda e, lhs=lhs, rhs=rhs, slot=slot, bank=bank: e.matmul(
                bank[:, (slot) * 128:(slot + 1) * 128], lhs, rhs,
                start=True, stop=True),
               rkeys, [bkey + "w"])
            SE(lambda e, h=h, slot=slot, bank=bank: e.copy(
                s_sc[:, h * 128:(h + 1) * 128],
                bank[:, slot * 128:(slot + 1) * 128]),
               [bkey + "w"], [bkey, "sc"])

    def _score_extract(w, with_prow):
        for k in range(NCH):
            srcx = s_sc[16 * k:16 * (k + 1), :].rearrange(
                "p (h i) -> p h i", h=H)[:, :, 16 * k:16 * (k + 1)]
            dstx = s_wdst[16 * k:16 * (k + 1), :].rearrange(
                "p (h i) -> p h i", h=H)
            dma("gpsimd", dstx, srcx, ["sc"], ["wdst"], tag="diagx")
        if with_prow:
            for c_ in range(NCH):
                pr = d_pd[:, 1 + w * WIN + 16 * c_:1 + w * WIN + 16 * (c_ + 1)] \
                    .unsqueeze(0).broadcast_to((16, H, 16))
                dma("gpsimd", s_prow[16 * c_:16 * (c_ + 1), :], pr,
                    ["pd"], ["prow"], tag="prow")

    def _score_maskdump(w, incl):
        par = w % 2
        dumpd = d_qk if incl else d_wd
        dkey = ("dqk" if incl else "dwd") + str(par)
        pcol = s_ppt[:, w * H:(w + 1) * H].unsqueeze(2).broadcast_to((128, H, 16))
        GE(lambda e, pcol=pcol: e.tensor_tensor(
            s_mt[:].rearrange("p (h i) -> p h i", h=H),
            s_prow[:].rearrange("p (h i) -> p h i", h=H), pcol,
            ALU.subtract),
           ["prow", "ppt"], ["mt"])
        GE(lambda e: e.tensor_scalar_min(s_mt[:], s_mt[:], 0.0), ["mt"], ["mt"])
        SE(lambda e: e.activation(out=s_mt[:], in_=s_mt[:], func=AF.Exp),
           ["mt"], ["mt"])
        GE(lambda e: e.tensor_tensor(s_wdst[:], s_wdst[:], s_mt[:], ALU.mult),
           ["mt", "wdst"], ["wdst"])
        mk = s_ttri if incl else s_ntriu
        mkk = "ttri" if incl else "ntriu"
        GE(lambda e, mk=mk: e.tensor_tensor(s_wdst[:], s_wdst[:], mk[:],
                                            ALU.mult),
           ["wdst", mkk], ["wdst"])
        for c_ in range(NCH):
            dstd = dumpd[par, c_].rearrange("h j i -> j h i")
            dma("gpsimd", dstd,
                s_wdst[16 * c_:16 * (c_ + 1), :].rearrange(
                    "p (h i) -> p h i", h=H),
                ["wdst"], [dkey], tag="wdump")

    def _score_qkload(w):
        par = w % 2
        srcq = d_qk[par].rearrange("c h i j -> i c h j")
        dma("sync", s_qkb[0:16, par * (NCH * H * C):(par + 1) * (NCH * H * C)]
            .rearrange("i (c h j) -> i c h j", c=NCH, h=H),
            srcq, [f"dqk{par}"], [f"qkb{par}"], tag="qkload")

    def scores_pieces(sb, wi):
        w = sb * 4 + wi
        return [
            lambda: _score_mms(wi, False, 0, 8),
            lambda: _score_mms(wi, False, 8, 16),
            lambda: _score_extract(w, True),
            lambda: _score_maskdump(w, False),
            lambda: _score_mms(wi, True, 0, 8),
            lambda: _score_mms(wi, True, 8, 16),
            lambda: (_score_extract(w, False), _score_maskdump(w, True)),
            lambda: _score_qkload(w),
        ]

    def scores_and_masks(sb, wi):
        for piece in scores_pieces(sb, wi):
            piece()
    def proj_superblock(sb):
        proj_x_part(sb)
        proj_mm_part(sb)

    def ebuild_window(sb, wi):
        w = sb * 4 + wi
        par = w % 2
        for h01 in range(2):
            pt = d_pd.rearrange("(hp h2) t -> h2 hp t", h2=2)[h01] \
                     .unsqueeze(0).broadcast_to((64, NP, T + 32))
            dma("sync", s_ao[h01 * 64:(h01 + 1) * 64, :],
                pt[:, :, 1 + w * WIN:1 + (w + 1) * WIN], ["pd"], ["ao2"])
        for h01 in range(2):
            pcv = d_pcs.rearrange("(hp h2) b -> h2 hp b", h2=2)[
                h01, :, w * NCH:(w + 1) * NCH]
            pcsb = pcv.unsqueeze(0).broadcast_to((64, NP, NCH))
            dma("sync", s_pw2[h01 * 64:(h01 + 1) * 64, 0:64], pcsb,
                ["pcs"], ["pw2"])
        pw2b = s_pw2[:, 0:64].rearrange("p (h c) -> p h c", h=NP) \
            .unsqueeze(3).broadcast_to((128, NP, NCH, 16))
        GE(lambda e, pw2b=pw2b: e.tensor_tensor(
            s_e1[:].rearrange("p (h c i) -> p h c i", h=NP, c=NCH),
            s_ao[:].rearrange("p (h c i) -> p h c i", h=NP, c=NCH),
            pw2b, ALU.subtract),
           ["ao2", "pw2"], ["e1"])
        SE(lambda e, par=par: e.activation(
            out=s_e1[:],
            in_=s_e1[:], func=AF.Exp),
           ["e1"], ["e1"])
        for h01 in range(2):
            pev = d_pcs.rearrange("(hp h2) b -> h2 hp b", h2=2)[
                h01, :, w * NCH + 1:(w + 1) * NCH + 1]
            pceb = pev.unsqueeze(0).broadcast_to((64, NP, NCH))
            dma("sync", s_pw2[h01 * 64:(h01 + 1) * 64, 0:64], pceb,
                ["pcs", "e1"], ["pw2"])
        pw2c = s_pw2[:, 0:64].rearrange("p (h c) -> p h c", h=NP) \
            .unsqueeze(3).broadcast_to((128, NP, NCH, 16))
        GE(lambda e, pw2c=pw2c: e.tensor_tensor(
            s_e2[:].rearrange("p (h c i) -> p h c i", h=NP, c=NCH),
            pw2c,
            s_ao[:].rearrange("p (h c i) -> p h c i", h=NP, c=NCH),
            ALU.subtract),
           ["ao2", "pw2"], ["e2"])
        SE(lambda e, par=par: e.activation(
            out=s_e2[:],
            in_=s_e2[:], func=AF.Exp),
           ["e2"], ["e2"])

    def wrep_load(w, k):
        par, cp = w % 2, k % 2
        for h01 in range(2):
            srcw = d_wd[par, k].rearrange("(hp h2) j i -> h2 hp (j i)", h2=2)[h01] \
                               .unsqueeze(0).broadcast_to((64, NP, C * C))
            dma("sync", s_wrep[h01 * 64:(h01 + 1) * 64,
                               cp * 2048:(cp + 1) * 2048],
                srcw, [f"dwd{par}"], [f"wrep{cp}"], tag="wrepload")
        wv = s_wrep[:, cp * 2048:(cp + 1) * 2048].rearrange(
            "p (h i j) -> p h i j", h=NP, i=C)
        bb3 = win_slice(s_bb, w % 4)[:, :, k * C:(k + 1) * C] \
            .unsqueeze(2).broadcast_to((128, NP, C, C))
        VE(lambda e, wv=wv, bb3=bb3: e.tensor_tensor(wv, wv, bb3, ALU.mult),
           [f"wrep{cp}", "bb"], [f"wrep{cp}"], tag="wrepscale")

    def chunk_builds(sb, wi, k):
        e1sl = s_e1[:].rearrange("p (h t) -> p h t", h=NP)[:, :, k * C:(k + 1) * C]
        e2sl = s_e2[:].rearrange("p (h t) -> p h t", h=NP)[:, :, k * C:(k + 1) * C]
        kbsl = win_slice(s_kb, wi)[:, :, k * C:(k + 1) * C]
        qbsl = win_slice(s_qb, wi)[:, :, k * C:(k + 1) * C]
        VE(lambda e: e.tensor_tensor(s_kc[:].rearrange("p (h t) -> p h t", h=NP),
                                     kbsl, e1sl, ALU.mult),
           ["kb", "e1"], ["kc"])
        VE(lambda e: e.tensor_tensor(s_qc[:].rearrange("p (h t) -> p h t", h=NP),
                                     qbsl, e1sl, ALU.mult),
           ["qb", "e1"], ["qc"])
        VE(lambda e: e.tensor_tensor(s_k2[:].rearrange("p (h t) -> p h t", h=NP),
                                     kbsl, e2sl, ALU.mult),
           ["kb", "e2"], ["k2"])

    def chunk_steps(sb, wi, k, piece=None):
        w = sb * 4 + wi
        par, cp = w % 2, k % 2
        e1v = s_e1[:].rearrange(
            "p (h t) -> p h t", h=NP)
        vbsl = win_slice(s_vb, wi)[:, :, k * C:(k + 1) * C]
        bbsl = win_slice(s_bb, wi)[:, :, k * C:(k + 1) * C]
        uv = s_u[:].rearrange("p (h t) -> p h t", h=NP)
        for hp in range(NP):
            for h01 in range(2):
                b0 = h01 * 64
                lhs = s_s[b0:b0 + 64, hp * 64:(hp + 1) * 64]
                last = (hp == NP - 1 and h01 == 1)
                first = (hp == 0)
                TE(lambda e, lhs=lhs, hp=hp, b0=b0, first=first: e.matmul(
                    p_kv[b0:b0 + 64, hp * C:(hp + 1) * C], lhs,
                    s_kc[b0:b0 + 64, hp * C:(hp + 1) * C], start=first,
                    stop=True, skip_group_check=True),
                   ["S", "kc", "pkv"], ["pkvw"] if last else [])
                TE(lambda e, lhs=lhs, hp=hp, b0=b0, first=first: e.matmul(
                    p_ys[b0:b0 + 64, hp * C:(hp + 1) * C], lhs,
                    s_qc[b0:b0 + 64, hp * C:(hp + 1) * C], start=first,
                    stop=True, skip_group_check=True),
                   ["S", "qc", "pys"], [])
        for rnd in range(2):
            for hpi in range(4):
                hp = rnd * 4 + hpi
                TE(lambda e, hp=hp, hpi=hpi: e.transpose(
                    p_tr[0:16, hpi * 128:(hpi + 1) * 128],
                    s_k2[:, hp * C:(hp + 1) * C],
                    s_idf[:]), ["k2", "idf", "ptr"],
                   ["ptrw"] if hpi == 3 else [])
            SE(lambda e, rnd=rnd: e.copy(
                s_ktc[0:16, rnd * 512:(rnd + 1) * 512], p_tr[0:16, 0:512]),
               ["ptrw"], ["ptr", "ktc"])
        if piece is not None:
            piece()
        gcol = e1v[:, :, k * C + C - 1].unsqueeze(2).broadcast_to((128, NP, 64))
        sv = s_s[:].rearrange("p (h e) -> p h e", h=NP)
        VE(lambda e, gcol=gcol, sv=sv: e.tensor_tensor(sv, sv, gcol, ALU.mult),
           ["S", "e1"], ["S"])
        VE(lambda e: e.tensor_tensor(
            uv, vbsl, p_kv[:, 0:128].rearrange("p (h t) -> p h t", h=NP),
            ALU.subtract),
           ["vb", "pkvw"], ["u", "pkv"])
        VE(lambda e: e.tensor_tensor(uv, uv, bbsl, ALU.mult),
           ["u", "bb"], ["u"])
        if k + 1 < NCH:
            chunk_builds(sb, wi, k + 1)
        for m in range(C - 1):
            ucol = uv[:, :, m]
            width = C - 1 - m
            wsl = s_wrep[:, cp * 2048:(cp + 1) * 2048].rearrange(
                "p (h i j) -> p h i j", h=NP, i=C)[:, :, m, m + 1:]
            ub = ucol.unsqueeze(2).broadcast_to((128, NP, width))
            ptv = s_ptm[:].rearrange("p (h t) -> p h t", h=NP)[:, :, 0:width]
            VE(lambda e, wsl=wsl, ub=ub, ptv=ptv: e.tensor_tensor(
                ptv, wsl, ub, ALU.mult), ["u", f"wrep{cp}"], ["ptm"],
               tag="slv_p")
            usl = uv[:, :, m + 1:]
            VE(lambda e, usl=usl, ptv=ptv: e.tensor_tensor(
                usl, usl, ptv, ALU.add), ["u", "ptm"], ["u"], tag="slv_u")
        for rnd in range(2):
            for hpi in range(4):
                hp = rnd * 4 + hpi
                TE(lambda e, hp=hp, hpi=hpi: e.transpose(
                    p_tr[0:16, hpi * 128:(hpi + 1) * 128],
                    s_u[:, hp * C:(hp + 1) * C],
                    s_idf[:]), ["u", "idf", "ptr"],
                   ["ptrw"] if hpi == 3 else [])
            SE(lambda e, rnd=rnd: e.copy(
                s_ut[0:16, rnd * 512:(rnd + 1) * 512], p_tr[0:16, 0:512]),
               ["ptrw"], ["ptr", "ut"])
        for hp in range(NP):
            for h01 in range(2):
                b0 = h01 * 64
                last = (hp == NP - 1 and h01 == 1)
                first = (hp == 0)
                TE(lambda e, hp=hp, b0=b0, first=first: e.matmul(
                    p_su[b0:b0 + 64, hp * 64:(hp + 1) * 64],
                    s_ktc[0:16, hp * 128 + b0:hp * 128 + b0 + 64],
                    s_ut[0:16, hp * 128 + b0:hp * 128 + b0 + 64],
                    start=first, stop=True, skip_group_check=True),
                   ["ktc", "ut", "psu"], ["psuw"] if last else [])
        VE(lambda e, sv=sv: e.tensor_tensor(
            sv, sv, p_su[:].rearrange("p (h e) -> p h e", h=NP), ALU.add),
           ["S", "psuw"], ["S", "psu"])
        qkoff = par * (NCH * H * C) + k * (H * C)
        for h in range(H):
            hp, h01 = h // 2, h % 2
            b0 = h01 * 64
            TE(lambda e, hp=hp, b0=b0, h=h, qkoff=qkoff: e.matmul(
                p_ys[b0:b0 + 64, hp * C:(hp + 1) * C],
                s_ut[0:16, hp * 128 + b0:hp * 128 + b0 + 64],
                s_qkb[0:16, qkoff + h * C:qkoff + (h + 1) * C],
                start=False, stop=True, skip_group_check=True),
               ["ut", f"qkb{par}", "pys"], ["pysw"] if h == H - 1 else [])
        yssl = s_ys[:].rearrange("p (h t) -> p h t", h=NP)[:, :, k * C:(k + 1) * C]
        SE(lambda e, yssl=yssl: e.copy(
            yssl, p_ys[:, 0:128].rearrange("p (h t) -> p h t", h=NP)),
           ["pysw"], ["pys", "ys"])

    def window_post(sb, wi):
        w = sb * 4 + wi
        SE(lambda e: e.activation(out=s_scr[:], in_=s_ys[:], func=AF.Square),
           ["ys"], ["scr"])
        for half in range(2):
            TE(lambda e, half=half: e.matmul(
                p_tr[0:2, 0:512], s_esel[:],
                s_scr[:, half * 512:(half + 1) * 512], start=True, stop=True),
               ["scr", "esel", "ptr"], ["ptrw"])
            VE(lambda e, half=half: e.tensor_scalar(
                out=s_rr2[:, half * 512:(half + 1) * 512], in0=p_tr[0:2, 0:512],
                scalar1=1.0 / DH, scalar2=EPS_RMS, op0=ALU.mult, op1=ALU.add),
               ["ptrw"], ["ptr", "rr2"])
        SE(lambda e: e.activation(out=s_rr2[:], in_=s_rr2[:], func=AF.Sqrt),
           ["rr2"], ["rr2"])
        VE(lambda e: e.reciprocal(s_rr2[:], s_rr2[:]), ["rr2"], ["rr2"])
        for half, bank, bkey in ((0, p_kv, "pkv"), (1, p_ys, "pys")):
            TE(lambda e, half=half, bank=bank: e.matmul(
                bank[:, 0:512], s_e2t[0:2, :],
                s_rr2[0:2, half * 512:(half + 1) * 512],
                start=True, stop=True),
               ["rr2", "e2t", bkey], [bkey + "w"])
            if half == 0:
                SE(lambda e, half=half, bank=bank: e.copy(
                    s_rrep[:, half * 512:(half + 1) * 512], bank[:, 0:512]),
                   [bkey + "w"], [bkey, "rrep"])
            else:
                VE(lambda e, half=half, bank=bank: e.tensor_copy(
                    s_rrep[:, half * 512:(half + 1) * 512], bank[:, 0:512]),
                   [bkey + "w"], [bkey, "rrep"])
        VE(lambda e: e.tensor_tensor(s_ys[:], s_ys[:], s_rrep[:], ALU.mult),
           ["ys", "rrep"], ["ys"])
        gsl = win_slice(s_gb, wi)
        VE(lambda e, gsl=gsl, wi=wi: e.scalar_tensor_tensor(
            out=s_lt[:, wi * D:(wi + 1) * D].rearrange("p (h t) -> p h t", h=NP),
            in0=s_ys[:].rearrange("p (h t) -> p h t", h=NP),
            scalar=s_nwcol[:], in1=gsl, op0=ALU.mult, op1=ALU.mult),
           ["ys", "nwcol", "gb"], ["lt"])

    OUT_BANKS = [(p_pr, "ppr"), (p_kv, "pkv"), (p_ys, "pys"), (p_su, "psu")]
    STAGING = [(s_ao, "ao2"), (s_scr, "scr"), (s_e1, "e1"), (s_e2, "e2")]

    def out_ln_residual(wi):
        stag, skey = STAGING[wi]
        ln_stats(stag[:], skey)
        VE(lambda e, stag=stag: e.tensor_scalar(
            out=stag[:], in0=stag[:], scalar1=s_st[:, 0:1],
            scalar2=s_st[:, 1:2], op0=ALU.add, op1=ALU.mult),
           [skey, "st"], [skey])
        xt = s_x[:, wi * D:(wi + 1) * D]
        VE(lambda e, xt=xt, stag=stag: e.tensor_tensor(
            xt, xt, stag[:], ALU.add), ["x", skey], ["x"])

    def attn_out_batch(sb):
        for cc in range(4):
            wsl = s_wrhs[:, cc * 512:cc * 512 + 512]
            dma("sync" if cc < 2 else "gpsimd", wsl,
                d_wto[cc * 128:(cc + 1) * 128, 0:512],
                [f"ws{cc}", "dwo"], [f"ws{cc}"], tag="wostream")
        for oh in range(2):
            for cc in range(8):
                par2 = cc % 4
                wsl = s_wrhs[:, par2 * 512:par2 * 512 + 512]
                if not (oh == 0 and cc < 4):
                    dma("sync" if par2 < 2 else "gpsimd", wsl,
                        d_wto[cc * 128:(cc + 1) * 128, oh * 512:(oh + 1) * 512],
                        [f"ws{par2}", "dwo"], [f"ws{par2}"], tag="wostream")
                for wi in range(4):
                    bank, bkey = OUT_BANKS[wi]
                    TE(lambda e, cc=cc, wsl=wsl, wi=wi, bank=bank: e.matmul(
                        bank[:], s_lt[:, wi * D + cc * 128:wi * D + (cc + 1) * 128],
                        wsl, start=(cc == 0), stop=(cc == 7)),
                       [f"ws{par2}", "lt", bkey] if cc == 0
                       else [f"ws{par2}", "lt"],
                       [bkey + "w"] if cc == 7 else [])
            for wi in range(4):
                stag, skey = STAGING[wi]
                bank, bkey = OUT_BANKS[wi]
                VE(lambda e, oh=oh, stag=stag, bank=bank: e.tensor_tensor(
                    stag[:, oh * 512:(oh + 1) * 512], bank[:],
                    s_borep[:, oh * 512:(oh + 1) * 512], ALU.add),
                   [bkey + "w", "borep"], [bkey, skey])
        for wi in range(4):
            out_ln_residual(wi)

    def ffn_superblock(sb):
        for fb in range(4):
            wsl = s_wrhs[:, fb * 512:fb * 512 + 512]
            dma("sync" if fb < 2 else "gpsimd", wsl,
                d_wtf[fb * 128:(fb + 1) * 128, 0:512],
                [f"ws{fb}", "dwf"], [f"ws{fb}"], tag="wfstream")
        for wi in range(4):
            xt = s_x[:, wi * D:(wi + 1) * D]
            ln_stats(xt, "x")
            VE(lambda e, xt=xt: e.tensor_scalar(out=s_q1[:], in0=xt,
                                                scalar1=s_st[:, 0:1],
                                                scalar2=s_st[:, 1:2],
                                                op0=ALU.add, op1=ALU.mult),
               ["x", "st"], ["wld"])
            emit_quantize(s_q1[:], s_xd[:], "wld", "xd")
            for fb in range(8):
                bank, bkey = (p_sc, "psc") if fb % 2 == 0 else (p_sc2, "psc2")
                TE(lambda e, fb=fb, bank=bank: e.matmul(
                    bank[:, 0:128],
                    s_xd[:, fb * 128:(fb + 1) * 128], s_idb[:],
                    start=True, stop=True),
                   ["xd", "idb", bkey], [bkey + "w"])
                if fb % 2 == 0:
                    SE(lambda e, fb=fb, bank=bank, wi=wi: e.copy(
                        s_lt[:, fb * SBT + wi * 128: fb * SBT + (wi + 1) * 128],
                        bank[:, 0:128]),
                       [bkey + "w"], [bkey, "lt"])
                else:
                    VE(lambda e, fb=fb, bank=bank, wi=wi: e.tensor_copy(
                        s_lt[:, fb * SBT + wi * 128: fb * SBT + (wi + 1) * 128],
                        bank[:, 0:128]),
                       [bkey + "w"], [bkey, "lt"])
        for oh in range(2):
            for fb in range(8):
                par2 = fb % 4
                wsl = s_wrhs[:, par2 * 512:par2 * 512 + 512]
                if not (oh == 0 and fb < 4):
                    dma("sync" if par2 < 2 else "gpsimd", wsl,
                        d_wtf[fb * 128:(fb + 1) * 128, oh * 512:(oh + 1) * 512],
                        [f"ws{par2}", "dwf"], [f"ws{par2}"], tag="wfstream")
                for wi in range(4):
                    bank, bkey = OUT_BANKS[wi]
                    TE(lambda e, fb=fb, wi=wi, wsl=wsl, bank=bank: e.matmul(
                        bank[:],
                        s_lt[:, fb * SBT + wi * 128:fb * SBT + (wi + 1) * 128],
                        wsl, start=(fb == 0), stop=(fb == 7)),
                       [f"ws{par2}", "lt", bkey] if fb == 0
                       else [f"ws{par2}", "lt"],
                       [bkey + "w"] if fb == 7 else [])
            for wi in range(4):
                stag, skey = STAGING[wi]
                bank, bkey = OUT_BANKS[wi]
                VE(lambda e, oh=oh, stag=stag, bank=bank: e.tensor_tensor(
                    stag[:, oh * 512:(oh + 1) * 512], bank[:],
                    s_bfrep[:, oh * 512:(oh + 1) * 512], ALU.add),
                   [bkey + "w", "bfrep"], [bkey, skey])
        for wi in range(4):
            out_ln_residual(wi)
        dstq = d_out.rearrange("(n p) d -> p n d", p=128)[:, sb * 4:(sb + 1) * 4, :]
        dma("sync", dstq, s_x[:].rearrange("p (n d) -> p n d", d=D), ["x"],
            ["xout"], tag="xstore")

    emit_preamble()
    for sb in range(NSB):
        if sb > 0:
            proj_superblock(sb)
        if STAGE == "proj":
            for key, tile, dd in [("kb", s_kb, dbg["kb"]), ("qb", s_qb, dbg["qb"]),
                                  ("vb", s_vb, dbg["vb"]), ("bb", s_bb, dbg["bb"])]:
                dv = dd.rearrange("p (h t) -> p h t", h=NP)[
                    :, :, sb * SBT:(sb + 1) * SBT]
                for hp in range(NP):
                    VE(lambda e, tile=tile, hp=hp: e.tensor_copy(
                        s_q1[:, 0:512], tile[:, hp * SBT:(hp + 1) * SBT]),
                       [key], ["wld"])
                    dma("sync", dv[:, hp, :], s_q1[:, 0:512], ["wld"], ["dbg"])
            continue
        for wi in range(WPS):
            if sb * 4 + wi >= NWINLIM:
                continue
            if wi == 0:
                norms_window(sb, 0)
                scores_and_masks(sb, 0)
                wrep_load(sb * 4, 0)
            pieces = []
            if wi + 1 < WPS and sb * 4 + wi + 1 < NWINLIM:
                norms_window(sb, wi + 1)
                pieces = scores_pieces(sb, wi + 1)
            ebuild_window(sb, wi)
            chunk_builds(sb, wi, 0)
            if STAGE == "scan" and sb == 0 and wi == 0:
                dma("sync", dbg["e1"], s_e1[:], ["e1"], ["dbg"])
                dma("sync", dbg["pw1"], s_ao[:], ["ao2"], ["dbg"])
                dma("sync", dbg["pw2"], s_pw2[:, 0:64], ["pw2"], ["dbg"])
                dma("gpsimd", dbg["kc"], win_slice(s_kb, wi)[:, :, 0:16],
                    ["kb"], ["dbg"])
            for k in range(NCH):
                if k >= NCHLIM:
                    continue
                if k + 1 < NCH:
                    wrep_load(sb * 4 + wi, k + 1)
                elif wi + 1 < WPS and sb * 4 + wi + 1 < NWINLIM:
                    wrep_load(sb * 4 + wi + 1, 0)
                chunk_steps(sb, wi, k,
                            piece=pieces[k] if k < len(pieces) else None)
                if STAGE == "scan" and sb == 0 and wi == 0 and k == 0:
                    dma("sync", dbg["u0"], s_u[:], ["u"], ["dbg"])
                    dma("gpsimd", dbg["v0"], win_slice(s_vb, wi)[:, :, 0:16],
                        ["vb"], ["dbg"])
                    dma("gpsimd", dbg["b0"], win_slice(s_bb, wi)[:, :, 0:16],
                        ["bb"], ["dbg"])
                    dma("sync", dbg["wr0"], s_wrep[:, 0:2048], ["wrep0"], ["dbg"])
                    dma("sync", dbg["qk0"], s_qkb[0:16, 0:2048], ["qkb0"], ["dbg"])
            if STAGE == "scan":
                w = sb * 4 + wi
                dv = dbg["ys"].rearrange("p (h t) -> p h t", h=NP)[
                    :, :, w * WIN:(w + 1) * WIN]
                dma("sync", dv, s_ys[:].rearrange("p (h t) -> p h t", h=NP),
                    ["ys"], ["dbg"])
                continue
            window_post(sb, wi)
        if STAGE == "scan":
            continue
        attn_out_batch(sb)
        if STAGE == "attn":
            dv = dbg["xm"].rearrange("(n p) d -> p n d", p=128)[
                :, sb * 4:(sb + 1) * 4, :]
            dma("sync", dv, s_x[:].rearrange("p (n d) -> p n d", d=D), ["x"],
                ["dbg"])
            continue
        ffn_superblock(sb)
    if STAGE == "proj":
        dma("sync", dbg["pp"], s_pp[:, 0:T + 1], ["pp"], ["dbg"])

    with nc.Block() as block:
        sems = {}
        for e in ENGINES:
            t = nc.semaphore(f"sem_{e}")
            ctxs.append(t)
            sems[e] = t.__enter__()
        for semk in P.dq_counts:
            t = nc.semaphore(semk)
            ctxs.append(t)
            sems[semk] = t.__enter__()
        final_counts = dict(P.counts)
        final_dq = dict(P.dq_counts)

        waited = {}
        for st_ in P.steps:
            for de, dv_ in st_["waits"].items():
                waited.setdefault(de, set()).add(dv_)
        for oe in ENGINES:
            waited.setdefault(oe, set()).add(P.counts[oe])
        cnt = {e: 0 for e in ENGINES}
        keep = []
        kept_cum = {e: 0 for e in ENGINES}
        remap = {e: {0: 0} for e in ENGINES}
        for st_ in P.steps:
            semk = st_["sem"]
            if semk in ENGINES:
                prev = cnt[semk]
                cur = prev + 1
                cnt[semk] = cur
                need = any(prev < w <= cur for w in waited.get(semk, ()))
                keep.append(need)
                if need:
                    kept_cum[semk] += 1
                remap[semk][cur] = kept_cum[semk]
            else:
                keep.append(True)

        def mapv(semk, v):
            if semk in ENGINES:
                return remap[semk][v]
            return v

        def make_stream(ename):
            def f(eng):
                last_wait = {}
                idx = 0
                for sti, st_ in enumerate(P.steps):
                    if st_["engine"] != ename:
                        continue
                    for de, dv_ in sorted(st_["waits"].items()):
                        v = mapv(de, dv_)
                        if v <= 0:
                            continue
                        if last_wait.get(de, -1) >= v:
                            continue
                        eng.wait_ge(sems[de], v)
                        last_wait[de] = v
                    inst = st_["fn"](eng)
                    if inst is None:
                        inst = eng.nop()
                    if keep[sti]:
                        inst.then_inc(sems[st_["sem"]], st_["inc"])
                if ename == "sync":
                    for oe in ENGINES:
                        if oe != "sync":
                            v = mapv(oe, final_counts[oe])
                            if v > 0 and last_wait.get(oe, -1) < v:
                                eng.wait_ge(sems[oe], v)
                    for semk, v in final_dq.items():
                        if last_wait.get(semk, -1) < v:
                            eng.wait_ge(sems[semk], v)
            return f

        block.sync(make_stream("sync"))
        block.tensor(make_stream("tensor"))
        block.vector(make_stream("vector"))
        block.scalar(make_stream("scalar"))
        block.gpsimd(make_stream("gpsimd"))

    for t in reversed(ctxs):
        t.__exit__(None, None, None)
    return nc


_NC_CACHE = {}


def kernel(L0_attn_x, L0_attn_Wq, L0_attn_Wk, L0_attn_Wv, L0_attn_Wg, L0_attn_Wb,
           L0_attn_Wa, L0_attn_dt_bias, L0_attn_A_log, L0_attn_norm_w, L0_attn_Wo,
           L0_attn_bo, L0_ffn_weight, L0_ffn_bias):
    if "prog" not in _NC_CACHE:
        _NC_CACHE["prog"] = build_program()
    nc = _NC_CACHE["prog"]
    consts = build_consts()
    x = np.ascontiguousarray(np.asarray(L0_attn_x, dtype=np.float32))
    base = dict(
        Wq=np.ascontiguousarray(np.asarray(L0_attn_Wq, np.float32)),
        Wk=np.ascontiguousarray(np.asarray(L0_attn_Wk, np.float32)),
        Wv=np.ascontiguousarray(np.asarray(L0_attn_Wv, np.float32)),
        Wg=np.ascontiguousarray(np.asarray(L0_attn_Wg, np.float32)),
        Wb=np.ascontiguousarray(np.asarray(L0_attn_Wb, np.float32)),
        Wa=np.ascontiguousarray(np.asarray(L0_attn_Wa, np.float32)),
        dt_bias=np.ascontiguousarray(np.asarray(L0_attn_dt_bias, np.float32)
                                     .reshape(H, 1)),
        A_log=np.ascontiguousarray(np.asarray(L0_attn_A_log, np.float32)
                                   .reshape(H, 1)),
        norm_w=np.ascontiguousarray(np.asarray(L0_attn_norm_w, np.float32)),
        Wo=np.ascontiguousarray(np.asarray(L0_attn_Wo, np.float32)),
        bo=np.ascontiguousarray(np.asarray(L0_attn_bo, np.float32)),
        Wf=np.ascontiguousarray(np.asarray(L0_ffn_weight, np.float32)),
        bfv=np.ascontiguousarray(np.asarray(L0_ffn_bias, np.float32)),
        ident_f32=consts["ident_f32"],
        ident_bf16=consts["ident_bf16"],
        negtril=consts["negtril"],
        triincl=consts["triincl"],
        ntriu=consts["ntriu"],
        esel2=consts["esel2"],
        esel2t=consts["esel2t"],
        mska=consts["mska"],
        mskb=consts["mskb"],
    )
    in_maps = []
    for b in range(NCORES):
        m = dict(base)
        m["x"] = np.ascontiguousarray(x[b])
        in_maps.append(m)
    res = run_bass_kernel_spmd(nc, in_maps, core_ids=list(range(NCORES)))
    out = np.stack([np.asarray(res.results[b]["out"]) for b in range(NCORES)],
                   axis=0)
    return out.astype(np.float32)
